# version 29
# baseline (speedup 1.0000x reference)
import contextlib
import numpy as np
import ml_dtypes
import concourse.bass as bass
import concourse.mybir as mybir
from concourse.bass_utils import run_bass_kernel_spmd

F32 = mybir.dt.float32
BF16 = mybir.dt.bfloat16
AF = mybir.ActivationFunctionType
ALU = mybir.AluOpType
AX = mybir.AxisListType

L, D, H, DK = 2048, 1024, 4, 256
NCH, NFM, NTM = 16, 8, 16
FFH = 4096
NCORE = 8
EL2, ERMS, ELN = 1e-6, 1e-5, 1e-5


class Tok:
    __slots__ = ("sem", "val")
    def __init__(self, sem, val):
        self.sem, self.val = sem, val


class Eng:
    GD_REF = None

    def __init__(self, name):
        self.name, self.sem, self.h, self.active, self.count = name, None, None, False, 0
        self.sem_pool = None
        self.pool_idx = 0
        self.self_sync = False

    def phase(self):
        if self.sem_pool is not None:
            self.pool_idx = (self.pool_idx + 1) % len(self.sem_pool)

    def do(self, fn, inc=0):
        if self.active:
            if self.self_sync and self.count > 0:
                self.h.wait_ge(self.sem, self.count)
            ins = fn(self.h)
            if inc:
                ins.then_inc(self.sem, inc)
        if inc:
            self.count += inc
        return Tok(self.sem, self.count)

    def dma(self, fn):
        if self.sem_pool is not None:
            sem, cnt = self.sem_pool[self.pool_idx]
            cnt += 16
            self.sem_pool[self.pool_idx] = (sem, cnt)
            if self.active:
                ins = fn(self.h)
                ins.then_inc(sem, 16)
            return Tok(sem, cnt)
        return self.do(fn, inc=16)

    def bump(self, n=1):
        if self.active:
            self.h.drain()
            self.h.sem_inc(self.sem, n)
        self.count += n
        return Tok(self.sem, self.count)

    def tok(self):
        return Tok(self.sem, self.count)

    POOLED = ()

    def wait(self, tok):
        if tok is None:
            return
        for eng in Eng.POOLED:
            if eng.sem_pool is not None:
                done = False
                for sem, cnt in eng.sem_pool:
                    if tok.sem is sem:
                        tok = Tok(sem, cnt)
                        done = True
                        break
                if done:
                    break
        if self.active:
            self.h.wait_ge(tok.sem, tok.val)


def build_core(debug_outs=(), stage_limit=99, d_sub=99, f_sub=99):
    nc = bass.Bass("TRN2", target_bir_lowering=False)
    DI = lambda n, s: nc.dram_tensor(n, s, F32, kind="ExternalInput").ap()
    DSC = lambda n, s, dt: nc.dram_tensor(
        n, s, dt, kind=("ExternalOutput" if n in debug_outs else "Internal")).ap()

    x_in = DI("x", [L, D])
    Wq_h, Wk_h, Wv_h = DI("Wq", [D, D]), DI("Wk", [D, D]), DI("Wv", [D, D])
    Wbp_h = DI("Wbp", [D, 4])
    Wrp_h = DI("Wrp", [D, 8])
    dq_h = DI("dq", [128, NFM * 4 * 128])
    dk_h = DI("dk", [128, NFM * 4 * 128])
    dvv_h = DI("dv", [128, NFM * 4 * 128])
    dss_h = DI("ds", [128, NFM * 5 * 128])
    tpl_h = DI("tpl", [128, 256 * 3 * 32])
    tapq_h = DI("tapq", [128, 32])
    tapk_h = DI("tapk", [128, 32])
    tapv_h = DI("tapv", [128, 32])
    taps_h = DI("taps5", [128, 40])
    Wg1_h = DI("Wg1", [1040, D])
    Wg1t_h = DI("Wg1t", [128, D])
    bg1_h = DI("bg1", [128, NFM])
    Wg2p_h = DI("Wg2p", [D, 4])
    bg2_h = DI("bg2", [H, 1])
    lt_h = DI("logit_temp", [1, 1])
    crl_h = DI("conv_res_logit", [H, 1])
    bres_h = DI("bres", [H, 1])
    Wog1_h = DI("Wog1", [1032, 512])
    Wog1t_h = DI("Wog1t", [128, 512])
    bog1_h = DI("bog1", [128, 4])
    Wog2_h = DI("Wog2", [512, 1])
    bog2_h = DI("bog2", [1, 1])
    nw_h = DI("norm_w", [128, NFM])
    Wo_h = DI("Wo", [D, D])
    wfc_h = DI("w_fc", [FFH, D])
    wpj_h = DI("w_proj", [D, FFH])
    uT_h = DI("uT", [16, FFH])
    vT_h = DI("vT", [D, 16])
    fbT_h = DI("fbT", [16, 16])
    gate_h = DI("gate", [1, 1])
    onesv_h = DI("onesv_c", [128, 512])
    mixmx_h = DI("mixmx_c", [128, 192])

    y_out = nc.dram_tensor("y", [L, D], F32, kind="ExternalOutput").ap()
    dbg = {}
    def dbg_ap(name, shape, dt=F32):
        if name in debug_outs:
            dbg[name] = nc.dram_tensor("dbg_" + name, shape, dt, kind="ExternalOutput").ap()
        return dbg.get(name)

    h_xbf = DSC("h_xbf", [L, D], BF16)
    h_qfm = DSC("h_qfm", [D, L], BF16)
    h_kfm = DSC("h_kfm", [D, L], BF16)
    h_kntm = DSC("h_kntm", [L, D], BF16)
    h_kbntm = DSC("h_kbntm", [L, D], BF16)
    h_knfm = DSC("h_knfm", [D, L], BF16)
    h_kbnfm = DSC("h_kbnfm", [D, L], BF16)
    h_vfm = DSC("h_vfm", [D, L], BF16)
    h_vtm = DSC("h_vtm", [L, D], BF16)
    h_vbtm = DSC("h_vbtm", [L, D], BF16)
    h_lsfm = DSC("h_lsfm", [D, L], BF16)
    h_ll = DSC("h_ll", [L, D], BF16)
    h_delta = DSC("h_delta", [L, D], BF16)
    h_E = DSC("h_E", [NCH * H * 128, 128], BF16)
    h_aT = DSC("h_aT", [NCH * H * 128, 128], BF16)
    h_o = DSC("h_o", [L, D], BF16)
    h_on = DSC("h_on", [L, D], BF16)
    h_x2 = DSC("h_x2", [L, D], F32)
    h_x2bf = DSC("h_x2bf", [L, D], BF16)
    h_tern = DSC("h_tern", [FFH, D], BF16)
    h_ternp = DSC("h_ternp", [D, FFH], BF16)
    h_ffm = DSC("h_ffm", [D, L], BF16)

    es = contextlib.ExitStack()
    with es:
        sbuf = lambda n, s, dt: es.enter_context(nc.sbuf_tensor(n, s, dt))
        psum_t = lambda n, s, dt=F32: es.enter_context(nc.psum_tensor(n, s, dt))

        TE, DV, AC, GP, SY, GD = (Eng(n) for n in ["te", "dv", "ac", "gp", "sy", "gd"])
        Eng.GD_REF = GD
        GD.sem_pool = [(es.enter_context(nc.semaphore(f"s_gd{i}")), 0) for i in range(8)]
        SY.sem_pool = [(es.enter_context(nc.semaphore(f"s_sy{i}")), 0) for i in range(8)]
        Eng.POOLED = (GD, SY)
        DV.self_sync = AC.self_sync = GP.self_sync = True
        ALLE = [TE, DV, AC, GP, SY, GD]
        for e in ALLE:
            e.sem = es.enter_context(nc.semaphore("s_" + e.name))

        masks = sbuf("masks", [128, 384], F32)
        ident_bf = sbuf("ident_bf", [128, 128], BF16)
        ident_f = sbuf("ident_f", [128, 128], F32)
        onesv = sbuf("onesv", [128, 512], BF16)
        mixmx = sbuf("mixmx", [128, 192], BF16)
        srows = sbuf("srows", [128, L], F32)
        srows_bf = sbuf("srows_bf", [128, L], BF16)
        stm = sbuf("stm", [128, NTM * 32], F32)
        consts = sbuf("consts", [128, 64], F32)
        S_f32 = sbuf("S_f32", [128, 2048], F32)
        S_bf = sbuf("S_bf", [128, 2048], BF16)
        arenaW = sbuf("arenaW", [128, 36864], BF16)
        arenaB = sbuf("arenaB", [128, 41984], BF16)
        arenaF = sbuf("arenaF", [128, 3144], F32)
        wsm = sbuf("wsm", [128, 4096], BF16)

        pA0 = psum_t("pA0", [128, 512])
        pA1 = psum_t("pA1", [128, 512])
        pB0 = psum_t("pB0", [128, 512])
        pB1 = psum_t("pB1", [128, 512])
        pC0 = psum_t("pC0", [128, 512])
        pC1 = psum_t("pC1", [128, 512])
        p1a = psum_t("p1a", [128, 512])
        p1b = psum_t("p1b", [128, 512])
        PA, PB, PC = (pA0, pA1), (pB0, pB1), (pC0, pC1)
        PD = (p1a, p1b)

        C_bg2, C_bres, C_crl, C_lt, C_rsp, C_b2r, C_bog2, C_gate = 0, 1, 2, 3, 4, 5, 6, 7
        C_stat, C_tmp, C_el2, C_eln, C_i256 = 8, 9, 10, 11, 32
        C_bog1, C_bg1, C_nw = 12, 16, 24

        R_scr = 0
        R_beta, R_coef, R_rq = 0, 4, 0
        R_gpark, R_ogpark = 32, 64
        R_logit, R_og, R_rrms = 0, 0, 4

        T_beta, T_pls, T_rq, T_p, T_og, T_rrms, T_sc, T_tmp = 0, 4, 8, 12, 16, 20, 24, 28

        SV_X, SV_X2, SV_AB, SV_RQ = 0, 4, 8, 12
        def sv(i, hh):
            j = 32 * (i + hh)
            return onesv[:, j:j + 32]

        W_wq, W_wk, W_wv = 0, 8192, 16384
        W_diag = 24576
        W_small = 29696
        W_tpl, W_tpl_n = 30208, 6144
        W_wg1, W_wg2 = 0, 9216
        W_wog1 = 9280
        W_wog2 = 13888
        W_wo = 13920
        W_vT, W_uT, W_fbT = 22112, 22240, 26336
        W_tern, W_pstr = 0, 32768

        B_xfm = 0
        B_pre = 16384
        B_vfm = 20488
        B_lsld = 36904
        B_v32 = 0
        B_lltmp = 20488
        B_llst = 28680
        S_kn, S_kbn, S_q = 0, 4096, 8192
        S_kntm, S_vb = 12288, 14336
        S_E, S_aT, S_dout = 16384, 17408, 18432
        B_g1 = 0
        B_gg = 16384
        B_mix = 32768
        B_otm = 38912
        B_g2 = 39936
        B_onfm = 16384
        B_x2fm = 0
        B_r2 = 16384
        B_fq = 32768

        F_acc, F_ln, F_sc, F_m = 0, 2048, 3072, 3136

        blk = es.enter_context(nc.Block())
        aF = lambda off, ln: arenaF[:, off:off + ln]

        def wait_all_sy(e):
            for (s, c) in SY.sem_pool:
                if c > 0:
                    e.wait(Tok(s, c))

        def barrier():
            toks = [e.bump() for e in (TE, DV, AC, GP)]
            toks += [Tok(s, c) for ee in (GD, SY) for (s, c) in ee.sem_pool if c > 0]
            for e in (TE, DV, AC, GP, SY):
                for t in toks:
                    e.wait(t)
            GD.phase()
            SY.phase()

        def mini_sync(engs=(SY, DV, AC, GP, TE)):
            toks = []
            for e in engs:
                if e.sem_pool is not None:
                    toks += [Tok(s, c) for (s, c) in e.sem_pool if c > 0]
                else:
                    toks.append(e.tok())
            for e in engs:
                for t in toks:
                    e.wait(t)
            if SY in engs:
                SY.phase()

        def mm(out_ap, lhsT, rhs, start, stop, tp=None):
            if isinstance(out_ap, tuple):
                tokk = None
                for i, oa in enumerate(out_ap):
                    tokk = TE.do(lambda h, i=i, oa=oa: h.matmul(
                        oa[:, 0:512], lhsT, rhs[:, i * 512:(i + 1) * 512],
                        start=start, stop=stop, tile_position=tp))
                return tokk
            return TE.do(lambda h: h.matmul(out_ap, lhsT, rhs, start=start, stop=stop,
                                            tile_position=tp))

        def st_init():
            t_gp = GP.do(lambda h: h.memset(masks[:, 0:128], -1.0), inc=1)
            GP.wait(t_gp)
            GP.do(lambda h: h.affine_select(out=masks[:, 0:128], in_=masks[:, 0:128],
                                            compare_op=ALU.is_gt, fill=0.0, base=0,
                                            pattern=[[-1, 128]], channel_multiplier=1))
            t_gp = GP.do(lambda h: h.memset(masks[:, 128:256], -1.0), inc=1)
            GP.wait(t_gp)
            GP.do(lambda h: h.affine_select(out=masks[:, 128:256], in_=masks[:, 128:256],
                                            compare_op=ALU.is_gt, fill=0.0, base=0,
                                            pattern=[[1, 128]], channel_multiplier=-1))
            t_gp = GP.do(lambda h: h.memset(masks[:, 256:384], 1.0), inc=1)
            GP.wait(t_gp)
            GP.do(lambda h: h.affine_select(out=masks[:, 256:384], in_=masks[:, 256:384],
                                            compare_op=ALU.is_ge, fill=0.0, base=0,
                                            pattern=[[1, 128]], channel_multiplier=-1))
            for t_ in (ident_bf, ident_f):
                t_gp = GP.do(lambda h, t=t_: h.memset(t[:], 0.0), inc=1)
                GP.wait(t_gp)
                GP.do(lambda h, t=t_: h.affine_select(out=t[:], in_=t[:],
                                                      compare_op=ALU.not_equal, fill=1.0,
                                                      base=0, pattern=[[-1, 128]],
                                                      channel_multiplier=1))
            GP.do(lambda h: h.memset(stm[:], 0.0))
            GP.do(lambda h: h.memset(consts[:, C_el2:C_el2 + 1], EL2))
            GP.do(lambda h: h.memset(consts[0:4, C_bres:C_bres + 1], 0.0))
            GP.do(lambda h: h.memset(consts[0:4, C_crl:C_crl + 1], 0.0))
            GP.do(lambda h: h.memset(consts[:, C_eln:C_eln + 1], ELN))
            GP.do(lambda h: h.memset(consts[:, C_i256:C_i256 + 1], 1.0 / 256))
            GP.do(lambda h: h.memset(srows[R_gpark:R_gpark + 16, :], 0.0))
            GP.do(lambda h: h.memset(srows[R_ogpark:R_ogpark + 8, :], 0.0))
            GP.do(lambda h: h.memset(srows_bf[:], 0.0))
            GP.do(lambda h: h.memset(S_f32[:], 0.0))
            GP.do(lambda h: h.memset(S_bf[:], 0.0))
            for pp in range(2):
                GP.do(lambda h, pp=pp: h.memset(
                    arenaB[:, B_pre + pp * 2052:B_pre + pp * 2052 + 4], 0.0))
            for fc in range(NFM):
                GP.do(lambda h, fc=fc: h.memset(
                    arenaB[:, B_vfm + fc * 2052:B_vfm + fc * 2052 + 4], 0.0))
            GP.bump()

            SY.dma(lambda h: h.dma_start(out=consts[0:4, C_bg2:C_bg2 + 1], in_=bg2_h))
            SY.dma(lambda h: h.dma_start(out=consts[4:8, C_bres:C_bres + 1], in_=bres_h))
            SY.dma(lambda h: h.dma_start(out=consts[4:8, C_crl:C_crl + 1], in_=crl_h))
            SY.dma(lambda h: h.dma_start(out=consts[0:4, C_lt:C_lt + 1],
                                         in_=lt_h.broadcast_to([4, 1])))
            SY.dma(lambda h: h.dma_start(out=consts[0:1, C_bog2:C_bog2 + 1], in_=bog2_h))
            SY.dma(lambda h: h.dma_start(out=consts[0:16, C_gate:C_gate + 1],
                                         in_=gate_h.broadcast_to([16, 1])))
            SY.dma(lambda h: h.dma_start(out=consts[:, C_bog1:C_bog1 + 4], in_=bog1_h))
            SY.dma(lambda h: h.dma_start(out=consts[:, C_bg1:C_bg1 + 8], in_=bg1_h))
            SY.dma(lambda h: h.dma_start(out=consts[:, C_nw:C_nw + 8], in_=nw_h))

            t_xbf = GD.dma(lambda h: h.dma_start(out=h_xbf[:, :], in_=x_in))
            for Wh, off in ((Wq_h, W_wq), (Wk_h, W_wk), (Wv_h, W_wv)):
                GD.dma(lambda h, Wh=Wh, off=off: h.dma_start(
                    out=arenaW[:, off:off + 8192].rearrange("p (kc j) -> p kc j", kc=8),
                    in_=Wh.rearrange("(kc p) j -> p kc j", p=128)))
            GD.dma(lambda h: h.dma_start(
                out=arenaW[:, W_small:W_small + 32].rearrange("p (kc j) -> p kc j", kc=8),
                in_=Wbp_h.rearrange("(kc p) j -> p kc j", p=128)))
            GD.dma(lambda h: h.dma_start(
                out=arenaW[:, W_small + 32:W_small + 96].rearrange("p (kc j) -> p kc j", kc=8),
                in_=Wrp_h.rearrange("(kc p) j -> p kc j", p=128)))
            WdF = 30720 // 2
            GD.dma(lambda h, WdF=WdF: h.dma_start(
                out=arenaW[:, :].bitcast(F32)[:, WdF:WdF + 32], in_=tapq_h))
            GD.dma(lambda h, WdF=WdF: h.dma_start(
                out=arenaW[:, :].bitcast(F32)[:, WdF + 32:WdF + 64], in_=tapk_h))
            GD.dma(lambda h, WdF=WdF: h.dma_start(
                out=arenaW[:, :].bitcast(F32)[:, WdF + 64:WdF + 96], in_=tapv_h))
            GD.dma(lambda h, WdF=WdF: h.dma_start(
                out=arenaW[:, :].bitcast(F32)[:, WdF + 96:WdF + 136], in_=taps_h))
            GD.dma(lambda h: h.dma_start(out=onesv[:, :], in_=onesv_h))
            GD.dma(lambda h: h.dma_start(out=mixmx[:, :], in_=mixmx_h))
            SY.wait(t_xbf)
            for fc in range(NFM):
                SY.dma(lambda h, fc=fc: h.dma_start_transpose(
                    out=arenaB[:, B_xfm + fc * 2048:B_xfm + (fc + 1) * 2048],
                    in_=h_xbf[:, fc * 128:(fc + 1) * 128]))
            barrier()

        def proj_conv(w_off, tap_off, diag_hbm, Kt, out_writer):
            GD.phase()
            t_diag = GD.dma(lambda h, diag_hbm=diag_hbm: h.dma_start(
                out=arenaW[:, W_diag:W_diag + NFM * Kt * 128],
                in_=diag_hbm[:, 0:NFM * Kt * 128]))
            TE.wait(t_diag)
            tapsF = arenaW[:, :].bitcast(F32)
            evac_t = [None, None]
            conv_dv = {0: None, 1: None}
            conv_te = {0: None, 1: None}
            for fc in range(NFM):
                q_hook()
                pp = fc % 2
                pre = arenaB[:, B_pre + pp * 2052:B_pre + pp * 2052 + 2052]
                for th in range(2):
                    ps = PA if th == 0 else PB
                    TE.wait(evac_t[th])
                    for kc in range(NFM):
                        mm(ps,
                           arenaW[:, w_off + kc * 1024 + fc * 128:w_off + kc * 1024 + (fc + 1) * 128],
                           arenaB[:, B_xfm + kc * 2048 + th * 1024:B_xfm + kc * 2048 + (th + 1) * 1024],
                           start=(kc == 0), stop=(kc == NFM - 1))
                    t_mm = TE.bump()
                    AC.wait(t_mm)
                    AC.do(lambda h, ps=ps, pre=pre, th=th: h.copy(
                        out=pre[:, 4 + th * 1024:4 + th * 1024 + 512], in_=ps[0][:, 0:512]), inc=1)
                    evac_t[th] = AC.do(lambda h, ps=ps, pre=pre, th=th: h.copy(
                        out=pre[:, 4 + th * 1024 + 512:4 + (th + 1) * 1024], in_=ps[1][:, 0:512]), inc=1)
                scr = aF(F_acc + pp * 1024, 1024)
                DV.wait(evac_t[0])
                DV.wait(conv_dv[pp])
                for t in range(Kt):
                    base = 4 - (Kt - 1) + t
                    tap = tapsF[:, tap_off + fc * Kt + t:tap_off + fc * Kt + t + 1]
                    if t == 0:
                        DV.do(lambda h, pre=pre, base=base, tap=tap, scr=scr: h.tensor_scalar(
                            out=scr, in0=pre[:, base:base + 1024],
                            scalar1=tap, scalar2=None, op0=ALU.mult), inc=1)
                    else:
                        DV.do(lambda h, pre=pre, base=base, tap=tap, scr=scr: h.scalar_tensor_tensor(
                            out=scr, in0=pre[:, base:base + 1024], scalar=tap,
                            in1=scr, op0=ALU.mult, op1=ALU.add), inc=1)
                t_cv0 = DV.tok()
                AC.wait(t_cv0)
                conv_dv[pp] = out_writer(fc, 0, [(scr, 0, 1024)])
                pcp = PC if pp == 0 else PD
                TE.wait(evac_t[1])
                TE.wait(conv_te[pp])
                for t in range(Kt):
                    base = 4 + 1024 - (Kt - 1) + t
                    mm(pcp,
                       arenaW[:, W_diag + (fc * Kt + t) * 128:W_diag + (fc * Kt + t + 1) * 128],
                       pre[:, base:base + 1024],
                       start=(t == 0), stop=(t == Kt - 1))
                t_cv1 = TE.bump()
                AC.wait(t_cv1)
                conv_te[pp] = out_writer(fc, 1, [(pcp[0][:, 0:512], 0, 512),
                                                 (pcp[1][:, 0:512], 512, 512)])
            barrier()

        def stats_mix3(slab, slab2, park, n):
            prev = {0: None, 1: None}
            for q in range(4):
                pc = PC[q % 2]
                TE.wait(prev[q % 2])
                mm(pc[park:park + n, 0:512],
                   mixmx[0:128, slab:slab + n],
                   srows_bf[0:128, q * 512:(q + 1) * 512], True, True, tp=(0, park))
                t_park = TE.bump()
                DV.wait(t_park)
                prev[q % 2] = DV.do(lambda h, q=q, pc=pc: h.tensor_add(
                    out=srows[park:park + n, q * 512:(q + 1) * 512],
                    in0=srows[park:park + n, q * 512:(q + 1) * 512],
                    in1=pc[park:park + n, 0:512]), inc=1)
            TE.wait(prev[0])
            TE.wait(prev[1])

        def stats_mix_p2(slab2, park, n):
            prev = {0: None, 1: None}
            for q in range(4):
                pc = PC[q % 2]
                TE.wait(prev[q % 2])
                mm(pc[park:park + n, 0:512],
                   mixmx[0:4, slab2:slab2 + n],
                   srows_bf[0:4, q * 512:(q + 1) * 512], True, True, tp=(0, park))
                t_park = TE.bump()
                DV.wait(t_park)
                prev[q % 2] = DV.do(lambda h, q=q, pc=pc: h.tensor_add(
                    out=srows[park:park + n, q * 512:(q + 1) * 512],
                    in0=srows[park:park + n, q * 512:(q + 1) * 512],
                    in1=pc[park:park + n, 0:512]), inc=1)
            TE.wait(prev[0])
            TE.wait(prev[1])

        def stats_fm2(get_chunk, slab, og_slab, rrms, pres):
            bank_first = {0: True, 1: True, 2: True, 3: True}
            bank_first_d = dict(bank_first)
            t_te_prev = {0: None, 1: None}
            for fc in range(NFM):
                pg = fc % 2 if len(pres) > 1 else 0
                pre0, pre1 = pres[pg]
                src_, t_src = get_chunk(fc)
                DV.wait(t_src)
                DV.wait(t_te_prev[pg])
                t_sq = DV.do(lambda h, s=src_, d=pre0: h.tensor_mul(out=d, in0=s, in1=s), inc=1)
                AC.wait(t_src)
                AC.wait(t_te_prev[pg])
                t_ab = AC.do(lambda h, s=src_, d=pre1: h.activation(out=d, in_=s, func=AF.Abs), inc=1)
                hh = fc // 2
                for sv_i, data, t_d, dup in ((SV_X, src_, t_src, False), (SV_X2, pre0, t_sq, True),
                                             (SV_AB, pre1, t_ab, False)):
                    TE.wait(t_d)
                    for q in range(4):
                        ps = (pA0, pA1, pB0, pB1)[q]
                        st = bank_first[q]
                        bank_first[q] = False
                        last = (fc == NFM - 1 and sv_i == SV_AB)
                        mm(ps[0:32, 0:512],
                           sv(sv_i, hh), data[:, q * 512:(q + 1) * 512],
                           start=st, stop=last)
                        if dup and f_sub >= 2:
                            psd = (pC0, pC1, p1a, p1b)[q]
                            std = bank_first_d[q]
                            bank_first_d[q] = False
                            mm(psd[96:128, 0:512],
                               sv(SV_RQ, hh), data[:, q * 512:(q + 1) * 512],
                               start=std, stop=(fc == NFM - 1), tp=(0, 96))
                t_te_prev[pg] = TE.bump()
            t_all = TE.tok()
            AC.wait(t_all)
            for q in range(4):
                psq = (pA0, pA1, pB0, pB1)[q]
                psd = (pC0, pC1, p1a, p1b)[q]
                AC.do(lambda h, q=q, psq=psq: h.copy(
                    out=srows[0:12, q * 512:(q + 1) * 512], in_=psq[0:12, 0:512]), inc=1)
                if f_sub >= 2:
                    AC.do(lambda h, q=q, psd=psd: h.copy(
                        out=srows[96:100, q * 512:(q + 1) * 512], in_=psd[96:100, 0:512]), inc=1)
            t_e = AC.tok()
            AC.wait(t_e)
            if f_sub >= 2:
                t_l2 = AC.do(lambda h: h.activation(out=srows[96:100, :], in_=srows[96:100, :],
                                                    func=AF.Sqrt), inc=1)
                DV.wait(t_l2)
            else:
                DV.wait(AC.tok())
            DV.do(lambda h: h.tensor_copy(out=srows_bf[0:12, :], in_=srows[0:12, :]), inc=1)
            t_c = DV.do(lambda h: h.tensor_copy(out=srows_bf[96:100, :], in_=srows[96:100, :]), inc=1)
            TE.wait(t_c)
            if f_sub < 3:
                barrier()
                return
            if slab is not None:
                stats_mix3(slab[0], slab[1], R_gpark, 16)
            if og_slab is not None:
                stats_mix3(og_slab[0], og_slab[1], R_ogpark, 8)
            AC.wait(DV.tok())
            t_m2 = AC.do(lambda h: h.activation(out=srows[0:4, :], in_=srows[0:4, :],
                                                func=AF.Square,
                                                scale=consts[0:4, C_i256:C_i256 + 1]), inc=1)
            DV.wait(t_m2)
            t_c2 = DV.do(lambda h: h.tensor_copy(out=srows_bf[0:4, :], in_=srows[0:4, :]), inc=1)
            TE.wait(t_c2)
            if f_sub < 4:
                barrier()
                return
            if slab is not None:
                stats_mix_p2(slab[1], R_gpark, 16)
            if og_slab is not None:
                stats_mix_p2(og_slab[1], R_ogpark, 8)
            if rrms:
                AC.wait(DV.tok())
                t_sr = AC.do(lambda h: h.activation(out=srows[0:8, :], in_=srows[0:8, :],
                                                    func=AF.Copy, bias=0.0,
                                                    scale=consts[0:8, C_i256:C_i256 + 1]), inc=1)
                DV.wait(t_sr)
            barrier()

        def small_rt(row0, nrows, slot0):
            prev = {0: None, 1: None}
            for t in range(NTM):
                pb = p1b if t % 2 == 0 else p1a
                TE.wait(prev[t % 2])
                t_tr = TE.do(lambda h, t=t, pb=pb: h.transpose(
                    pb[:, 0:nrows], srows[row0:row0 + nrows, t * 128:(t + 1) * 128],
                    ident_f[0:nrows, 0:nrows]), inc=1)
                AC.wait(t_tr)
                prev[t % 2] = AC.do(lambda h, t=t, pb=pb: h.copy(
                    out=stm[:, t * 32 + slot0:t * 32 + slot0 + nrows], in_=pb[:, 0:nrows]), inc=1)
            barrier()

        def stage_A_v():
            def writer(fc, th, segs):
                tk = None
                off = B_vfm + fc * 2052 + 4 + th * 1024
                for ap, o, n in segs:
                    tk = AC.do(lambda h, ap=ap, o=o, n=n, off=off: h.activation(
                        out=arenaB[:, off + o:off + o + n], in_=ap, func=AF.Silu), inc=1)
                return tk
            proj_conv(W_wv, 30720 // 2 + 64, dvv_h, 4, writer)
            for q in range(4):
                for kc in range(NFM):
                    mm(p1a[0:8, 0:512],
                       arenaW[:, W_small + 32 + kc * 8:W_small + 32 + (kc + 1) * 8],
                       arenaB[:, B_xfm + kc * 2048 + q * 512:B_xfm + kc * 2048 + (q + 1) * 512],
                       start=(kc == 0), stop=(kc == NFM - 1))
                t3 = TE.bump()
                AC.wait(t3)
                t4 = AC.do(lambda h, q=q: h.activation(
                    out=srows[0:8, q * 512:(q + 1) * 512], in_=p1a[0:8, 0:512],
                    func=AF.Sigmoid, bias=consts[0:8, C_bres:C_bres + 1]), inc=1)
                TE.wait(t4)
            AC.do(lambda h: h.activation(out=consts[0:8, C_stat:C_stat + 1],
                                         in_=consts[0:8, C_crl:C_crl + 1], func=AF.Sigmoid), inc=1)
            t5 = AC.tok()
            DV.wait(t5)
            DV.do(lambda h: h.tensor_scalar(out=srows[0:8, :], in0=srows[0:8, :],
                                            scalar1=consts[0:8, C_stat:C_stat + 1], scalar2=None,
                                            op0=ALU.mult), inc=1)
            t6 = DV.tok()
            TE.wait(t6)
            AC.wait(t6)
            for q in range(4):
                for kc in range(NFM):
                    mm(p1a[0:4, 0:512], arenaW[:, W_small + kc * 4:W_small + (kc + 1) * 4],
                       arenaB[:, B_xfm + kc * 2048 + q * 512:B_xfm + kc * 2048 + (q + 1) * 512],
                       start=(kc == 0), stop=(kc == NFM - 1))
                t1 = TE.bump()
                AC.wait(t1)
                t2 = AC.do(lambda h, q=q: h.activation(
                    out=srows[0:4, q * 512:(q + 1) * 512], in_=p1a[0:4, 0:512],
                    func=AF.Sigmoid), inc=1)
                TE.wait(t2)
            barrier()

        def stage_A_k():
            st_tok = {}
            def writer(fc, th, segs):
                base = (fc % 2) * 2048
                if th == 0:
                    AC.wait(st_tok.get(fc - 2))
                t = None
                for ap, o, n in segs:
                    t = AC.do(lambda h, th=th, base=base, ap=ap, o=o, n=n: h.activation(
                        out=wsm[:, base + th * 1024 + o:base + th * 1024 + o + n],
                        in_=ap, func=AF.Silu), inc=1)
                if th == 1:
                    SY.phase()
                    SY.wait(t)
                    st_tok[fc] = SY.dma(lambda h, fc=fc, base=base: h.dma_start(
                        out=h_kfm[fc * 128:(fc + 1) * 128, :], in_=wsm[:, base:base + 2048]))
                return t
            proj_conv(W_wk, 30720 // 2 + 32, dk_h, 4, writer)

        def stage_A_q():
            st_tok = {}
            def writer(fc, th, segs):
                base = (fc % 2) * 2048
                if th == 0:
                    AC.wait(st_tok.get(fc - 2))
                t = None
                for ap, o, n in segs:
                    t = AC.do(lambda h, th=th, base=base, ap=ap, o=o, n=n: h.activation(
                        out=wsm[:, base + th * 1024 + o:base + th * 1024 + o + n],
                        in_=ap, func=AF.Silu), inc=1)
                if th == 1:
                    SY.phase()
                    SY.wait(t)
                    st_tok[fc] = SY.dma(lambda h, fc=fc, base=base: h.dma_start(
                        out=h_qfm[fc * 128:(fc + 1) * 128, :], in_=wsm[:, base:base + 2048]))
                return t
            proj_conv(W_wq, 30720 // 2, dq_h, 4, writer)
            bank_first = {0: True, 1: True, 2: True, 3: True}
            mm_done = {0: None, 1: None}
            for fc in range(NFM):
                par = fc % 2
                slot = wsm[:, par * 2048:(par + 1) * 2048]
                SY.phase()
                SY.wait(mm_done[par])
                tok = SY.dma(lambda h, fc=fc, slot=slot: h.dma_start(
                    out=slot, in_=h_qfm[fc * 128:(fc + 1) * 128, :]))
                AC.wait(tok)
                t_sq = AC.do(lambda h, slot=slot: h.activation(out=slot, in_=slot,
                                                               func=AF.Square), inc=1)
                TE.wait(t_sq)
                hh = fc // 2
                for q in range(4):
                    ps = (pA0, pA1, pB0, pB1)[q]
                    st = bank_first[q]
                    bank_first[q] = False
                    mm(ps[0:32, 0:512],
                       sv(SV_RQ, hh), slot[:, q * 512:(q + 1) * 512],
                       start=st, stop=(fc == NFM - 1))
                mm_done[par] = TE.bump()
            t_all = TE.bump()
            AC.wait(t_all)
            for q in range(4):
                psq = (pA0, pA1, pB0, pB1)[q]
                t_q2 = AC.do(lambda h, q=q, psq=psq: h.activation(
                    out=srows[R_rq:R_rq + 4, q * 512:(q + 1) * 512], in_=psq[0:4, 0:512],
                    func=AF.Sqrt, bias=consts[0:4, C_el2:C_el2 + 1]), inc=1)
            DV.wait(t_q2)
            DV.do(lambda h: h.reciprocal(out=srows[R_rq:R_rq + 4, :], in_=srows[R_rq:R_rq + 4, :]), inc=1)
            barrier()

        def stage_D():
            pre0 = arenaB[:, B_pre:B_pre + 2048]
            pre1 = arenaB[:, B_pre + 2052:B_pre + 2052 + 2048]
            presD = [(pre0, pre1), (arenaB[:, 0:2048], arenaB[:, 2048:4096])]
            stats_fm2(lambda fc: (arenaB[:, B_vfm + fc * 2052 + 4:B_vfm + fc * 2052 + 4 + 2048], None),
                      (48, 96 + 48), (72, 168), False, presD)
            if d_sub <= 1:
                return
            GD.phase()
            t_diag = GD.dma(lambda h: h.dma_start(
                out=arenaW[:, W_diag:W_diag + NFM * 5 * 128], in_=dss_h[:, :]))
            TE.wait(t_diag)
            tapsF = arenaW[:, :].bitcast(F32)
            TS_OFF = 30720 // 2 + 96
            cv_dv = {0: None, 1: None}
            cv_te = {0: None, 1: None}
            st1 = {0: None, 1: None}
            for fc in range(NFM):
                pp = fc % 2
                scr = aF(F_acc + pp * 1024, 1024)
                for t in range(5):
                    base = B_vfm + fc * 2052 + t
                    tap = tapsF[:, TS_OFF + fc * 5 + t:TS_OFF + fc * 5 + t + 1]
                    if t == 0:
                        DV.do(lambda h, base=base, tap=tap, scr=scr: h.tensor_scalar(
                            out=scr, in0=arenaB[:, base:base + 1024],
                            scalar1=tap, scalar2=None, op0=ALU.mult), inc=1)
                    else:
                        DV.do(lambda h, base=base, tap=tap, scr=scr: h.scalar_tensor_tensor(
                            out=scr, in0=arenaB[:, base:base + 1024], scalar=tap,
                            in1=scr, op0=ALU.mult, op1=ALU.add), inc=1)
                DV.wait(cv_dv[pp])
                DV.do(lambda h, pp=pp, scr=scr: h.tensor_copy(
                    out=wsm[:, 2048 + pp * 1024:2048 + (pp + 1) * 1024], in_=scr), inc=1)
                t0c = DV.tok()
                SY.phase()
                SY.wait(t0c)
                cv_dv[pp] = SY.dma(lambda h, fc=fc, pp=pp: h.dma_start(
                    out=h_lsfm[fc * 128:(fc + 1) * 128, 0:1024],
                    in_=wsm[:, 2048 + pp * 1024:2048 + (pp + 1) * 1024]))
                pcp = PC if pp == 0 else PD
                TE.wait(cv_te[pp])
                for t in range(5):
                    base = B_vfm + fc * 2052 + 1024 + t
                    mm(pcp,
                       arenaW[:, W_diag + (fc * 5 + t) * 128:W_diag + (fc * 5 + t + 1) * 128],
                       arenaB[:, base:base + 1024],
                       start=(t == 0), stop=(t == 4))
                tmm = TE.bump()
                AC.wait(tmm)
                AC.wait(st1[pp])
                AC.do(lambda h, pp=pp: h.copy(
                    out=wsm[:, pp * 1024:pp * 1024 + 512], in_=pC0[:, 0:512]
                    if pp == 0 else p1a[:, 0:512]), inc=1)
                tev = AC.do(lambda h, pp=pp: h.copy(
                    out=wsm[:, pp * 1024 + 512:(pp + 1) * 1024], in_=pC1[:, 0:512]
                    if pp == 0 else p1b[:, 0:512]), inc=1)
                cv_te[pp] = tev
                SY.phase()
                SY.wait(tev)
                st1[pp] = SY.dma(lambda h, fc=fc, pp=pp: h.dma_start(
                    out=h_lsfm[fc * 128:(fc + 1) * 128, 1024:2048],
                    in_=wsm[:, pp * 1024:(pp + 1) * 1024]))
            barrier()
            if d_sub <= 2:
                return
            def get_ls(fc):
                pg = fc % 2
                SY.phase()
                SY.wait(TE.tok())
                tok = SY.dma(lambda h, fc=fc, pg=pg: h.dma_start(
                    out=arenaB[:, B_lsld + pg * 2048:B_lsld + pg * 2048 + 2048],
                    in_=h_lsfm[fc * 128:(fc + 1) * 128, :]))
                return arenaB[:, B_lsld + pg * 2048:B_lsld + pg * 2048 + 2048], tok
            stats_fm2(get_ls, (0, 96), None, False, presD)
            if d_sub <= 3:
                return
            for fc in range(NFM):
                SY.dma(lambda h, fc=fc: h.dma_start(
                    out=h_vfm[fc * 128:(fc + 1) * 128, :],
                    in_=arenaB[:, B_vfm + fc * 2052 + 4:B_vfm + fc * 2052 + 4 + 2048]))
            barrier()
            if d_sub <= 4:
                return
            wait_all_sy(SY)
            st_v = {0: None, 1: None}
            st_vb = {0: None, 1: None}
            vb_dv = {0: None, 1: None}
            for t in range(NTM):
                par = t % 2
                vsl = wsm[:, par * 2048:par * 2048 + 1024]
                vbs = wsm[:, par * 2048 + 1024:par * 2048 + 2048]
                SY.phase()
                SY.wait(st_v[par])
                SY.wait(vb_dv[par])
                tok = SY.dma(lambda h, t=t, vsl=vsl: h.dma_start_transpose(
                    out=vsl, in_=h_vfm[:, t * 128:(t + 1) * 128]))
                GD.phase()
                GP.wait(tok)
                st_v[par] = GD.dma(lambda h, t=t, vsl=vsl: h.dma_start(
                    out=h_vtm[t * 128:(t + 1) * 128, :], in_=vsl))
                DV.wait(tok)
                DV.wait(st_vb[par])
                for hh in range(H):
                    DV.do(lambda h, t=t, hh=hh, vsl=vsl, vbs=vbs: h.tensor_scalar(
                        out=vbs[:, hh * 256:(hh + 1) * 256],
                        in0=vsl[:, hh * 256:(hh + 1) * 256],
                        scalar1=stm[:, t * 32 + T_beta + hh:t * 32 + T_beta + hh + 1],
                        scalar2=None, op0=ALU.mult), inc=1)
                t_vb = DV.tok()
                vb_dv[par] = t_vb
                GD.phase()
                GP.wait(t_vb)
                st_vb[par] = GD.dma(lambda h, t=t, vbs=vbs: h.dma_start(
                    out=h_vbtm[t * 128:(t + 1) * 128, :], in_=vbs))
            barrier()
            if d_sub <= 5:
                return
            t_z = GP.do(lambda h: h.memset(arenaB[:, B_v32:B_v32 + 512], 0.0), inc=1)
            TE.wait(t_z)
            GD.phase()
            t_v32 = None
            for g in range(4):
                t_v32 = GD.dma(lambda h, g=g: h.dma_start(
                    out=arenaB[:, B_v32 + 512:B_v32 + 512 + 64 * 256]
                        .rearrange("p (b e) -> p b e", b=64)[32 * g:32 * (g + 1)],
                    in_=h_vtm.rearrange("(b r) (g e) -> g r b e", r=32, e=256)[g]))
            TE.wait(t_v32)
            if d_sub <= 6:
                return
            v32r = arenaB[:, B_v32:B_v32 + 66 * 256].rearrange("p (b e) -> p b e", e=256)
            lltv = arenaB[:, B_lltmp:B_lltmp + 8192].rearrange("p (b e) -> p b e", e=128)
            llta = arenaB[:, B_lltmp:B_lltmp + 8192].rearrange("p (t a e) -> p a t e", a=4, e=128)
            t_tpl = {}
            GD.phase()
            t_tpl[0] = GD.dma(lambda h: h.dma_start(
                out=arenaW[:, 0:W_tpl_n],
                in_=tpl_h.rearrange("p (e x) -> p e x", e=256)[:, 0:64]))
            for eb in range(4):
                W_tplb = (eb % 2) * W_tpl_n
                if eb + 1 < 4:
                    GD.phase()
                    GP.wait(TE.tok())
                    t_tpl[eb + 1] = GD.dma(lambda h, eb=eb: h.dma_start(
                        out=arenaW[:, ((eb + 1) % 2) * W_tpl_n:((eb + 1) % 2) * W_tpl_n + W_tpl_n],
                        in_=tpl_h.rearrange("p (e x) -> p e x", e=256)[:, (eb + 1) * 64:(eb + 2) * 64]))
                TE.wait(t_tpl[eb])
                for e8 in range(8):
                    for ee in range(8):
                        e_loc = e8 * 8 + ee
                        e_glob = eb * 64 + e_loc
                        for g in range(4):
                            psg = (pC0, pC1, p1a, p1b)[g]
                            for dd in range(3):
                                first = (ee == 0 and dd == 0)
                                mm(psg[32 * g:32 * (g + 1), ee * 64:(ee + 1) * 64],
                                   arenaW[32 * g:32 * (g + 1),
                                          W_tplb + (e_loc * 3 + dd) * 32:W_tplb + (e_loc * 3 + dd + 1) * 32],
                                   v32r[32 * g:32 * (g + 1), (2 - dd):(2 - dd) + 64,
                                        e_glob:e_glob + 1],
                                   start=first, stop=(ee == 7 and dd == 2),
                                   tp=(32 * g, 32 * g))
                    t_g = TE.bump()
                    AC.wait(t_g)
                    DV.wait(t_g)
                    ebase = (eb * 64 + e8 * 8) % 128
                    t_eva = t_evd = None
                    for g in range(4):
                        psg = (pC0, pC1, p1a, p1b)[g]
                        if g % 2 == 0:
                            t_eva = AC.do(lambda h, ebase=ebase, g=g, psg=psg: h.copy(
                                out=lltv[32 * g:32 * (g + 1), :, ebase:ebase + 8].transpose([0, 2, 1]),
                                in_=psg[32 * g:32 * (g + 1), 0:512].rearrange("p (s b) -> p s b", s=8)), inc=1)
                        else:
                            t_evd = DV.do(lambda h, ebase=ebase, g=g, psg=psg: h.tensor_copy(
                                out=lltv[32 * g:32 * (g + 1), :, ebase:ebase + 8].transpose([0, 2, 1]),
                                in_=psg[32 * g:32 * (g + 1), 0:512].rearrange("p (s b) -> p s b", s=8)), inc=1)
                    TE.wait(t_eva)
                    TE.wait(t_evd)
                if eb % 2 == 1:
                    GP.wait(AC.tok())
                    GP.wait(DV.tok())
                    ehalf = eb // 2
                    GD.phase()
                    t_asm = None
                    for g in range(4):
                        for a in range(4):
                            t_asm = GD.dma(lambda h, g=g, a=a, ehalf=ehalf: h.dma_start(
                                out=h_ll.rearrange("(t a r) (g e) -> g a r t e", a=4, r=32, e=256)
                                    [g, a][:, :, ehalf * 128:(ehalf + 1) * 128],
                                in_=llta[32 * g:32 * (g + 1), a, :, :]))
                    AC.wait(t_asm)
                    DV.wait(t_asm)
            barrier()
            if d_sub <= 7:
                return
            def get_ll(fc):
                pg = fc % 2
                SY.phase()
                SY.wait(TE.tok())
                tok = SY.dma(lambda h, fc=fc, pg=pg: h.dma_start_transpose(
                    out=arenaB[:, B_llst + pg * 2048:B_llst + pg * 2048 + 2048],
                    in_=h_ll[:, fc * 128:(fc + 1) * 128]))
                return arenaB[:, B_llst + pg * 2048:B_llst + pg * 2048 + 2048], tok
            presL = [(arenaB[:, B_llst + 4096:B_llst + 6144],
                      arenaB[:, B_llst + 6144:B_llst + 8192]),
                     (arenaB[:, B_llst + 8192:B_llst + 10240],
                      arenaB[:, B_llst + 10240:B_llst + 12288])]
            stats_fm2(get_ll, (16, 96 + 16), None, False, presL)

        def stage_B_k():
            st_kn = {0: None, 1: None}
            st_kbn = {0: None, 1: None}
            red_tok = {0: None, 1: None}
            for t in range(NTM):
                par = t % 2
                kraw = wsm[:, par * 2048:par * 2048 + 1024]
                kn_s = wsm[:, par * 2048 + 1024:par * 2048 + 2048]
                mbs = F_m + 0 if par == 0 else F_sc + 48
                SY.phase()
                SY.wait(st_kbn[par])
                tok = SY.dma(lambda h, t=t, kraw=kraw: h.dma_start_transpose(
                    out=kraw, in_=h_kfm[:, t * 128:(t + 1) * 128]))
                AC.wait(tok)
                AC.wait(red_tok[par])
                t_sq = AC.do(lambda h, kraw=kraw, par=par: h.activation(
                    out=aF(F_acc + par * 1024, 1024), in_=kraw, func=AF.Square), inc=1)
                DV.wait(t_sq)
                t_rd = DV.do(lambda h, par=par, mbs=mbs: h.tensor_reduce(
                    out=aF(mbs, 4),
                    in_=aF(F_acc + par * 1024, 1024).rearrange("p (hh e) -> p hh e", hh=4),
                    axis=AX.X, op=ALU.add), inc=1)
                red_tok[par] = t_rd
                AC.wait(t_rd)
                t_rk = AC.do(lambda h, mbs=mbs: h.activation(out=aF(mbs + 4, 4), in_=aF(mbs, 4),
                                                             func=AF.Sqrt,
                                                             bias=consts[:, C_el2:C_el2 + 1]), inc=1)
                DV.wait(t_rk)
                DV.do(lambda h, mbs=mbs: h.reciprocal(out=aF(mbs + 4, 4), in_=aF(mbs + 4, 4)), inc=1)
                DV.wait(st_kn[par])
                for hh in range(H):
                    DV.do(lambda h, hh=hh, kraw=kraw, kn_s=kn_s, mbs=mbs: h.tensor_scalar(
                        out=kn_s[:, hh * 256:(hh + 1) * 256],
                        in0=kraw[:, hh * 256:(hh + 1) * 256],
                        scalar1=aF(mbs + 4 + hh, 1), scalar2=None, op0=ALU.mult), inc=1)
                t_kn = DV.tok()
                SY.phase()
                SY.wait(t_kn)
                st_kn[par] = SY.dma(lambda h, t=t, kn_s=kn_s: h.dma_start(
                    out=h_kntm[t * 128:(t + 1) * 128, :], in_=kn_s))
                for hh in range(H):
                    DV.do(lambda h, t=t, hh=hh, kraw=kraw, kn_s=kn_s: h.tensor_scalar(
                        out=kraw[:, hh * 256:(hh + 1) * 256],
                        in0=kn_s[:, hh * 256:(hh + 1) * 256],
                        scalar1=stm[:, t * 32 + T_beta + hh:t * 32 + T_beta + hh + 1],
                        scalar2=None, op0=ALU.mult), inc=1)
                t_kbn = DV.tok()
                SY.phase()
                SY.wait(t_kbn)
                st_kbn[par] = SY.dma(lambda h, t=t, kraw=kraw: h.dma_start(
                    out=h_kbntm[t * 128:(t + 1) * 128, :], in_=kraw))
            barrier()
            st = {0: None, 1: None}
            for (tsrc, tdst) in ((h_kntm, h_knfm), (h_kbntm, h_kbnfm)):
                for fc in range(NFM):
                    par = fc % 2
                    SY.phase()
                    SY.wait(st[par])
                    t1 = SY.dma(lambda h, fc=fc, par=par, tsrc=tsrc: h.dma_start_transpose(
                        out=wsm[:, par * 2048:(par + 1) * 2048],
                        in_=tsrc[:, fc * 128:(fc + 1) * 128]))
                    GD.phase()
                    GP.wait(t1)
                    st[par] = GD.dma(lambda h, fc=fc, par=par, tdst=tdst: h.dma_start(
                        out=tdst[fc * 128:(fc + 1) * 128, :],
                        in_=wsm[:, par * 2048:(par + 1) * 2048]))
            barrier()

        def stage_C_pre():
            ev_t = {hh: None for hh in range(H)}
            est_tok = {}
            for c in range(NCH):
                pp = c % 2
                SY.phase()
                SY.wait(TE.tok())
                t_kn = SY.dma(lambda h, c=c, pp=pp: h.dma_start(
                    out=arenaB[:, S_kn + pp * 1024:S_kn + pp * 1024 + 1024]
                        .rearrange("p (kc t) -> p kc t", kc=8),
                    in_=h_knfm.rearrange("(kc p) t -> p kc t", p=128)[:, :, c * 128:(c + 1) * 128]))
                t_kbn = SY.dma(lambda h, c=c, pp=pp: h.dma_start(
                    out=arenaB[:, S_kbn + pp * 1024:S_kbn + pp * 1024 + 1024]
                        .rearrange("p (kc t) -> p kc t", kc=8),
                    in_=h_kbnfm.rearrange("(kc p) t -> p kc t", p=128)[:, :, c * 128:(c + 1) * 128]))
                t_q = SY.dma(lambda h, c=c, pp=pp: h.dma_start(
                    out=arenaB[:, S_q + pp * 1024:S_q + pp * 1024 + 1024]
                        .rearrange("p (kc t) -> p kc t", kc=8),
                    in_=h_qfm.rearrange("(kc p) t -> p kc t", p=128)[:, :, c * 128:(c + 1) * 128]))
                knb = lambda kc: arenaB[:, S_kn + pp * 1024 + kc * 128:S_kn + pp * 1024 + (kc + 1) * 128]
                kbnb = lambda kc: arenaB[:, S_kbn + pp * 1024 + kc * 128:S_kbn + pp * 1024 + (kc + 1) * 128]
                qb = lambda kc: arenaB[:, S_q + pp * 1024 + kc * 128:S_q + pp * 1024 + (kc + 1) * 128]
                TE.wait(t_kn); TE.wait(t_kbn); TE.wait(t_q)
                aT_dma = {}
                for hh in range(H):
                    ps = (pA0, pA1, pB0, pB1)[hh]
                    TE.wait(ev_t[hh])
                    mm(ps[:, 0:128], kbnb(2 * hh), knb(2 * hh), True, False)
                    mm(ps[:, 0:128], kbnb(2 * hh + 1), knb(2 * hh + 1), False, False)
                    mm(ps[:, 128:256], knb(2 * hh), kbnb(2 * hh), False, False)
                    mm(ps[:, 128:256], knb(2 * hh + 1), kbnb(2 * hh + 1), False, False)
                    mm(ps[:, 256:384], knb(2 * hh), qb(2 * hh), False, False)
                    mm(ps[:, 256:384], knb(2 * hh + 1), qb(2 * hh + 1), False, True)
                    t_a = TE.bump()
                    DV.wait(t_a)
                    wb = hh * 1024
                    t_m = DV.do(lambda h, ps=ps, wb=wb: h.tensor_tensor(
                        out=wsm[:, wb:wb + 384], in0=ps[:, 0:384],
                        in1=masks[:, 0:384], op=ALU.mult), inc=1)
                    SY.wait(t_m)
                    inst = c * H + hh
                    aT_dma[hh] = SY.dma(lambda h, inst=inst, wb=wb: h.dma_start(
                        out=h_aT[inst * 128:(inst + 1) * 128, :], in_=wsm[:, wb + 256:wb + 384]))
                    ev_t[hh] = DV.do(lambda h, wb=wb: h.tensor_add(
                        out=wsm[:, wb + 512:wb + 640], in0=ident_bf[:],
                        in1=wsm[:, wb + 128:wb + 256]), inc=1)
                mp = {hh: hh * 1024 + 0 for hh in range(H)}
                qp = {hh: hh * 1024 + 128 for hh in range(H)}
                rr = {hh: hh * 1024 + 512 for hh in range(H)}
                alt = {hh: hh * 1024 + 640 for hh in range(H)}
                for s in range(1, 7):
                    for hh in range(H):
                        ps = (pA0, pA1, pB0, pB1)[hh]
                        ENG = AC if hh % 2 == 0 else DV
                        TE.wait(ev_t[hh])
                        if s == 2:
                            TE.wait(aT_dma[hh])
                        mm(ps[:, 0:128], wsm[:, qp[hh]:qp[hh] + 128],
                           wsm[:, mp[hh]:mp[hh] + 128], True, False)
                        mm(ps[:, 128:256], wsm[:, mp[hh]:mp[hh] + 128],
                           wsm[:, qp[hh]:qp[hh] + 128], False, (s == 1))
                        if s >= 2:
                            mm(ps[:, 256:384], ident_bf[:], wsm[:, rr[hh]:rr[hh] + 128], False, False)
                            mm(ps[:, 256:384], wsm[:, mp[hh]:mp[hh] + 128],
                               wsm[:, rr[hh]:rr[hh] + 128], False, True)
                        t_lvl = TE.bump()
                        ENG.wait(t_lvl)
                        if s == 3:
                            ENG.wait(est_tok.get(hh))
                        nm = alt[hh]
                        if s >= 2:
                            t_ev = ENG.do(lambda h, ps=ps, nm=nm: (
                                h.copy(out=wsm[:, nm:nm + 384], in_=ps[:, 0:384])
                                if h is not DV.h else
                                h.tensor_copy(out=wsm[:, nm:nm + 384], in_=ps[:, 0:384])), inc=1)
                            mp[hh], qp[hh], rr[hh] = nm, nm + 128, nm + 256
                            alt[hh] = hh * 1024 + (640 if nm == hh * 1024 else 0)
                        else:
                            t_ev = ENG.do(lambda h, ps=ps, nm=nm: (
                                h.copy(out=wsm[:, nm:nm + 256], in_=ps[:, 0:256])
                                if h is not DV.h else
                                h.tensor_copy(out=wsm[:, nm:nm + 256], in_=ps[:, 0:256])), inc=1)
                            mp[hh], qp[hh] = nm, nm + 128
                            alt[hh] = hh * 1024 + 0
                        ev_t[hh] = t_ev
                for hh in range(H):
                    ps = (pA0, pA1, pB0, pB1)[hh]
                    ENG = AC if hh % 2 == 0 else DV
                    TE.wait(ev_t[hh])
                    mm(ps[:, 0:128], ident_bf[:], wsm[:, rr[hh]:rr[hh] + 128], True, False)
                    mm(ps[:, 0:128], wsm[:, mp[hh]:mp[hh] + 128],
                       wsm[:, rr[hh]:rr[hh] + 128], False, True)
                    t_e = TE.bump()
                    ENG.wait(t_e)
                    dst = hh * 1024 + 896
                    t_f = ENG.do(lambda h, ps=ps, dst=dst: (
                        h.copy(out=wsm[:, dst:dst + 128], in_=ps[:, 0:128])
                        if h is not DV.h else
                        h.tensor_copy(out=wsm[:, dst:dst + 128], in_=ps[:, 0:128])), inc=1)
                    ev_t[hh] = t_f
                    SY.phase()
                    SY.wait(t_f)
                    inst = c * H + hh
                    est_tok[hh] = SY.dma(lambda h, inst=inst, dst=dst: h.dma_start(
                        out=h_E[inst * 128:(inst + 1) * 128, :], in_=wsm[:, dst:dst + 128]))
            barrier()

        def stage_C_scan():
            dout_dma = {}
            for c in range(NCH):
                pp = c % 2
                SY.phase()
                t_kbn = SY.dma(lambda h, c=c, pp=pp: h.dma_start(
                    out=arenaB[:, S_kbn + pp * 1024:S_kbn + pp * 1024 + 1024]
                        .rearrange("p (kc t) -> p kc t", kc=8),
                    in_=h_kbnfm.rearrange("(kc p) t -> p kc t", p=128)[:, :, c * 128:(c + 1) * 128]))
                t_q = SY.dma(lambda h, c=c, pp=pp: h.dma_start(
                    out=arenaB[:, S_q + pp * 1024:S_q + pp * 1024 + 1024]
                        .rearrange("p (kc t) -> p kc t", kc=8),
                    in_=h_qfm.rearrange("(kc p) t -> p kc t", p=128)[:, :, c * 128:(c + 1) * 128]))
                t_kntm = SY.dma(lambda h, c=c, pp=pp: h.dma_start(
                    out=arenaB[:, S_kntm + pp * 1024:S_kntm + pp * 1024 + 1024],
                    in_=h_kntm[c * 128:(c + 1) * 128, :]))
                t_vb = SY.dma(lambda h, c=c, pp=pp: h.dma_start(
                    out=arenaB[:, S_vb + pp * 1024:S_vb + pp * 1024 + 1024],
                    in_=h_vbtm[c * 128:(c + 1) * 128, :]))
                t_E = SY.dma(lambda h, c=c, pp=pp: h.dma_start(
                    out=arenaB[:, S_E + pp * 512:S_E + pp * 512 + 512]
                        .rearrange("p (hh e) -> p hh e", hh=4),
                    in_=h_E[c * H * 128:(c + 1) * H * 128, :].rearrange("(hh p) e -> p hh e", p=128)))
                t_aT = SY.dma(lambda h, c=c, pp=pp: h.dma_start(
                    out=arenaB[:, S_aT + pp * 512:S_aT + pp * 512 + 512]
                        .rearrange("p (hh e) -> p hh e", hh=4),
                    in_=h_aT[c * H * 128:(c + 1) * H * 128, :].rearrange("(hh p) e -> p hh e", p=128)))
                kbnb = lambda kc: arenaB[:, S_kbn + pp * 1024 + kc * 128:S_kbn + pp * 1024 + (kc + 1) * 128]
                qb = lambda kc: arenaB[:, S_q + pp * 1024 + kc * 128:S_q + pp * 1024 + (kc + 1) * 128]
                kntm = arenaB[:, S_kntm + pp * 1024:S_kntm + pp * 1024 + 1024]
                vbt = arenaB[:, S_vb + pp * 1024:S_vb + pp * 1024 + 1024]
                Eb = lambda hh: arenaB[:, S_E + pp * 512 + hh * 128:S_E + pp * 512 + (hh + 1) * 128]
                aTb = lambda hh: arenaB[:, S_aT + pp * 512 + hh * 128:S_aT + pp * 512 + (hh + 1) * 128]
                dout = arenaB[:, S_dout + pp * 1024:S_dout + pp * 1024 + 1024]
                for t_ in (t_kbn, t_q, t_kntm, t_E, t_aT):
                    TE.wait(t_)
                DV.wait(t_vb)
                for hh in range(H):
                    ps = pA0 if hh < 2 else pB0
                    off = (hh % 2) * 256
                    mm(ps[:, off:off + 256], kbnb(2 * hh), S_bf[:, hh * 256:hh * 256 + 256],
                       start=(off == 0), stop=False)
                    mm(ps[:, off:off + 256], kbnb(2 * hh + 1),
                       S_bf[:, 1024 + hh * 256:1024 + hh * 256 + 256], False, (off != 0))
                t_g = TE.bump()
                DV.wait(t_g)
                for hh in range(H):
                    ps = pA0 if hh < 2 else pB0
                    off = (hh % 2) * 256
                    DV.do(lambda h, ps=ps, off=off, hh=hh: h.scalar_tensor_tensor(
                        out=wsm[:, hh * 256:(hh + 1) * 256], in0=ps[:, off:off + 256],
                        scalar=-1.0, in1=vbt[:, hh * 256:(hh + 1) * 256],
                        op0=ALU.mult, op1=ALU.add), inc=1)
                t_gev = DV.tok()
                TE.wait(t_gev)
                for hh in range(H):
                    pc = PC[hh // 2]
                    mm(pc[:, (hh % 2) * 256:(hh % 2) * 256 + 256], Eb(hh),
                       wsm[:, hh * 256:(hh + 1) * 256],
                       start=(hh % 2 == 0), stop=(hh % 2 == 1))
                t_ui = TE.bump()
                AC.wait(t_ui)
                AC.do(lambda h: h.copy(out=wsm[:, 1024:1536], in_=pC0[:, 0:512]), inc=1)
                t_uev = AC.do(lambda h: h.copy(out=wsm[:, 1536:2048], in_=pC1[:, 0:512]), inc=1)
                TE.wait(t_uev)
                for hh in range(H):
                    ps = pA0 if hh < 2 else pB0
                    off = (hh % 2) * 256
                    mm(ps[:, off:off + 256], qb(2 * hh), S_bf[:, hh * 256:hh * 256 + 256],
                       start=(off == 0), stop=False)
                    mm(ps[:, off:off + 256], qb(2 * hh + 1),
                       S_bf[:, 1024 + hh * 256:1024 + hh * 256 + 256], False, False)
                    mm(ps[:, off:off + 256], aTb(hh), wsm[:, 1024 + hh * 256:1024 + (hh + 1) * 256],
                       False, (off != 0))
                t_oi = TE.bump()
                AC.wait(t_oi)
                AC.wait(dout_dma.get(c - 2))
                for hh in range(H):
                    ps = pA0 if hh < 2 else pB0
                    off = (hh % 2) * 256
                    AC.do(lambda h, ps=ps, off=off, hh=hh, c=c: h.activation(
                        out=dout[:, hh * 256:(hh + 1) * 256], in_=ps[:, off:off + 256],
                        func=AF.Copy, scale=stm[:, c * 32 + T_rq + hh:c * 32 + T_rq + hh + 1]), inc=1)
                t_oev = AC.tok()
                SY.wait(t_oev)
                dout_dma[c] = SY.dma(lambda h, c=c, dout=dout: h.dma_start(
                    out=h_delta[c * 128:(c + 1) * 128, :], in_=dout[:]))
                prev_sup = {0: None, 1: None}
                for hh in range(H):
                    hp = hh % 2
                    pbank = p1a if hp == 0 else p1b
                    TE.wait(prev_sup[hp])
                    mm(pbank[:, 0:256], kntm[:, hh * 256:hh * 256 + 128],
                       wsm[:, 1024 + hh * 256:1024 + (hh + 1) * 256], True, False)
                    mm(pbank[:, 256:512], kntm[:, hh * 256 + 128:hh * 256 + 256],
                       wsm[:, 1024 + hh * 256:1024 + (hh + 1) * 256], False, True)
                    t_ds = TE.bump()
                    DV.wait(t_ds)
                    for half in range(2):
                        DV.do(lambda h, hh=hh, half=half, pbank=pbank: h.tensor_add(
                            out=S_f32[:, half * 1024 + hh * 256:half * 1024 + (hh + 1) * 256],
                            in0=S_f32[:, half * 1024 + hh * 256:half * 1024 + (hh + 1) * 256],
                            in1=pbank[:, half * 256:(half + 1) * 256]), inc=1)
                    t_sup = DV.tok()
                    prev_sup[hp] = t_sup
                    AC.wait(t_sup)
                    for half in range(2):
                        AC.do(lambda h, hh=hh, half=half: h.copy(
                            out=S_bf[:, half * 1024 + hh * 256:half * 1024 + (hh + 1) * 256],
                            in_=S_f32[:, half * 1024 + hh * 256:half * 1024 + (hh + 1) * 256]), inc=1)
                TE.wait(AC.tok())
            barrier()

        def stage_E():
            GD.phase()
            GD.dma(lambda h: h.dma_start(
                out=arenaW[:, W_wg1:W_wg1 + 8192].rearrange("p (kc j) -> p kc j", kc=8),
                in_=Wg1_h[0:1024].rearrange("(kc p) j -> p kc j", p=128)))
            GD.dma(lambda h: h.dma_start(
                out=arenaW[:, W_wg1 + 8192:W_wg1 + 9216], in_=Wg1t_h))
            GD.dma(lambda h: h.dma_start(
                out=arenaW[:, W_wg2:W_wg2 + 32].rearrange("p (kc j) -> p kc j", kc=8),
                in_=Wg2p_h.rearrange("(kc p) j -> p kc j", p=128)))
            GD.dma(lambda h: h.dma_start(
                out=arenaW[:, W_wog1:W_wog1 + 4096].rearrange("p (kc j) -> p kc j", kc=8),
                in_=Wog1_h[0:1024].rearrange("(kc p) j -> p kc j", p=128)))
            GD.dma(lambda h: h.dma_start(
                out=arenaW[:, W_wog1 + 4096:W_wog1 + 4608], in_=Wog1t_h))
            GD.dma(lambda h: h.dma_start(
                out=arenaW[:, W_wog2:W_wog2 + 4].rearrange("p (kc j) -> p kc j", kc=4),
                in_=Wog2_h.rearrange("(kc p) j -> p kc j", p=128),
                allow_slow_non_contiguous=True))
            t_wo = GD.dma(lambda h: h.dma_start(
                out=arenaW[:, W_wo:W_wo + 8192].rearrange("p (kc j) -> p kc j", kc=8),
                in_=Wo_h.rearrange("(kc p) j -> p kc j", p=128)))
            GD.dma(lambda h: h.dma_start(
                out=arenaW[:, W_vT:W_vT + 128].rearrange("p (kc j) -> p kc j", kc=8),
                in_=vT_h.rearrange("(kc p) j -> p kc j", p=128)))
            GD.dma(lambda h: h.dma_start(out=arenaW[0:16, W_uT:W_uT + 4096], in_=uT_h))
            GD.dma(lambda h: h.dma_start(out=arenaW[0:16, W_fbT:W_fbT + 16], in_=fbT_h))
            for fc in range(NFM):
                SY.dma(lambda h, fc=fc: h.dma_start_transpose(
                    out=arenaB[:, B_g1 + fc * 2048:B_g1 + (fc + 1) * 2048],
                    in_=h_xbf[:, fc * 128:(fc + 1) * 128]))
            AC.wait(t_wo)
            for kc in range(NFM):
                AC.do(lambda h, kc=kc: h.activation(
                    out=arenaW[:, W_wo + kc * 1024:W_wo + (kc + 1) * 1024],
                    in_=arenaW[:, W_wo + kc * 1024:W_wo + (kc + 1) * 1024],
                    func=AF.Copy, scale=consts[:, C_nw + kc:C_nw + kc + 1]), inc=1)
            pre0 = arenaB[:, B_mix:B_mix + 2048]
            pre1 = arenaB[:, B_mix + 2048:B_mix + 4096]
            presE = [(arenaB[:, B_gg:B_gg + 2048], arenaB[:, B_gg + 2048:B_gg + 4096]),
                     (arenaB[:, B_gg + 4096:B_gg + 6144], arenaB[:, B_gg + 6144:B_gg + 8192])]
            def get_d(fc):
                pg = fc % 2
                SY.phase()
                SY.wait(TE.tok())
                tok = SY.dma(lambda h, fc=fc, pg=pg: h.dma_start_transpose(
                    out=arenaB[:, B_mix + pg * 2048:B_mix + pg * 2048 + 2048],
                    in_=h_delta[:, fc * 128:(fc + 1) * 128]))
                return arenaB[:, B_mix + pg * 2048:B_mix + pg * 2048 + 2048], tok
            stats_fm2(get_d, (32, 96 + 32), None, False, presE)
            t_cp = DV.do(lambda h: h.tensor_copy(out=srows_bf[R_gpark:R_gpark + 16, :],
                                                 in_=srows[R_gpark:R_gpark + 16, :]), inc=1)
            TE.wait(t_cp)
            prev_ev = {0: None, 1: None}
            for fc in range(NFM):
                for th in range(2):
                    par = (fc * 2 + th) % 2
                    ps = PA if par == 0 else PB
                    TE.wait(prev_ev[par])
                    for kc in range(NFM):
                        mm(ps,
                           arenaW[:, W_wg1 + kc * 1024 + fc * 128:W_wg1 + kc * 1024 + (fc + 1) * 128],
                           arenaB[:, B_g1 + kc * 2048 + th * 1024:B_g1 + kc * 2048 + (th + 1) * 1024],
                           start=(kc == 0), stop=False)
                    mm(ps,
                       arenaW[:, W_wg1 + 8192 + fc * 128:W_wg1 + 8192 + (fc + 1) * 128],
                       srows_bf[:, th * 1024:(th + 1) * 1024],
                       False, True)
                    t1 = TE.bump()
                    AC.wait(t1)
                    for hf in range(2):
                        t2 = AC.do(lambda h, fc=fc, th=th, ps=ps, hf=hf: h.activation(
                            out=arenaB[:, B_gg + fc * 2048 + th * 1024 + hf * 512:B_gg + fc * 2048 + th * 1024 + (hf + 1) * 512],
                            in_=ps[hf][:, 0:512], func=AF.Gelu,
                            bias=consts[:, C_bg1 + fc:C_bg1 + fc + 1]), inc=1)
                    prev_ev[par] = t2
            TE.wait(prev_ev[0])
            TE.wait(prev_ev[1])
            AC.do(lambda h: h.activation(out=consts[0:4, C_tmp:C_tmp + 1],
                                         in_=consts[0:4, C_lt:C_lt + 1], func=AF.Exp), inc=1)
            t_e1 = AC.tok()
            DV.wait(t_e1)
            t_e2 = DV.do(lambda h: h.tensor_scalar(out=consts[0:4, C_tmp:C_tmp + 1],
                                                   in0=consts[0:4, C_tmp:C_tmp + 1],
                                                   scalar1=1.0, scalar2=None, op0=ALU.add), inc=1)
            AC.wait(t_e2)
            t_sp = AC.do(lambda h: h.activation(out=consts[0:4, C_tmp:C_tmp + 1],
                                                in_=consts[0:4, C_tmp:C_tmp + 1], func=AF.Ln), inc=1)
            DV.wait(t_sp)
            DV.do(lambda h: h.reciprocal(out=consts[0:4, C_rsp:C_rsp + 1],
                                         in_=consts[0:4, C_tmp:C_tmp + 1]), inc=1)
            t_rsp = DV.do(lambda h: h.tensor_mul(out=consts[0:4, C_b2r:C_b2r + 1],
                                                 in0=consts[0:4, C_bg2:C_bg2 + 1],
                                                 in1=consts[0:4, C_rsp:C_rsp + 1]), inc=1)
            AC.wait(t_rsp)
            for q in range(4):
                for kc in range(NFM):
                    mm(p1a[0:4, 0:512], arenaW[:, W_wg2 + kc * 4:W_wg2 + (kc + 1) * 4],
                       arenaB[:, B_gg + kc * 2048 + q * 512:B_gg + kc * 2048 + (q + 1) * 512],
                       start=(kc == 0), stop=(kc == NFM - 1))
                t1 = TE.bump()
                DV.wait(t1)
                t2 = DV.do(lambda h, q=q: h.scalar_tensor_tensor(
                    out=srows[R_logit:R_logit + 4, q * 512:(q + 1) * 512], in0=p1a[0:4, 0:512],
                    scalar=consts[0:4, C_rsp:C_rsp + 1],
                    in1=consts[0:4, C_b2r:C_b2r + 1].broadcast_to([4, 512]),
                    op0=ALU.mult, op1=ALU.add), inc=1)
                TE.wait(t2)
            barrier()
            small_rt(R_logit, 4, T_p)
            sp3 = stm[:, 0:512].rearrange("p (t c) -> p t c", c=32)
            P_ = sp3[:, :, T_p:T_p + 4]
            PLS_ = sp3[:, :, T_pls:T_pls + 4]
            mcol = lambda off: aF(F_acc + off, 16).rearrange(
                "p (t o) -> p t o", o=1).broadcast_to([128, 16, 4])
            DV.do(lambda h: h.tensor_reduce(out=aF(F_acc, 16), in_=P_, axis=AX.X,
                                            op=ALU.max), inc=1)
            DV.do(lambda h: h.tensor_tensor(out=P_, in0=P_, in1=mcol(0), op=ALU.subtract), inc=1)
            t1 = DV.tok()
            AC.wait(t1)
            t2 = AC.do(lambda h: h.activation(out=P_, in_=P_, func=AF.Exp), inc=1)
            DV.wait(t2)
            DV.do(lambda h: h.tensor_reduce(out=aF(F_acc, 16), in_=P_, axis=AX.X,
                                            op=ALU.add), inc=1)
            DV.do(lambda h: h.reciprocal(out=aF(F_acc, 16), in_=aF(F_acc, 16)), inc=1)
            DV.do(lambda h: h.tensor_tensor(out=P_, in0=P_, in1=mcol(0), op=ALU.mult), inc=1)
            DV.do(lambda h: h.tensor_scalar(out=P_, in0=P_, scalar1=0.02, scalar2=None,
                                            op0=ALU.max), inc=1)
            DV.do(lambda h: h.tensor_reduce(out=aF(F_acc, 16), in_=P_, axis=AX.X,
                                            op=ALU.add), inc=1)
            DV.do(lambda h: h.reciprocal(out=aF(F_acc, 16), in_=aF(F_acc, 16)), inc=1)
            DV.do(lambda h: h.tensor_tensor(out=P_, in0=P_, in1=mcol(0), op=ALU.mult), inc=1)
            DV.do(lambda h: h.tensor_tensor(
                out=PLS_, in0=PLS_,
                in1=sp3[:, :, T_p:T_p + 1].broadcast_to([128, 16, 4]), op=ALU.add), inc=1)
            barrier()
            mix_dv = {}
            st_tok = {}
            for t in range(NTM):
                pg = t % 2
                SY.phase()
                SY.wait(mix_dv.get(t - 2))
                t_ls = SY.dma(lambda h, t=t, pg=pg: h.dma_start_transpose(
                    out=arenaB[:, B_mix + pg * 1024:B_mix + pg * 1024 + 1024],
                    in_=h_lsfm[:, t * 128:(t + 1) * 128]))
                GD.phase()
                GP.wait(mix_dv.get(t - 2))
                GD.dma(lambda h, t=t, pg=pg: h.dma_start(
                    out=arenaB[:, B_mix + 2048 + pg * 1024:B_mix + 2048 + pg * 1024 + 1024],
                    in_=h_ll[t * 128:(t + 1) * 128, :]))
                GD.dma(lambda h, t=t, pg=pg: h.dma_start(
                    out=arenaB[:, B_mix + 4096 + pg * 1024:B_mix + 4096 + pg * 1024 + 1024],
                    in_=h_delta[t * 128:(t + 1) * 128, :]))
                t_vv = GD.dma(lambda h, t=t, pg=pg: h.dma_start(
                    out=arenaB[:, B_otm + 1024 + pg * 1024:B_otm + 1024 + pg * 1024 + 1024],
                    in_=h_vtm[t * 128:(t + 1) * 128, :]))
                lsb = arenaB[:, B_mix + pg * 1024:B_mix + pg * 1024 + 1024]
                llb = arenaB[:, B_mix + 2048 + pg * 1024:B_mix + 2048 + pg * 1024 + 1024]
                ddb = arenaB[:, B_mix + 4096 + pg * 1024:B_mix + 4096 + pg * 1024 + 1024]
                vvb = arenaB[:, B_otm + 1024 + pg * 1024:B_otm + 1024 + pg * 1024 + 1024]
                DV.wait(t_ls)
                DV.wait(t_vv)
                DV.wait(st_tok.get(t - 2))
                for hh in range(H):
                    DV.do(lambda h, hh=hh, lsb=lsb, t=t, pg=pg: h.tensor_scalar(
                        out=aF(F_acc + pg * 1024 + hh * 256, 256),
                        in0=lsb[:, hh * 256:(hh + 1) * 256],
                        scalar1=stm[:, t * 32 + T_pls + hh:t * 32 + T_pls + hh + 1],
                        scalar2=None, op0=ALU.mult), inc=1)
                DV.do(lambda h, llb=llb, t=t, pg=pg: h.scalar_tensor_tensor(
                    out=aF(F_acc + pg * 1024, 1024), in0=llb,
                    scalar=stm[:, t * 32 + T_p + 1:t * 32 + T_p + 2],
                    in1=aF(F_acc + pg * 1024, 1024), op0=ALU.mult, op1=ALU.add), inc=1)
                DV.do(lambda h, ddb=ddb, t=t, pg=pg: h.scalar_tensor_tensor(
                    out=aF(F_acc + pg * 1024, 1024), in0=ddb,
                    scalar=stm[:, t * 32 + T_p + 2:t * 32 + T_p + 3],
                    in1=aF(F_acc + pg * 1024, 1024), op0=ALU.mult, op1=ALU.add), inc=1)
                t_ot = DV.do(lambda h, vvb=vvb, t=t, pg=pg: h.scalar_tensor_tensor(
                    out=aF(F_acc + pg * 1024, 1024), in0=vvb,
                    scalar=stm[:, t * 32 + T_p + 3:t * 32 + T_p + 4],
                    in1=aF(F_acc + pg * 1024, 1024), op0=ALU.mult, op1=ALU.add), inc=1)
                mix_dv[t] = t_ot
                GD.phase()
                GP.wait(t_ot)
                st_tok[t] = GD.dma(lambda h, t=t, pg=pg: h.dma_start(
                    out=h_o[t * 128:(t + 1) * 128, :], in_=aF(F_acc + pg * 1024, 1024)))
            barrier()
            def get_o(fc):
                pg = fc % 2
                SY.phase()
                SY.wait(TE.tok())
                tok = SY.dma(lambda h, fc=fc, pg=pg: h.dma_start_transpose(
                    out=arenaB[:, B_mix + pg * 2048:B_mix + pg * 2048 + 2048],
                    in_=h_o[:, fc * 128:(fc + 1) * 128]))
                return arenaB[:, B_mix + pg * 2048:B_mix + pg * 2048 + 2048], tok
            stats_fm2(get_o, None, (64, 160), True, presE)
            t_cg = DV.do(lambda h: h.tensor_copy(out=srows_bf[R_ogpark:R_ogpark + 8, :],
                                                 in_=srows[R_ogpark:R_ogpark + 8, :]), inc=1)
            TE.wait(t_cg)
            og_ps = [p1a[0:1, 0:512], p1b[0:1, 0:512], pC0[0:1, 0:512], pC1[0:1, 0:512]]
            prev_ev = {0: None, 1: None}
            prev_g2 = {0: None, 1: None}
            for fo in range(4):
                g2buf = arenaB[:, B_mix + (fo % 2) * 2048:B_mix + (fo % 2) * 2048 + 2048]
                for th in range(2):
                    ps = PA if th == 0 else PB
                    TE.wait(prev_ev[th])
                    for kc in range(NFM):
                        mm(ps,
                           arenaW[:, W_wog1 + kc * 512 + fo * 128:W_wog1 + kc * 512 + (fo + 1) * 128],
                           arenaB[:, B_g1 + kc * 2048 + th * 1024:B_g1 + kc * 2048 + (th + 1) * 1024],
                           start=(kc == 0), stop=False)
                    mm(ps,
                       arenaW[:, W_wog1 + 4096 + fo * 128:W_wog1 + 4096 + (fo + 1) * 128],
                       srows_bf[:, th * 1024:(th + 1) * 1024],
                       False, True)
                    t1 = TE.bump()
                    AC.wait(t1)
                    AC.wait(prev_g2[fo % 2])
                    for hf in range(2):
                        t2 = AC.do(lambda h, fo=fo, th=th, ps=ps, g2buf=g2buf, hf=hf: h.activation(
                            out=g2buf[:, th * 1024 + hf * 512:th * 1024 + (hf + 1) * 512],
                            in_=ps[hf][:, 0:512], func=AF.Gelu,
                            bias=consts[:, C_bog1 + fo:C_bog1 + fo + 1]), inc=1)
                    prev_ev[th] = t2
                TE.wait(prev_ev[0])
                TE.wait(prev_ev[1])
                for q in range(4):
                    mm(og_ps[q], arenaW[:, W_wog2 + fo:W_wog2 + fo + 1],
                       g2buf[:, q * 512:(q + 1) * 512],
                       start=(fo == 0), stop=(fo == 3))
                prev_g2[fo % 2] = TE.bump()
            t_og = TE.tok()
            AC.wait(t_og)
            for q in range(4):
                AC.do(lambda h, q=q: h.activation(
                    out=srows[R_og:R_og + 1, q * 512:(q + 1) * 512], in_=og_ps[q],
                    func=AF.Sigmoid, bias=consts[0:1, C_bog2:C_bog2 + 1]), inc=1)
            barrier()
            small_rt(0, 8, T_og)
            sg3 = stm[:, 0:512].rearrange("p (t c) -> p t c", c=32)
            OG_ = sg3[:, :, T_og:T_og + 1]
            SC_ = sg3[:, :, T_sc:T_sc + 4]
            DV.do(lambda h: h.tensor_tensor(
                out=sg3[:, :, T_tmp:T_tmp + 1], in0=OG_, in1=OG_, op=ALU.mult), inc=1)
            DV.do(lambda h: h.tensor_tensor(
                out=SC_, in0=sg3[:, :, T_rrms:T_rrms + 4],
                in1=sg3[:, :, T_tmp:T_tmp + 1].broadcast_to([128, 16, 4]),
                op=ALU.mult), inc=1)
            t_ms = DV.tok()
            AC.wait(t_ms)
            AC.do(lambda h: h.activation(
                out=SC_, in_=SC_, func=AF.Sqrt, bias=consts[:, C_eln:C_eln + 1]), inc=1)
            t_sq = AC.tok()
            DV.wait(t_sq)
            DV.do(lambda h: h.reciprocal(out=SC_, in_=SC_), inc=1)
            DV.do(lambda h: h.tensor_tensor(
                out=SC_, in0=SC_,
                in1=OG_.broadcast_to([128, 16, 4]), op=ALU.mult), inc=1)
            on_dv = {}
            on_st = {}
            for t in range(NTM):
                pg = t % 2
                SY.phase()
                SY.wait(on_dv.get(t - 2))
                t_o = SY.dma(lambda h, t=t, pg=pg: h.dma_start(
                    out=arenaB[:, B_mix + pg * 1024:B_mix + pg * 1024 + 1024],
                    in_=h_o[t * 128:(t + 1) * 128, :]))
                DV.wait(t_o)
                DV.wait(on_st.get(t - 2))
                for hh in range(H):
                    DV.do(lambda h, t=t, pg=pg, hh=hh: h.tensor_scalar(
                        out=arenaB[:, B_mix + 2048 + pg * 1024 + hh * 256:B_mix + 2048 + pg * 1024 + (hh + 1) * 256],
                        in0=arenaB[:, B_mix + pg * 1024 + hh * 256:B_mix + pg * 1024 + (hh + 1) * 256],
                        scalar1=stm[:, t * 32 + T_sc + hh:t * 32 + T_sc + hh + 1],
                        scalar2=None, op0=ALU.mult), inc=1)
                t_on = DV.tok()
                on_dv[t] = t_on
                SY.phase()
                SY.wait(t_on)
                on_st[t] = SY.dma(lambda h, t=t, pg=pg: h.dma_start(
                    out=h_on[t * 128:(t + 1) * 128, :],
                    in_=arenaB[:, B_mix + 2048 + pg * 1024:B_mix + 2048 + pg * 1024 + 1024]))
            barrier()
            for fc in range(NFM):
                SY.dma(lambda h, fc=fc: h.dma_start_transpose(
                    out=arenaB[:, B_onfm + fc * 2048:B_onfm + (fc + 1) * 2048],
                    in_=h_on[:, fc * 128:(fc + 1) * 128]))
            barrier()
            prev_res = {0: None, 1: None}
            cp_tok = {0: None, 1: None}
            st_tok = {0: None, 1: None}
            bst_tok = {0: None, 1: None}
            for tc in range(NTM):
                par = tc % 2
                ps = PA if par == 0 else PB
                ln = aF(F_ln, 1024) if par == 0 else aF(F_acc, 1024)
                mb = F_m + par * 4
                SY.phase()
                SY.wait(cp_tok[par])
                SY.wait(st_tok[par])
                t_x = SY.dma(lambda h, tc=tc, ln=ln: h.dma_start(
                    out=ln, in_=x_in[tc * 128:(tc + 1) * 128, :]))
                TE.wait(prev_res[par])
                for kc in range(NFM):
                    mm(ps,
                       arenaB[:, B_onfm + kc * 2048 + tc * 128:B_onfm + kc * 2048 + (tc + 1) * 128],
                       arenaW[:, W_wo + kc * 1024:W_wo + (kc + 1) * 1024],
                       start=(kc == 0), stop=(kc == NFM - 1))
                t_mm2 = TE.bump()
                DV.wait(t_mm2); DV.wait(t_x)
                DV.do(lambda h, ps=ps, ln=ln: h.tensor_add(out=ln[:, 0:512], in0=ln[:, 0:512],
                                                           in1=ps[0][:, 0:512]), inc=1)
                DV.do(lambda h, ps=ps, ln=ln: h.tensor_add(out=ln[:, 512:1024], in0=ln[:, 512:1024],
                                                           in1=ps[1][:, 0:512]), inc=1)
                t_res = DV.tok()
                prev_res[par] = t_res
                DV.do(lambda h, ln=ln, mb=mb: h.reduce_sum(out=aF(mb, 1), in_=ln, axis=AX.X), inc=1)
                AC.wait(DV.tok())
                t_sq = AC.do(lambda h, ln=ln, mb=mb: h.activation(
                    out=aF(F_acc + 1024, 1024), in_=ln, func=AF.Square,
                    accum_out=aF(mb + 1, 1)), inc=1)
                DV.wait(t_sq)
                DV.do(lambda h, mb=mb: h.tensor_scalar(out=aF(mb, 1), in0=aF(mb, 1),
                                                       scalar1=1.0 / 1024, scalar2=None,
                                                       op0=ALU.mult), inc=1)
                DV.do(lambda h, mb=mb: h.tensor_mul(out=aF(mb + 2, 1), in0=aF(mb, 1),
                                                    in1=aF(mb, 1)), inc=1)
                t_v2 = DV.do(lambda h, mb=mb: h.scalar_tensor_tensor(
                    out=aF(mb + 3, 1), in0=aF(mb + 1, 1), scalar=1.0 / 1024,
                    in1=aF(mb + 2, 1), op0=ALU.mult, op1=ALU.subtract), inc=1)
                AC.wait(t_v2)
                t_rs = AC.do(lambda h, mb=mb: h.activation(out=aF(mb + 3, 1), in_=aF(mb + 3, 1),
                                                           func=AF.Sqrt,
                                                           bias=consts[:, C_eln:C_eln + 1]), inc=1)
                DV.wait(t_rs)
                DV.do(lambda h, mb=mb: h.reciprocal(out=aF(mb + 3, 1), in_=aF(mb + 3, 1)), inc=1)
                t_ap = DV.do(lambda h, ln=ln, mb=mb: h.tensor_scalar(
                    out=ln, in0=ln,
                    scalar1=aF(mb, 1), scalar2=aF(mb + 3, 1),
                    op0=ALU.subtract, op1=ALU.mult), inc=1)
                SY.phase()
                SY.wait(t_ap)
                st_tok[par] = SY.dma(lambda h, tc=tc, ln=ln: h.dma_start(
                    out=h_x2[tc * 128:(tc + 1) * 128, :], in_=ln))
                DV.wait(bst_tok[par])
                t_bf = DV.do(lambda h, ln=ln, par=par: h.tensor_copy(
                    out=wsm[:, par * 1024:(par + 1) * 1024], in_=ln), inc=1)
                cp_tok[par] = t_bf
                SY.phase()
                SY.wait(t_bf)
                bst_tok[par] = SY.dma(lambda h, tc=tc, par=par: h.dma_start(
                    out=h_x2bf[tc * 128:(tc + 1) * 128, :], in_=wsm[:, par * 1024:(par + 1) * 1024]))
            barrier()


        QW = 0
        q_tok = {}
        q_red = {}
        q_state = {"k": 0}

        def _q_buf(k):
            pg = k % 2
            return arenaB[:, B_lsld + pg * 2048:B_lsld + pg * 2048 + 2048].bitcast(F32)[:, 0:1024]

        def _q_load(k):
            qbuf = _q_buf(k)
            GD.phase()
            GP.wait(q_red.get(k - 2))
            if k < 32:
                jc = k
                q_tok[k] = GD.dma(lambda h, jc=jc, qbuf=qbuf: h.dma_start(
                    out=qbuf, in_=wfc_h[jc * 128:(jc + 1) * 128, :]))
            else:
                pc, qq = (k - 32) // 4, (k - 32) % 4
                q_tok[k] = GD.dma(lambda h, pc=pc, qq=qq, qbuf=qbuf: h.dma_start(
                    out=qbuf,
                    in_=wpj_h[pc * 128:(pc + 1) * 128, qq * 1024:(qq + 1) * 1024]))

        def _q_reduce(k):
            qbuf = _q_buf(k)
            DV.wait(q_tok[k])
            if k < 32:
                jc = k
                q_red[k] = DV.do(lambda h, jc=jc, qbuf=qbuf: h.tensor_reduce(
                    out=aF(F_sc + jc, 1), in_=qbuf, axis=AX.X,
                    op=ALU.add, apply_absolute_value=True), inc=1)
            else:
                pc, qq = (k - 32) // 4, (k - 32) % 4
                if qq == 0:
                    q_red[k] = DV.do(lambda h, pc=pc, qbuf=qbuf: h.tensor_reduce(
                        out=aF(F_sc + 32 + pc, 1), in_=qbuf, axis=AX.X,
                        op=ALU.add, apply_absolute_value=True), inc=1)
                else:
                    DV.do(lambda h, qbuf=qbuf: h.tensor_reduce(
                        out=aF(F_sc + 48, 1), in_=qbuf, axis=AX.X,
                        op=ALU.add, apply_absolute_value=True), inc=1)
                    q_red[k] = DV.do(lambda h, pc=pc: h.tensor_add(
                        out=aF(F_sc + 32 + pc, 1), in0=aF(F_sc + 32 + pc, 1),
                        in1=aF(F_sc + 48, 1)), inc=1)

        N_QITEMS = 64
        def q_hook():
            for _ in range(3):
                k = q_state["k"]
                if k < N_QITEMS:
                    if k - 2 >= 0:
                        _q_reduce(k - 2)
                    _q_load(k)
                    q_state["k"] = k + 1

        def stage_Qscale():
            while q_state["k"] < N_QITEMS:
                q_hook()
            for k in (N_QITEMS - 2, N_QITEMS - 1):
                if k not in q_red:
                    _q_reduce(k)

        def stage_Qfc():
            for jc in range(32):
                pg = jc % 2
                SY.phase()
                SY.wait(DV.tok())
                t_w = SY.dma(lambda h, jc=jc, pg=pg: h.dma_start(
                    out=aF(F_acc + pg * 1024, 1024),
                    in_=wfc_h[jc * 128:(jc + 1) * 128, :]))
                wb = aF(F_acc + pg * 1024, 1024)
                DV.do(lambda h, jc=jc: h.tensor_scalar(
                    out=aF(F_sc + jc, 1), in0=aF(F_sc + jc, 1),
                    scalar1=1.0 / 1024, scalar2=1e-5,
                    op0=ALU.mult, op1=ALU.max), inc=1)
                DV.do(lambda h, jc=jc: h.tensor_scalar(out=aF(F_sc + 50, 1), in0=aF(F_sc + jc, 1),
                                                       scalar1=0.5, scalar2=None, op0=ALU.mult), inc=1)
                DV.do(lambda h, jc=jc: h.tensor_scalar(out=aF(F_sc + 51, 1), in0=aF(F_sc + jc, 1),
                                                       scalar1=-0.5, scalar2=None, op0=ALU.mult), inc=1)
                DV.do(lambda h, jc=jc: h.tensor_scalar(out=aF(F_sc + 52, 1), in0=aF(F_sc + jc, 1),
                                                       scalar1=-1.0, scalar2=None, op0=ALU.mult), inc=1)
                DV.wait(t_w)
                DV.do(lambda h, wb=wb, jc=jc: h.tensor_scalar(
                    out=arenaW[:, QW:QW + 1024], in0=wb, scalar1=aF(F_sc + 50, 1),
                    scalar2=aF(F_sc + jc, 1), op0=ALU.is_gt, op1=ALU.mult), inc=1)
                DV.do(lambda h, wb=wb: h.tensor_scalar(
                    out=arenaW[:, QW + 1024:QW + 2048], in0=wb, scalar1=aF(F_sc + 51, 1),
                    scalar2=aF(F_sc + 52, 1), op0=ALU.is_lt, op1=ALU.mult), inc=1)
                t_t = DV.do(lambda h: h.tensor_add(out=arenaW[:, QW + 2048:QW + 3072],
                                                   in0=arenaW[:, QW:QW + 1024],
                                                   in1=arenaW[:, QW + 1024:QW + 2048]), inc=1)
                SY.phase()
                SY.wait(t_t)
                t_s = SY.dma(lambda h, jc=jc: h.dma_start(
                    out=h_tern[jc * 128:(jc + 1) * 128, :], in_=arenaW[:, QW + 2048:QW + 3072]))
                DV.wait(t_s)

        def stage_Qpj():
            for pc in range(8):
                wb = aF(F_acc, 2048)
                DV.do(lambda h, pc=pc: h.tensor_scalar(
                    out=aF(F_sc + 32 + pc, 1), in0=aF(F_sc + 32 + pc, 1),
                    scalar1=1.0 / 4096, scalar2=1e-5,
                    op0=ALU.mult, op1=ALU.max), inc=1)
                DV.do(lambda h, pc=pc: h.tensor_scalar(out=aF(F_sc + 50, 1),
                                                       in0=aF(F_sc + 32 + pc, 1),
                                                       scalar1=0.5, scalar2=None, op0=ALU.mult), inc=1)
                DV.do(lambda h, pc=pc: h.tensor_scalar(out=aF(F_sc + 51, 1),
                                                       in0=aF(F_sc + 32 + pc, 1),
                                                       scalar1=-0.5, scalar2=None, op0=ALU.mult), inc=1)
                for half in (1, 0):
                    SY.phase()
                    SY.wait(DV.tok())
                    t_w = SY.dma(lambda h, pc=pc, half=half: h.dma_start(
                        out=wb, in_=wpj_h[pc * 128:(pc + 1) * 128,
                                          half * 2048:(half + 1) * 2048]))
                    DV.wait(t_w)
                    DV.do(lambda h, wb=wb: h.tensor_scalar(
                        out=arenaW[:, QW:QW + 2048], in0=wb, scalar1=aF(F_sc + 50, 1),
                        scalar2=1.0, op0=ALU.is_gt, op1=ALU.mult), inc=1)
                    DV.do(lambda h, wb=wb: h.tensor_scalar(
                        out=arenaW[:, QW + 2048:QW + 4096], in0=wb, scalar1=aF(F_sc + 51, 1),
                        scalar2=-1.0, op0=ALU.is_lt, op1=ALU.mult), inc=1)
                    t_a = DV.do(lambda h: h.tensor_add(out=arenaW[:, QW:QW + 2048],
                                                       in0=arenaW[:, QW:QW + 2048],
                                                       in1=arenaW[:, QW + 2048:QW + 4096]), inc=1)
                    SY.phase()
                    SY.wait(t_a)
                    t_s = SY.dma(lambda h, pc=pc, half=half: h.dma_start(
                        out=h_ternp[pc * 128:(pc + 1) * 128, half * 2048:(half + 1) * 2048],
                        in_=arenaW[:, QW:QW + 2048]))
                    DV.wait(t_s)

        def stage_F():
            barrier()
            for dc in range(NFM):
                SY.dma(lambda h, dc=dc: h.dma_start_transpose(
                    out=arenaW[:, W_tern + dc * 4096:W_tern + (dc + 1) * 4096],
                    in_=h_tern[:, dc * 128:(dc + 1) * 128]))
            for fc in range(NFM):
                SY.dma(lambda h, fc=fc: h.dma_start_transpose(
                    out=arenaB[:, B_x2fm + fc * 2048:B_x2fm + (fc + 1) * 2048],
                    in_=h_x2bf[:, fc * 128:(fc + 1) * 128]))
            barrier()
            for th in range(2):
                for kc in range(NFM):
                    mm((pC0[0:16, 0:512], pC1[0:16, 0:512]),
                       arenaW[:, W_vT + kc * 16:W_vT + (kc + 1) * 16],
                       arenaB[:, B_x2fm + kc * 2048 + th * 1024:B_x2fm + kc * 2048 + (th + 1) * 1024],
                       start=(kc == 0), stop=(kc == NFM - 1))
                t1 = TE.bump()
                AC.wait(t1)
                AC.do(lambda h, th=th: h.copy(
                    out=wsm[0:16, th * 1024:th * 1024 + 512], in_=pC0[0:16, 0:512]), inc=1)
                t2 = AC.do(lambda h, th=th: h.copy(
                    out=wsm[0:16, th * 1024 + 512:(th + 1) * 1024], in_=pC1[0:16, 0:512]), inc=1)
                TE.wait(t2)
            for th in range(2):
                mm((pC0[0:16, 0:512], pC1[0:16, 0:512]), arenaW[0:16, W_fbT:W_fbT + 16],
                   wsm[0:16, th * 1024:(th + 1) * 1024], True, True)
                t1 = TE.bump()
                AC.wait(t1)
                AC.do(lambda h, th=th: h.activation(
                    out=wsm[0:16, 2048 + th * 1024:2048 + th * 1024 + 512], in_=pC0[0:16, 0:512],
                    func=AF.Copy, scale=consts[0:16, C_gate:C_gate + 1]), inc=1)
                t2 = AC.do(lambda h, th=th: h.activation(
                    out=wsm[0:16, 2048 + th * 1024 + 512:2048 + (th + 1) * 1024], in_=pC1[0:16, 0:512],
                    func=AF.Copy, scale=consts[0:16, C_gate:C_gate + 1]), inc=1)
                TE.wait(t2)
            barrier()
            ln2_state = {"ap": {0: None, 1: None}, "yst": {0: None, 1: None},
                         "add": {0: None, 1: None}}
            def ln2_quarter(tq4, prev_st):
                ap_tok = ln2_state["ap"]
                yst_tok = ln2_state["yst"]
                add_tok = ln2_state["add"]
                SY.phase()
                SY.wait(prev_st[0])
                SY.wait(prev_st[1])
                for tc in range(tq4 * 4, tq4 * 4 + 4):
                    par = tc % 2
                    ln = aF(F_ln, 1024) if par == 0 else aF(F_acc, 1024)
                    mb = F_m + par * 4
                    SY.phase()
                    SY.wait(ap_tok[par])
                    SY.wait(yst_tok[par])
                    t_x = SY.dma(lambda h, tc=tc, ln=ln: h.dma_start(
                        out=ln, in_=h_x2[tc * 128:(tc + 1) * 128, :]))
                    SY.phase()
                    SY.wait(add_tok[par])
                    t_f = SY.dma(lambda h, tc=tc, par=par: h.dma_start_transpose(
                        out=wsm[:, par * 1024:(par + 1) * 1024], in_=h_ffm[:, tc * 128:(tc + 1) * 128]))
                    DV.wait(t_x); DV.wait(t_f)
                    t_add = DV.do(lambda h, ln=ln, par=par: h.tensor_add(
                        out=ln, in0=ln, in1=wsm[:, par * 1024:(par + 1) * 1024]), inc=1)
                    add_tok[par] = t_add
                    DV.do(lambda h, ln=ln, mb=mb: h.reduce_sum(out=aF(mb, 1), in_=ln, axis=AX.X), inc=1)
                    AC.wait(DV.tok())
                    t_sq = AC.do(lambda h, ln=ln, mb=mb: h.activation(
                        out=aF(F_acc + 1024, 1024), in_=ln, func=AF.Square,
                        accum_out=aF(mb + 1, 1)), inc=1)
                    DV.wait(t_sq)
                    DV.do(lambda h, mb=mb: h.tensor_scalar(out=aF(mb, 1), in0=aF(mb, 1),
                                                           scalar1=1.0 / 1024, scalar2=None,
                                                           op0=ALU.mult), inc=1)
                    DV.do(lambda h, mb=mb: h.tensor_mul(out=aF(mb + 2, 1), in0=aF(mb, 1),
                                                        in1=aF(mb, 1)), inc=1)
                    t_v2 = DV.do(lambda h, mb=mb: h.scalar_tensor_tensor(
                        out=aF(mb + 3, 1), in0=aF(mb + 1, 1), scalar=1.0 / 1024,
                        in1=aF(mb + 2, 1), op0=ALU.mult, op1=ALU.subtract), inc=1)
                    AC.wait(t_v2)
                    t_rs = AC.do(lambda h, mb=mb: h.activation(out=aF(mb + 3, 1), in_=aF(mb + 3, 1),
                                                               func=AF.Sqrt,
                                                               bias=consts[:, C_eln:C_eln + 1]), inc=1)
                    DV.wait(t_rs)
                    DV.do(lambda h, mb=mb: h.reciprocal(out=aF(mb + 3, 1), in_=aF(mb + 3, 1)), inc=1)
                    t_ap = DV.do(lambda h, ln=ln, mb=mb: h.tensor_scalar(
                        out=ln, in0=ln,
                        scalar1=aF(mb, 1), scalar2=aF(mb + 3, 1),
                        op0=ALU.subtract, op1=ALU.mult), inc=1)
                    ap_tok[par] = t_ap
                    SY.phase()
                    SY.wait(t_ap)
                    yst_tok[par] = SY.dma(lambda h, tc=tc, ln=ln: h.dma_start(
                        out=y_out[tc * 128:(tc + 1) * 128, :], in_=ln))
            prev_st = {0: None, 1: None}
            for tq4 in range(4):
                th, tq = tq4 // 2, tq4 % 2
                t_ld = {}
                for j0 in range(3):
                    SY.phase()
                    SY.wait(TE.tok())
                    t_ld[j0] = SY.dma(lambda h, j0=j0: h.dma_start_transpose(
                        out=arenaW[:, W_pstr + (j0 % 4) * 1024:W_pstr + (j0 % 4) * 1024 + 1024],
                        in_=h_ternp[:, j0 * 128:(j0 + 1) * 128]))
                prev_sq = {0: None, 1: None, 2: None, 3: None}
                for jc in range(32):
                    pg = jc % 4
                    ps = (pA0, pA1, pB0, pB1)[pg]
                    TE.wait(prev_sq[pg])
                    for kc in range(NFM):
                        mm(ps[:, 0:512],
                           arenaW[:, W_tern + kc * 4096 + jc * 128:W_tern + kc * 4096 + (jc + 1) * 128],
                           arenaB[:, B_x2fm + kc * 2048 + tq4 * 512:B_x2fm + kc * 2048 + (tq4 + 1) * 512],
                           start=(kc == 0), stop=False)
                    mm(ps[:, 0:512], arenaW[0:16, W_uT + jc * 128:W_uT + (jc + 1) * 128],
                       wsm[0:16, 2048 + tq4 * 512:2048 + (tq4 + 1) * 512], False, True)
                    t1 = TE.bump()
                    DV.wait(t1)
                    t_r = DV.do(lambda h, ps=ps, pg=pg: h.tensor_scalar(
                        out=arenaB[:, B_fq + pg * 512:B_fq + pg * 512 + 512], in0=ps[:, 0:512],
                        scalar1=0.0, scalar2=None, op0=ALU.max), inc=1)
                    AC.wait(t_r)
                    t_sq = AC.do(lambda h, jc=jc, pg=pg: h.activation(
                        out=arenaB[:, B_r2 + jc * 512:B_r2 + (jc + 1) * 512],
                        in_=arenaB[:, B_fq + pg * 512:B_fq + pg * 512 + 512], func=AF.Square), inc=1)
                    prev_sq[pg] = t_sq
                for pg in range(4):
                    TE.wait(prev_sq[pg])
                psl = [pA0[:, 0:512], pA1[:, 0:512], pB0[:, 0:512], pB1[:, 0:512],
                       pC0[:, 0:512], pC1[:, 0:512], p1a[:, 0:512], p1b[:, 0:512]]
                for jc in range(32):
                    sl = jc % 4
                    if jc + 3 < 32:
                        SY.phase()
                        SY.wait(TE.tok())
                        t_ld[jc + 3] = SY.dma(lambda h, jc=jc: h.dma_start_transpose(
                            out=arenaW[:, W_pstr + ((jc + 3) % 4) * 1024:W_pstr + ((jc + 3) % 4) * 1024 + 1024],
                            in_=h_ternp[:, (jc + 3) * 128:(jc + 4) * 128]))
                    TE.wait(t_ld[jc])
                    for jout in range(NFM):
                        mm(psl[jout],
                           arenaW[:, W_pstr + sl * 1024 + jout * 128:W_pstr + sl * 1024 + (jout + 1) * 128],
                           arenaB[:, B_r2 + jc * 512:B_r2 + (jc + 1) * 512],
                           start=(jc == 0), stop=(jc == 31))
                    TE.bump()
                t1 = TE.tok()
                AC.wait(t1)
                for jout in range(NFM):
                    jp = jout % 2
                    AC.wait(prev_st[jp])
                    t2 = AC.do(lambda h, jout=jout, jp=jp: h.activation(
                        out=arenaB[:, B_fq + 2048 + jp * 512:B_fq + 2048 + jp * 512 + 512],
                        in_=psl[jout], func=AF.Copy, scale=aF(F_sc + 32 + jout, 1)), inc=1)
                    SY.phase()
                    SY.wait(t2)
                    prev_st[jp] = SY.dma(lambda h, jout=jout, tq4=tq4, jp=jp: h.dma_start(
                        out=h_ffm[jout * 128:(jout + 1) * 128, tq4 * 512:(tq4 + 1) * 512],
                        in_=arenaB[:, B_fq + 2048 + jp * 512:B_fq + 2048 + jp * 512 + 512]))
                TE.wait(AC.tok())
                ln2_quarter(tq4, prev_st)
            barrier()
            wait_all_sy(SY)

        def script():
            stages = [st_init, stage_A_v,
                      lambda: small_rt(0, 8, T_beta),
                      stage_A_k, stage_A_q,
                      lambda: small_rt(R_rq, 4, T_rq),
                      stage_Qscale,
                      stage_D, stage_B_k, stage_C_pre, stage_C_scan,
                      stage_E, stage_Qfc, stage_Qpj, stage_F]
            for i, st in enumerate(stages):
                if i >= stage_limit:
                    break
                st()
            if stage_limit < len(stages):
                barrier()
                wait_all_sy(SY)

        def reset_all():
            for e in ALLE:
                e.h, e.active, e.count = None, False, 0
            for ee in (GD, SY):
                ee.sem_pool = [(s, 0) for (s, c) in ee.sem_pool]
                ee.pool_idx = 0
            q_state["k"] = 0
            q_red.clear()
            q_tok.clear()

        @blk.sync
        def _(sync):
            reset_all(); SY.h, SY.active = sync, True
            script()

        @blk.gpsimd
        def _(gph):
            reset_all(); GP.h, GP.active = gph, True; GD.h, GD.active = gph, True
            script()

        @blk.tensor
        def _(teh):
            reset_all(); TE.h, TE.active = teh, True
            script()

        @blk.scalar
        def _(ach):
            reset_all(); AC.h, AC.active = ach, True
            script()

        @blk.vector
        def _(dvh):
            reset_all(); DV.h, DV.active = dvh, True
            script()

    return nc, dbg



def _const_onesv():
    out = np.zeros((128, 512), np.float32)
    for hh in range(4):
        out[:, 32 * (0 + hh) + 0 + hh] = 1.0
        out[:, 32 * (4 + hh) + 4 + hh] = 1.0
        out[:, 32 * (8 + hh) + 8 + hh] = 1.0
        out[:, 32 * (12 + hh) + 0 + hh] = 1.0
    return out


def _const_mixmx():
    out = np.zeros((128, 192), np.float32)
    for t in range(4):
        for hh in range(4):
            out[0 + hh, 16 * t + 4 * t + 0] = 0.25 / 256
            out[4 + hh, 16 * t + 4 * t + 1] = 0.25 / 256
    out[:] = 0.0
    for t in range(4):
        c = 16 * t
        j = 4 * t
        for hh in range(4):
            out[0 + hh, c + j + 0] = 0.25 / 256
            out[4 + hh, c + j + 1] = 0.25 / 256
            out[8 + hh, c + j + 2] = 0.25 / 256
            out[96 + hh, c + j + 3] = 0.25
            out[0 + hh, 96 + c + j + 1] = -0.25
    for base, j0, p2 in ((64, 0, 160), (72, 4, 168)):
        for hh in range(4):
            out[0 + hh, base + j0 + 0] = 0.25 / 256
            out[4 + hh, base + j0 + 1] = 0.25 / 256
            out[8 + hh, base + j0 + 2] = 0.25 / 256
            out[96 + hh, base + j0 + 3] = 0.25
            out[0 + hh, p2 + j0 + 1] = -0.25
    return out


def _tapcol(w):
    return np.ascontiguousarray(w.reshape(8, 128, 4).transpose(1, 0, 2).reshape(128, 32))


def _pad_tail(w, row0):
    out = np.zeros((128, w.shape[1]), np.float32)
    out[row0:row0 + w.shape[0]] = w
    return out


def _host_inputs(inputs, core):
    g = lambda k: np.asarray(inputs[k], np.float32)
    x = np.asarray(inputs["L0_attn_hidden_states"])[core]

    def diag_layout(w):
        K = w.shape[1]
        out = np.zeros((128, NFM * K * 128), np.float32)
        for fc in range(NFM):
            for t in range(K):
                b = out[:, (fc * K + t) * 128:(fc * K + t + 1) * 128]
                np.fill_diagonal(b, w[fc * 128:(fc + 1) * 128, t])
        return out

    def tpl_layout(w):
        out = np.zeros((128, 256 * 3 * 32), np.float32)
        r = np.arange(32)[:, None]
        j = np.arange(32)[None, :]
        for gg in range(4):
            for e in range(256):
                c = 256 * gg + e
                for d in range(3):
                    t = 63 - 32 * d + r - j
                    T = np.where((t >= 0) & (t <= 63), w[c][np.clip(t, 0, 63)], 0.0)
                    out[32 * gg:32 * (gg + 1), (e * 3 + d) * 32:(e * 3 + d + 1) * 32] = T
        return out

    def col128(v):
        return np.ascontiguousarray(np.asarray(v, np.float32).reshape(-1, 128).T)

    def pad_cols(w, total, at):
        out = np.zeros((w.shape[0], total), np.float32)
        out[:, at:at + w.shape[1]] = w
        return out

    d = {
        "x": np.ascontiguousarray(x, np.float32),
        "Wq": g("L0_attn_Wq"), "Wk": g("L0_attn_Wk"), "Wv": g("L0_attn_Wv"),
        "Wbp": g("L0_attn_Wb"),
        "Wrp": pad_cols(g("L0_attn_Wres"), 8, 4),
        "dq": diag_layout(g("L0_attn_wq_conv")),
        "dk": diag_layout(g("L0_attn_wk_conv")),
        "dv": diag_layout(g("L0_attn_wv_conv")),
        "ds": diag_layout(g("L0_attn_fir_short").reshape(1024, 5)),
        "tpl": tpl_layout(g("L0_attn_fir_long").reshape(1024, 64)),
        "tapq": _tapcol(g("L0_attn_wq_conv")),
        "tapk": _tapcol(g("L0_attn_wk_conv")),
        "tapv": _tapcol(g("L0_attn_wv_conv")),
        "taps5": np.ascontiguousarray(g("L0_attn_fir_short").reshape(8, 128, 5)
                                      .transpose(1, 0, 2).reshape(128, 40)),
        "Wg1": g("L0_attn_Wg1"),
        "Wg1t": _pad_tail(g("L0_attn_Wg1")[1024:1040], 32),
        "bg1": col128(g("L0_attn_bg1")),
        "Wg2p": g("L0_attn_Wg2"),
        "bg2": g("L0_attn_bg2").reshape(H, 1),
        "logit_temp": g("L0_attn_logit_temp").reshape(1, 1),
        "conv_res_logit": g("L0_attn_conv_res_logit").reshape(H, 1),
        "bres": g("L0_attn_bres").reshape(H, 1),
        "Wog1": g("L0_attn_Wog1"),
        "Wog1t": _pad_tail(g("L0_attn_Wog1")[1024:1032], 64),
        "bog1": col128(g("L0_attn_bog1")),
        "Wog2": g("L0_attn_Wog2"),
        "bog2": g("L0_attn_bog2").reshape(1, 1),
        "norm_w": col128(np.tile(g("L0_attn_norm_w"), 4)),
        "Wo": g("L0_attn_Wo"),
        "w_fc": g("L0_ffn_w_fc"), "w_proj": g("L0_ffn_w_proj"),
        "uT": np.ascontiguousarray(g("L0_ffn_u").T),
        "vT": np.ascontiguousarray(g("L0_ffn_v").T),
        "fbT": np.ascontiguousarray(g("L0_ffn_fast_b").T),
        "gate": g("L0_ffn_gate").reshape(1, 1),
        "onesv_c": _const_onesv(),
        "mixmx_c": _const_mixmx(),
    }
    return {k: np.ascontiguousarray(v, np.float32) for k, v in d.items()}


def kernel(**inputs):
    nc, _ = build_core()
    in_maps = [_host_inputs(inputs, c) for c in range(NCORE)]
    res = run_bass_kernel_spmd(nc, in_maps, core_ids=list(range(NCORE)))
    return np.stack([r["y"] for r in res.results], axis=0).astype(np.float32)


# version 30
# speedup vs baseline: 1.0072x; 1.0072x over previous
import contextlib
import numpy as np
import ml_dtypes
import concourse.bass as bass
import concourse.mybir as mybir
from concourse.bass_utils import run_bass_kernel_spmd

F32 = mybir.dt.float32
BF16 = mybir.dt.bfloat16
AF = mybir.ActivationFunctionType
ALU = mybir.AluOpType
AX = mybir.AxisListType

L, D, H, DK = 2048, 1024, 4, 256
NCH, NFM, NTM = 16, 8, 16
FFH = 4096
NCORE = 8
EL2, ERMS, ELN = 1e-6, 1e-5, 1e-5


class Tok:
    __slots__ = ("sem", "val")
    def __init__(self, sem, val):
        self.sem, self.val = sem, val


class Eng:
    GD_REF = None

    def __init__(self, name):
        self.name, self.sem, self.h, self.active, self.count = name, None, None, False, 0
        self.sem_pool = None
        self.pool_idx = 0
        self.self_sync = False

    def phase(self):
        if self.sem_pool is not None:
            self.pool_idx = (self.pool_idx + 1) % len(self.sem_pool)

    def do(self, fn, inc=0):
        if self.active:
            if self.self_sync and self.count > 0:
                self.h.wait_ge(self.sem, self.count)
            ins = fn(self.h)
            if inc:
                ins.then_inc(self.sem, inc)
        if inc:
            self.count += inc
        return Tok(self.sem, self.count)

    def dma(self, fn):
        if self.sem_pool is not None:
            sem, cnt = self.sem_pool[self.pool_idx]
            cnt += 16
            self.sem_pool[self.pool_idx] = (sem, cnt)
            if self.active:
                ins = fn(self.h)
                ins.then_inc(sem, 16)
            return Tok(sem, cnt)
        return self.do(fn, inc=16)

    def bump(self, n=1):
        if self.active:
            self.h.drain()
            self.h.sem_inc(self.sem, n)
        self.count += n
        return Tok(self.sem, self.count)

    def tok(self):
        return Tok(self.sem, self.count)

    POOLED = ()

    def wait(self, tok):
        if tok is None:
            return
        for eng in Eng.POOLED:
            if eng.sem_pool is not None:
                done = False
                for sem, cnt in eng.sem_pool:
                    if tok.sem is sem:
                        tok = Tok(sem, cnt)
                        done = True
                        break
                if done:
                    break
        if self.active:
            self.h.wait_ge(tok.sem, tok.val)


def build_core(debug_outs=(), stage_limit=99, d_sub=99, f_sub=99):
    nc = bass.Bass("TRN2", target_bir_lowering=False)
    DI = lambda n, s: nc.dram_tensor(n, s, F32, kind="ExternalInput").ap()
    DSC = lambda n, s, dt: nc.dram_tensor(
        n, s, dt, kind=("ExternalOutput" if n in debug_outs else "Internal")).ap()

    x_in = DI("x", [L, D])
    Wq_h, Wk_h, Wv_h = DI("Wq", [D, D]), DI("Wk", [D, D]), DI("Wv", [D, D])
    Wbp_h = DI("Wbp", [D, 4])
    Wrp_h = DI("Wrp", [D, 8])
    dq_h = DI("dq", [128, NFM * 4 * 128])
    dk_h = DI("dk", [128, NFM * 4 * 128])
    dvv_h = DI("dv", [128, NFM * 4 * 128])
    dss_h = DI("ds", [128, NFM * 5 * 128])
    tpl_h = DI("tpl", [128, 256 * 3 * 32])
    tapq_h = DI("tapq", [128, 32])
    tapk_h = DI("tapk", [128, 32])
    tapv_h = DI("tapv", [128, 32])
    taps_h = DI("taps5", [128, 40])
    Wg1_h = DI("Wg1", [1040, D])
    Wg1t_h = DI("Wg1t", [128, D])
    bg1_h = DI("bg1", [128, NFM])
    Wg2p_h = DI("Wg2p", [D, 4])
    bg2_h = DI("bg2", [H, 1])
    lt_h = DI("logit_temp", [1, 1])
    crl_h = DI("conv_res_logit", [H, 1])
    bres_h = DI("bres", [H, 1])
    Wog1_h = DI("Wog1", [1032, 512])
    Wog1t_h = DI("Wog1t", [128, 512])
    bog1_h = DI("bog1", [128, 4])
    Wog2_h = DI("Wog2", [512, 1])
    bog2_h = DI("bog2", [1, 1])
    nw_h = DI("norm_w", [128, NFM])
    Wo_h = DI("Wo", [D, D])
    wfc_h = DI("w_fc", [FFH, D])
    wpj_h = DI("w_proj", [D, FFH])
    uT_h = DI("uT", [16, FFH])
    vT_h = DI("vT", [D, 16])
    fbT_h = DI("fbT", [16, 16])
    gate_h = DI("gate", [1, 1])
    onesv_h = DI("onesv_c", [128, 512])
    mixmx_h = DI("mixmx_c", [128, 192])

    y_out = nc.dram_tensor("y", [L, D], F32, kind="ExternalOutput").ap()
    dbg = {}
    def dbg_ap(name, shape, dt=F32):
        if name in debug_outs:
            dbg[name] = nc.dram_tensor("dbg_" + name, shape, dt, kind="ExternalOutput").ap()
        return dbg.get(name)

    h_xbf = DSC("h_xbf", [L, D], BF16)
    h_qfm = DSC("h_qfm", [D, L], BF16)
    h_kfm = DSC("h_kfm", [D, L], BF16)
    h_kntm = DSC("h_kntm", [L, D], BF16)
    h_kbntm = DSC("h_kbntm", [L, D], BF16)
    h_knfm = DSC("h_knfm", [D, L], BF16)
    h_kbnfm = DSC("h_kbnfm", [D, L], BF16)
    h_vfm = DSC("h_vfm", [D, L], BF16)
    h_vtm = DSC("h_vtm", [L, D], BF16)
    h_vbtm = DSC("h_vbtm", [L, D], BF16)
    h_lsfm = DSC("h_lsfm", [D, L], BF16)
    h_ll = DSC("h_ll", [L, D], BF16)
    h_delta = DSC("h_delta", [L, D], BF16)
    h_E = DSC("h_E", [NCH * H * 128, 128], BF16)
    h_aT = DSC("h_aT", [NCH * H * 128, 128], BF16)
    h_o = DSC("h_o", [L, D], BF16)
    h_on = DSC("h_on", [L, D], BF16)
    h_x2 = DSC("h_x2", [L, D], F32)
    h_x2bf = DSC("h_x2bf", [L, D], BF16)
    h_tern = DSC("h_tern", [FFH, D], BF16)
    h_ternp = DSC("h_ternp", [D, FFH], BF16)
    h_ffm = DSC("h_ffm", [D, L], BF16)

    es = contextlib.ExitStack()
    with es:
        sbuf = lambda n, s, dt: es.enter_context(nc.sbuf_tensor(n, s, dt))
        psum_t = lambda n, s, dt=F32: es.enter_context(nc.psum_tensor(n, s, dt))

        TE, DV, AC, GP, SY, GD = (Eng(n) for n in ["te", "dv", "ac", "gp", "sy", "gd"])
        Eng.GD_REF = GD
        GD.sem_pool = [(es.enter_context(nc.semaphore(f"s_gd{i}")), 0) for i in range(8)]
        SY.sem_pool = [(es.enter_context(nc.semaphore(f"s_sy{i}")), 0) for i in range(8)]
        Eng.POOLED = (GD, SY)
        DV.self_sync = AC.self_sync = GP.self_sync = True
        ALLE = [TE, DV, AC, GP, SY, GD]
        for e in ALLE:
            e.sem = es.enter_context(nc.semaphore("s_" + e.name))

        masks = sbuf("masks", [128, 384], F32)
        ident_bf = sbuf("ident_bf", [128, 128], BF16)
        ident_f = sbuf("ident_f", [128, 128], F32)
        onesv = sbuf("onesv", [128, 512], BF16)
        mixmx = sbuf("mixmx", [128, 192], BF16)
        srows = sbuf("srows", [128, L], F32)
        srows_bf = sbuf("srows_bf", [128, L], BF16)
        stm = sbuf("stm", [128, NTM * 32], F32)
        consts = sbuf("consts", [128, 64], F32)
        S_f32 = sbuf("S_f32", [128, 2048], F32)
        S_bf = sbuf("S_bf", [128, 2048], BF16)
        arenaW = sbuf("arenaW", [128, 36864], BF16)
        arenaB = sbuf("arenaB", [128, 41984], BF16)
        arenaF = sbuf("arenaF", [128, 3144], F32)
        wsm = sbuf("wsm", [128, 4096], BF16)

        pA0 = psum_t("pA0", [128, 512])
        pA1 = psum_t("pA1", [128, 512])
        pB0 = psum_t("pB0", [128, 512])
        pB1 = psum_t("pB1", [128, 512])
        pC0 = psum_t("pC0", [128, 512])
        pC1 = psum_t("pC1", [128, 512])
        p1a = psum_t("p1a", [128, 512])
        p1b = psum_t("p1b", [128, 512])
        PA, PB, PC = (pA0, pA1), (pB0, pB1), (pC0, pC1)
        PD = (p1a, p1b)

        C_bg2, C_bres, C_crl, C_lt, C_rsp, C_b2r, C_bog2, C_gate = 0, 1, 2, 3, 4, 5, 6, 7
        C_stat, C_tmp, C_el2, C_eln, C_i256 = 8, 9, 10, 11, 32
        C_bog1, C_bg1, C_nw = 12, 16, 24

        R_scr = 0
        R_beta, R_coef, R_rq = 0, 4, 0
        R_gpark, R_ogpark = 32, 64
        R_logit, R_og, R_rrms = 0, 0, 4

        T_beta, T_pls, T_rq, T_p, T_og, T_rrms, T_sc, T_tmp = 0, 4, 8, 12, 16, 20, 24, 28

        SV_X, SV_X2, SV_AB, SV_RQ = 0, 4, 8, 12
        def sv(i, hh):
            j = 32 * (i + hh)
            return onesv[:, j:j + 32]

        W_wq, W_wk, W_wv = 0, 8192, 16384
        W_diag = 24576
        W_small = 29696
        W_tpl, W_tpl_n = 30208, 6144
        W_wg1, W_wg2 = 0, 9216
        W_wog1 = 9280
        W_wog2 = 13888
        W_wo = 13920
        W_vT, W_uT, W_fbT = 22112, 22240, 26336
        W_tern, W_pstr = 0, 32768

        B_xfm = 0
        B_pre = 16384
        B_vfm = 20488
        B_lsld = 36904
        B_v32 = 0
        B_lltmp = 20488
        B_llst = 28680
        S_kn, S_kbn, S_q = 0, 4096, 8192
        S_kntm, S_vb = 12288, 14336
        S_E, S_aT, S_dout = 16384, 17408, 18432
        B_g1 = 0
        B_gg = 16384
        B_mix = 32768
        B_otm = 38912
        B_g2 = 39936
        B_onfm = 16384
        B_x2fm = 0
        B_r2 = 16384
        B_fq = 32768

        F_acc, F_ln, F_sc, F_m = 0, 2048, 3072, 3136

        blk = es.enter_context(nc.Block())
        aF = lambda off, ln: arenaF[:, off:off + ln]

        def wait_all_sy(e):
            for (s, c) in SY.sem_pool:
                if c > 0:
                    e.wait(Tok(s, c))

        def barrier():
            toks = [e.bump() for e in (TE, DV, AC, GP)]
            toks += [Tok(s, c) for ee in (GD, SY) for (s, c) in ee.sem_pool if c > 0]
            for e in (TE, DV, AC, GP, SY):
                for t in toks:
                    e.wait(t)
            GD.phase()
            SY.phase()

        def mini_sync(engs=(SY, DV, AC, GP, TE)):
            toks = []
            for e in engs:
                if e.sem_pool is not None:
                    toks += [Tok(s, c) for (s, c) in e.sem_pool if c > 0]
                else:
                    toks.append(e.tok())
            for e in engs:
                for t in toks:
                    e.wait(t)
            if SY in engs:
                SY.phase()

        def mm(out_ap, lhsT, rhs, start, stop, tp=None):
            if isinstance(out_ap, tuple):
                tokk = None
                for i, oa in enumerate(out_ap):
                    tokk = TE.do(lambda h, i=i, oa=oa: h.matmul(
                        oa[:, 0:512], lhsT, rhs[:, i * 512:(i + 1) * 512],
                        start=start, stop=stop, tile_position=tp))
                return tokk
            return TE.do(lambda h: h.matmul(out_ap, lhsT, rhs, start=start, stop=stop,
                                            tile_position=tp))

        def st_init():
            t_gp = GP.do(lambda h: h.memset(masks[:, 0:128], -1.0), inc=1)
            GP.wait(t_gp)
            GP.do(lambda h: h.affine_select(out=masks[:, 0:128], in_=masks[:, 0:128],
                                            compare_op=ALU.is_gt, fill=0.0, base=0,
                                            pattern=[[-1, 128]], channel_multiplier=1))
            t_gp = GP.do(lambda h: h.memset(masks[:, 128:256], -1.0), inc=1)
            GP.wait(t_gp)
            GP.do(lambda h: h.affine_select(out=masks[:, 128:256], in_=masks[:, 128:256],
                                            compare_op=ALU.is_gt, fill=0.0, base=0,
                                            pattern=[[1, 128]], channel_multiplier=-1))
            t_gp = GP.do(lambda h: h.memset(masks[:, 256:384], 1.0), inc=1)
            GP.wait(t_gp)
            GP.do(lambda h: h.affine_select(out=masks[:, 256:384], in_=masks[:, 256:384],
                                            compare_op=ALU.is_ge, fill=0.0, base=0,
                                            pattern=[[1, 128]], channel_multiplier=-1))
            for t_ in (ident_bf, ident_f):
                t_gp = GP.do(lambda h, t=t_: h.memset(t[:], 0.0), inc=1)
                GP.wait(t_gp)
                GP.do(lambda h, t=t_: h.affine_select(out=t[:], in_=t[:],
                                                      compare_op=ALU.not_equal, fill=1.0,
                                                      base=0, pattern=[[-1, 128]],
                                                      channel_multiplier=1))
            GP.do(lambda h: h.memset(stm[:], 0.0))
            GP.do(lambda h: h.memset(consts[:, C_el2:C_el2 + 1], EL2))
            GP.do(lambda h: h.memset(consts[0:4, C_bres:C_bres + 1], 0.0))
            GP.do(lambda h: h.memset(consts[0:4, C_crl:C_crl + 1], 0.0))
            GP.do(lambda h: h.memset(consts[:, C_eln:C_eln + 1], ELN))
            GP.do(lambda h: h.memset(consts[:, C_i256:C_i256 + 1], 1.0 / 256))
            GP.do(lambda h: h.memset(srows[R_gpark:R_gpark + 16, :], 0.0))
            GP.do(lambda h: h.memset(srows[R_ogpark:R_ogpark + 8, :], 0.0))
            GP.do(lambda h: h.memset(srows_bf[:], 0.0))
            GP.do(lambda h: h.memset(S_f32[:], 0.0))
            GP.do(lambda h: h.memset(S_bf[:], 0.0))
            for pp in range(2):
                GP.do(lambda h, pp=pp: h.memset(
                    arenaB[:, B_pre + pp * 2052:B_pre + pp * 2052 + 4], 0.0))
            for fc in range(NFM):
                GP.do(lambda h, fc=fc: h.memset(
                    arenaB[:, B_vfm + fc * 2052:B_vfm + fc * 2052 + 4], 0.0))
            GP.bump()

            SY.dma(lambda h: h.dma_start(out=consts[0:4, C_bg2:C_bg2 + 1], in_=bg2_h))
            SY.dma(lambda h: h.dma_start(out=consts[4:8, C_bres:C_bres + 1], in_=bres_h))
            SY.dma(lambda h: h.dma_start(out=consts[4:8, C_crl:C_crl + 1], in_=crl_h))
            SY.dma(lambda h: h.dma_start(out=consts[0:4, C_lt:C_lt + 1],
                                         in_=lt_h.broadcast_to([4, 1])))
            SY.dma(lambda h: h.dma_start(out=consts[0:1, C_bog2:C_bog2 + 1], in_=bog2_h))
            SY.dma(lambda h: h.dma_start(out=consts[0:16, C_gate:C_gate + 1],
                                         in_=gate_h.broadcast_to([16, 1])))
            SY.dma(lambda h: h.dma_start(out=consts[:, C_bog1:C_bog1 + 4], in_=bog1_h))
            SY.dma(lambda h: h.dma_start(out=consts[:, C_bg1:C_bg1 + 8], in_=bg1_h))
            SY.dma(lambda h: h.dma_start(out=consts[:, C_nw:C_nw + 8], in_=nw_h))

            t_xbf = GD.dma(lambda h: h.dma_start(out=h_xbf[:, :], in_=x_in))
            for Wh, off in ((Wq_h, W_wq), (Wk_h, W_wk), (Wv_h, W_wv)):
                GD.dma(lambda h, Wh=Wh, off=off: h.dma_start(
                    out=arenaW[:, off:off + 8192].rearrange("p (kc j) -> p kc j", kc=8),
                    in_=Wh.rearrange("(kc p) j -> p kc j", p=128)))
            GD.dma(lambda h: h.dma_start(
                out=arenaW[:, W_small:W_small + 32].rearrange("p (kc j) -> p kc j", kc=8),
                in_=Wbp_h.rearrange("(kc p) j -> p kc j", p=128)))
            GD.dma(lambda h: h.dma_start(
                out=arenaW[:, W_small + 32:W_small + 96].rearrange("p (kc j) -> p kc j", kc=8),
                in_=Wrp_h.rearrange("(kc p) j -> p kc j", p=128)))
            WdF = 30720 // 2
            GD.dma(lambda h, WdF=WdF: h.dma_start(
                out=arenaW[:, :].bitcast(F32)[:, WdF:WdF + 32], in_=tapq_h))
            GD.dma(lambda h, WdF=WdF: h.dma_start(
                out=arenaW[:, :].bitcast(F32)[:, WdF + 32:WdF + 64], in_=tapk_h))
            GD.dma(lambda h, WdF=WdF: h.dma_start(
                out=arenaW[:, :].bitcast(F32)[:, WdF + 64:WdF + 96], in_=tapv_h))
            GD.dma(lambda h, WdF=WdF: h.dma_start(
                out=arenaW[:, :].bitcast(F32)[:, WdF + 96:WdF + 136], in_=taps_h))
            GD.dma(lambda h: h.dma_start(out=onesv[:, :], in_=onesv_h))
            GD.dma(lambda h: h.dma_start(out=mixmx[:, :], in_=mixmx_h))
            SY.wait(t_xbf)
            for fc in range(NFM):
                SY.dma(lambda h, fc=fc: h.dma_start_transpose(
                    out=arenaB[:, B_xfm + fc * 2048:B_xfm + (fc + 1) * 2048],
                    in_=h_xbf[:, fc * 128:(fc + 1) * 128]))
            barrier()

        def proj_conv(w_off, tap_off, diag_hbm, Kt, out_writer):
            GD.phase()
            t_diag = GD.dma(lambda h, diag_hbm=diag_hbm: h.dma_start(
                out=arenaW[:, W_diag:W_diag + NFM * Kt * 128],
                in_=diag_hbm[:, 0:NFM * Kt * 128]))
            TE.wait(t_diag)
            tapsF = arenaW[:, :].bitcast(F32)
            evac_t = [None, None]
            conv_dv = {0: None, 1: None}
            conv_te = {0: None, 1: None}
            for fc in range(NFM):
                q_hook()
                pp = fc % 2
                pre = arenaB[:, B_pre + pp * 2052:B_pre + pp * 2052 + 2052]
                for th in range(2):
                    ps = PA if th == 0 else PB
                    TE.wait(evac_t[th])
                    for kc in range(NFM):
                        mm(ps,
                           arenaW[:, w_off + kc * 1024 + fc * 128:w_off + kc * 1024 + (fc + 1) * 128],
                           arenaB[:, B_xfm + kc * 2048 + th * 1024:B_xfm + kc * 2048 + (th + 1) * 1024],
                           start=(kc == 0), stop=(kc == NFM - 1))
                    t_mm = TE.bump()
                    AC.wait(t_mm)
                    AC.do(lambda h, ps=ps, pre=pre, th=th: h.copy(
                        out=pre[:, 4 + th * 1024:4 + th * 1024 + 512], in_=ps[0][:, 0:512]), inc=1)
                    evac_t[th] = AC.do(lambda h, ps=ps, pre=pre, th=th: h.copy(
                        out=pre[:, 4 + th * 1024 + 512:4 + (th + 1) * 1024], in_=ps[1][:, 0:512]), inc=1)
                scr = aF(F_acc + pp * 1024, 1024)
                DV.wait(evac_t[0])
                DV.wait(conv_dv[pp])
                for t in range(Kt):
                    base = 4 - (Kt - 1) + t
                    tap = tapsF[:, tap_off + fc * Kt + t:tap_off + fc * Kt + t + 1]
                    if t == 0:
                        DV.do(lambda h, pre=pre, base=base, tap=tap, scr=scr: h.tensor_scalar(
                            out=scr, in0=pre[:, base:base + 1024],
                            scalar1=tap, scalar2=None, op0=ALU.mult), inc=1)
                    else:
                        DV.do(lambda h, pre=pre, base=base, tap=tap, scr=scr: h.scalar_tensor_tensor(
                            out=scr, in0=pre[:, base:base + 1024], scalar=tap,
                            in1=scr, op0=ALU.mult, op1=ALU.add), inc=1)
                t_cv0 = DV.tok()
                AC.wait(t_cv0)
                conv_dv[pp] = out_writer(fc, 0, [(scr, 0, 1024)])
                pcp = PC if pp == 0 else PD
                TE.wait(evac_t[1])
                TE.wait(conv_te[pp])
                for t in range(Kt):
                    base = 4 + 1024 - (Kt - 1) + t
                    mm(pcp,
                       arenaW[:, W_diag + (fc * Kt + t) * 128:W_diag + (fc * Kt + t + 1) * 128],
                       pre[:, base:base + 1024],
                       start=(t == 0), stop=(t == Kt - 1))
                t_cv1 = TE.bump()
                AC.wait(t_cv1)
                conv_te[pp] = out_writer(fc, 1, [(pcp[0][:, 0:512], 0, 512),
                                                 (pcp[1][:, 0:512], 512, 512)])
            barrier()

        def stats_mix3(slab, slab2, park, n):
            prev = {0: None, 1: None}
            for q in range(4):
                pc = PC[q % 2]
                TE.wait(prev[q % 2])
                mm(pc[park:park + n, 0:512],
                   mixmx[0:128, slab:slab + n],
                   srows_bf[0:128, q * 512:(q + 1) * 512], True, True, tp=(0, park))
                t_park = TE.bump()
                DV.wait(t_park)
                prev[q % 2] = DV.do(lambda h, q=q, pc=pc: h.tensor_add(
                    out=srows[park:park + n, q * 512:(q + 1) * 512],
                    in0=srows[park:park + n, q * 512:(q + 1) * 512],
                    in1=pc[park:park + n, 0:512]), inc=1)
            TE.wait(prev[0])
            TE.wait(prev[1])

        def stats_mix_p2(slab2, park, n):
            prev = {0: None, 1: None}
            for q in range(4):
                pc = PC[q % 2]
                TE.wait(prev[q % 2])
                mm(pc[park:park + n, 0:512],
                   mixmx[0:4, slab2:slab2 + n],
                   srows_bf[0:4, q * 512:(q + 1) * 512], True, True, tp=(0, park))
                t_park = TE.bump()
                DV.wait(t_park)
                prev[q % 2] = DV.do(lambda h, q=q, pc=pc: h.tensor_add(
                    out=srows[park:park + n, q * 512:(q + 1) * 512],
                    in0=srows[park:park + n, q * 512:(q + 1) * 512],
                    in1=pc[park:park + n, 0:512]), inc=1)
            TE.wait(prev[0])
            TE.wait(prev[1])

        def stats_fm2(get_chunk, slab, og_slab, rrms, pres):
            bank_first = {0: True, 1: True, 2: True, 3: True}
            bank_first_d = dict(bank_first)
            t_te_prev = {0: None, 1: None}
            for fc in range(NFM):
                pg = fc % 2 if len(pres) > 1 else 0
                pre0, pre1 = pres[pg]
                src_, t_src = get_chunk(fc)
                DV.wait(t_src)
                DV.wait(t_te_prev[pg])
                t_sq = DV.do(lambda h, s=src_, d=pre0: h.tensor_mul(out=d, in0=s, in1=s), inc=1)
                AC.wait(t_src)
                AC.wait(t_te_prev[pg])
                t_ab = AC.do(lambda h, s=src_, d=pre1: h.activation(out=d, in_=s, func=AF.Abs), inc=1)
                hh = fc // 2
                for sv_i, data, t_d, dup in ((SV_X, src_, t_src, False), (SV_X2, pre0, t_sq, True),
                                             (SV_AB, pre1, t_ab, False)):
                    TE.wait(t_d)
                    for q in range(4):
                        ps = (pA0, pA1, pB0, pB1)[q]
                        st = bank_first[q]
                        bank_first[q] = False
                        last = (fc == NFM - 1 and sv_i == SV_AB)
                        mm(ps[0:32, 0:512],
                           sv(sv_i, hh), data[:, q * 512:(q + 1) * 512],
                           start=st, stop=last)
                        if dup and f_sub >= 2:
                            psd = (pC0, pC1, p1a, p1b)[q]
                            std = bank_first_d[q]
                            bank_first_d[q] = False
                            mm(psd[96:128, 0:512],
                               sv(SV_RQ, hh), data[:, q * 512:(q + 1) * 512],
                               start=std, stop=(fc == NFM - 1), tp=(0, 96))
                t_te_prev[pg] = TE.bump()
            t_all = TE.tok()
            AC.wait(t_all)
            for q in range(4):
                psq = (pA0, pA1, pB0, pB1)[q]
                psd = (pC0, pC1, p1a, p1b)[q]
                AC.do(lambda h, q=q, psq=psq: h.copy(
                    out=srows[0:12, q * 512:(q + 1) * 512], in_=psq[0:12, 0:512]), inc=1)
                if f_sub >= 2:
                    AC.do(lambda h, q=q, psd=psd: h.copy(
                        out=srows[96:100, q * 512:(q + 1) * 512], in_=psd[96:100, 0:512]), inc=1)
            t_e = AC.tok()
            AC.wait(t_e)
            if f_sub >= 2:
                t_l2 = AC.do(lambda h: h.activation(out=srows[96:100, :], in_=srows[96:100, :],
                                                    func=AF.Sqrt), inc=1)
                DV.wait(t_l2)
            else:
                DV.wait(AC.tok())
            DV.do(lambda h: h.tensor_copy(out=srows_bf[0:12, :], in_=srows[0:12, :]), inc=1)
            t_c = DV.do(lambda h: h.tensor_copy(out=srows_bf[96:100, :], in_=srows[96:100, :]), inc=1)
            TE.wait(t_c)
            if f_sub < 3:
                barrier()
                return
            if slab is not None:
                stats_mix3(slab[0], slab[1], R_gpark, 16)
            if og_slab is not None:
                stats_mix3(og_slab[0], og_slab[1], R_ogpark, 8)
            AC.wait(DV.tok())
            t_m2 = AC.do(lambda h: h.activation(out=srows[0:4, :], in_=srows[0:4, :],
                                                func=AF.Square,
                                                scale=consts[0:4, C_i256:C_i256 + 1]), inc=1)
            DV.wait(t_m2)
            t_c2 = DV.do(lambda h: h.tensor_copy(out=srows_bf[0:4, :], in_=srows[0:4, :]), inc=1)
            TE.wait(t_c2)
            if f_sub < 4:
                barrier()
                return
            if slab is not None:
                stats_mix_p2(slab[1], R_gpark, 16)
            if og_slab is not None:
                stats_mix_p2(og_slab[1], R_ogpark, 8)
            if rrms:
                AC.wait(DV.tok())
                t_sr = AC.do(lambda h: h.activation(out=srows[0:8, :], in_=srows[0:8, :],
                                                    func=AF.Copy, bias=0.0,
                                                    scale=consts[0:8, C_i256:C_i256 + 1]), inc=1)
                DV.wait(t_sr)
            barrier()

        def small_rt(row0, nrows, slot0):
            prev = {0: None, 1: None}
            for t in range(NTM):
                pb = p1b if t % 2 == 0 else p1a
                TE.wait(prev[t % 2])
                t_tr = TE.do(lambda h, t=t, pb=pb: h.transpose(
                    pb[:, 0:nrows], srows[row0:row0 + nrows, t * 128:(t + 1) * 128],
                    ident_f[0:nrows, 0:nrows]), inc=1)
                AC.wait(t_tr)
                prev[t % 2] = AC.do(lambda h, t=t, pb=pb: h.copy(
                    out=stm[:, t * 32 + slot0:t * 32 + slot0 + nrows], in_=pb[:, 0:nrows]), inc=1)
            barrier()

        def stage_A_v():
            def writer(fc, th, segs):
                tk = None
                off = B_vfm + fc * 2052 + 4 + th * 1024
                for ap, o, n in segs:
                    tk = AC.do(lambda h, ap=ap, o=o, n=n, off=off: h.activation(
                        out=arenaB[:, off + o:off + o + n], in_=ap, func=AF.Silu), inc=1)
                return tk
            proj_conv(W_wv, 30720 // 2 + 64, dvv_h, 4, writer)
            for q in range(4):
                for kc in range(NFM):
                    mm(p1a[0:8, 0:512],
                       arenaW[:, W_small + 32 + kc * 8:W_small + 32 + (kc + 1) * 8],
                       arenaB[:, B_xfm + kc * 2048 + q * 512:B_xfm + kc * 2048 + (q + 1) * 512],
                       start=(kc == 0), stop=(kc == NFM - 1))
                t3 = TE.bump()
                AC.wait(t3)
                t4 = AC.do(lambda h, q=q: h.activation(
                    out=srows[0:8, q * 512:(q + 1) * 512], in_=p1a[0:8, 0:512],
                    func=AF.Sigmoid, bias=consts[0:8, C_bres:C_bres + 1]), inc=1)
                TE.wait(t4)
            AC.do(lambda h: h.activation(out=consts[0:8, C_stat:C_stat + 1],
                                         in_=consts[0:8, C_crl:C_crl + 1], func=AF.Sigmoid), inc=1)
            t5 = AC.tok()
            DV.wait(t5)
            DV.do(lambda h: h.tensor_scalar(out=srows[0:8, :], in0=srows[0:8, :],
                                            scalar1=consts[0:8, C_stat:C_stat + 1], scalar2=None,
                                            op0=ALU.mult), inc=1)
            t6 = DV.tok()
            TE.wait(t6)
            AC.wait(t6)
            for q in range(4):
                for kc in range(NFM):
                    mm(p1a[0:4, 0:512], arenaW[:, W_small + kc * 4:W_small + (kc + 1) * 4],
                       arenaB[:, B_xfm + kc * 2048 + q * 512:B_xfm + kc * 2048 + (q + 1) * 512],
                       start=(kc == 0), stop=(kc == NFM - 1))
                t1 = TE.bump()
                AC.wait(t1)
                t2 = AC.do(lambda h, q=q: h.activation(
                    out=srows[0:4, q * 512:(q + 1) * 512], in_=p1a[0:4, 0:512],
                    func=AF.Sigmoid), inc=1)
                TE.wait(t2)
            barrier()

        def stage_A_k():
            st_tok = {}
            def writer(fc, th, segs):
                base = (fc % 2) * 2048
                if th == 0:
                    AC.wait(st_tok.get(fc - 2))
                t = None
                for ap, o, n in segs:
                    t = AC.do(lambda h, th=th, base=base, ap=ap, o=o, n=n: h.activation(
                        out=wsm[:, base + th * 1024 + o:base + th * 1024 + o + n],
                        in_=ap, func=AF.Silu), inc=1)
                if th == 1:
                    SY.phase()
                    SY.wait(t)
                    st_tok[fc] = SY.dma(lambda h, fc=fc, base=base: h.dma_start(
                        out=h_kfm[fc * 128:(fc + 1) * 128, :], in_=wsm[:, base:base + 2048]))
                return t
            proj_conv(W_wk, 30720 // 2 + 32, dk_h, 4, writer)

        def stage_A_q():
            st_tok = {}
            def writer(fc, th, segs):
                base = (fc % 2) * 2048
                if th == 0:
                    AC.wait(st_tok.get(fc - 2))
                t = None
                for ap, o, n in segs:
                    t = AC.do(lambda h, th=th, base=base, ap=ap, o=o, n=n: h.activation(
                        out=wsm[:, base + th * 1024 + o:base + th * 1024 + o + n],
                        in_=ap, func=AF.Silu), inc=1)
                if th == 1:
                    SY.phase()
                    SY.wait(t)
                    st_tok[fc] = SY.dma(lambda h, fc=fc, base=base: h.dma_start(
                        out=h_qfm[fc * 128:(fc + 1) * 128, :], in_=wsm[:, base:base + 2048]))
                return t
            proj_conv(W_wq, 30720 // 2, dq_h, 4, writer)
            bank_first = {0: True, 1: True, 2: True, 3: True}
            mm_done = {0: None, 1: None}
            for fc in range(NFM):
                par = fc % 2
                slot = wsm[:, par * 2048:(par + 1) * 2048]
                SY.phase()
                SY.wait(mm_done[par])
                tok = SY.dma(lambda h, fc=fc, slot=slot: h.dma_start(
                    out=slot, in_=h_qfm[fc * 128:(fc + 1) * 128, :]))
                AC.wait(tok)
                t_sq = AC.do(lambda h, slot=slot: h.activation(out=slot, in_=slot,
                                                               func=AF.Square), inc=1)
                TE.wait(t_sq)
                hh = fc // 2
                for q in range(4):
                    ps = (pA0, pA1, pB0, pB1)[q]
                    st = bank_first[q]
                    bank_first[q] = False
                    mm(ps[0:32, 0:512],
                       sv(SV_RQ, hh), slot[:, q * 512:(q + 1) * 512],
                       start=st, stop=(fc == NFM - 1))
                mm_done[par] = TE.bump()
            t_all = TE.bump()
            AC.wait(t_all)
            for q in range(4):
                psq = (pA0, pA1, pB0, pB1)[q]
                t_q2 = AC.do(lambda h, q=q, psq=psq: h.activation(
                    out=srows[R_rq:R_rq + 4, q * 512:(q + 1) * 512], in_=psq[0:4, 0:512],
                    func=AF.Sqrt, bias=consts[0:4, C_el2:C_el2 + 1]), inc=1)
            DV.wait(t_q2)
            DV.do(lambda h: h.reciprocal(out=srows[R_rq:R_rq + 4, :], in_=srows[R_rq:R_rq + 4, :]), inc=1)
            barrier()

        def stage_D():
            pre0 = arenaB[:, B_pre:B_pre + 2048]
            pre1 = arenaB[:, B_pre + 2052:B_pre + 2052 + 2048]
            presD = [(pre0, pre1), (arenaB[:, 0:2048], arenaB[:, 2048:4096])]
            stats_fm2(lambda fc: (arenaB[:, B_vfm + fc * 2052 + 4:B_vfm + fc * 2052 + 4 + 2048], None),
                      (48, 96 + 48), (72, 168), False, presD)
            if d_sub <= 1:
                return
            GD.phase()
            t_diag = GD.dma(lambda h: h.dma_start(
                out=arenaW[:, W_diag:W_diag + NFM * 5 * 128], in_=dss_h[:, :]))
            TE.wait(t_diag)
            tapsF = arenaW[:, :].bitcast(F32)
            TS_OFF = 30720 // 2 + 96
            cv_dv = {0: None, 1: None}
            cv_te = {0: None, 1: None}
            st1 = {0: None, 1: None}
            for fc in range(NFM):
                pp = fc % 2
                scr = aF(F_acc + pp * 1024, 1024)
                for t in range(5):
                    base = B_vfm + fc * 2052 + t
                    tap = tapsF[:, TS_OFF + fc * 5 + t:TS_OFF + fc * 5 + t + 1]
                    if t == 0:
                        DV.do(lambda h, base=base, tap=tap, scr=scr: h.tensor_scalar(
                            out=scr, in0=arenaB[:, base:base + 1024],
                            scalar1=tap, scalar2=None, op0=ALU.mult), inc=1)
                    else:
                        DV.do(lambda h, base=base, tap=tap, scr=scr: h.scalar_tensor_tensor(
                            out=scr, in0=arenaB[:, base:base + 1024], scalar=tap,
                            in1=scr, op0=ALU.mult, op1=ALU.add), inc=1)
                DV.wait(cv_dv[pp])
                DV.do(lambda h, pp=pp, scr=scr: h.tensor_copy(
                    out=wsm[:, 2048 + pp * 1024:2048 + (pp + 1) * 1024], in_=scr), inc=1)
                t0c = DV.tok()
                SY.phase()
                SY.wait(t0c)
                cv_dv[pp] = SY.dma(lambda h, fc=fc, pp=pp: h.dma_start(
                    out=h_lsfm[fc * 128:(fc + 1) * 128, 0:1024],
                    in_=wsm[:, 2048 + pp * 1024:2048 + (pp + 1) * 1024]))
                pcp = PC if pp == 0 else PD
                TE.wait(cv_te[pp])
                for t in range(5):
                    base = B_vfm + fc * 2052 + 1024 + t
                    mm(pcp,
                       arenaW[:, W_diag + (fc * 5 + t) * 128:W_diag + (fc * 5 + t + 1) * 128],
                       arenaB[:, base:base + 1024],
                       start=(t == 0), stop=(t == 4))
                tmm = TE.bump()
                AC.wait(tmm)
                AC.wait(st1[pp])
                AC.do(lambda h, pp=pp: h.copy(
                    out=wsm[:, pp * 1024:pp * 1024 + 512], in_=pC0[:, 0:512]
                    if pp == 0 else p1a[:, 0:512]), inc=1)
                tev = AC.do(lambda h, pp=pp: h.copy(
                    out=wsm[:, pp * 1024 + 512:(pp + 1) * 1024], in_=pC1[:, 0:512]
                    if pp == 0 else p1b[:, 0:512]), inc=1)
                cv_te[pp] = tev
                SY.phase()
                SY.wait(tev)
                st1[pp] = SY.dma(lambda h, fc=fc, pp=pp: h.dma_start(
                    out=h_lsfm[fc * 128:(fc + 1) * 128, 1024:2048],
                    in_=wsm[:, pp * 1024:(pp + 1) * 1024]))
            barrier()
            if d_sub <= 2:
                return
            def get_ls(fc):
                pg = fc % 2
                SY.phase()
                SY.wait(TE.tok())
                tok = SY.dma(lambda h, fc=fc, pg=pg: h.dma_start(
                    out=arenaB[:, B_lsld + pg * 2048:B_lsld + pg * 2048 + 2048],
                    in_=h_lsfm[fc * 128:(fc + 1) * 128, :]))
                return arenaB[:, B_lsld + pg * 2048:B_lsld + pg * 2048 + 2048], tok
            stats_fm2(get_ls, (0, 96), None, False, presD)
            if d_sub <= 3:
                return
            for fc in range(NFM):
                SY.dma(lambda h, fc=fc: h.dma_start(
                    out=h_vfm[fc * 128:(fc + 1) * 128, :],
                    in_=arenaB[:, B_vfm + fc * 2052 + 4:B_vfm + fc * 2052 + 4 + 2048]))
            barrier()
            if d_sub <= 4:
                return
            wait_all_sy(SY)
            st_v = {0: None, 1: None}
            st_vb = {0: None, 1: None}
            vb_dv = {0: None, 1: None}
            for t in range(NTM):
                par = t % 2
                vsl = wsm[:, par * 2048:par * 2048 + 1024]
                vbs = wsm[:, par * 2048 + 1024:par * 2048 + 2048]
                SY.phase()
                SY.wait(st_v[par])
                SY.wait(vb_dv[par])
                tok = SY.dma(lambda h, t=t, vsl=vsl: h.dma_start_transpose(
                    out=vsl, in_=h_vfm[:, t * 128:(t + 1) * 128]))
                SY.phase()
                SY.wait(tok)
                st_v[par] = SY.dma(lambda h, t=t, vsl=vsl: h.dma_start(
                    out=h_vtm[t * 128:(t + 1) * 128, :], in_=vsl))
                DV.wait(tok)
                DV.wait(st_vb[par])
                for hh in range(H):
                    DV.do(lambda h, t=t, hh=hh, vsl=vsl, vbs=vbs: h.tensor_scalar(
                        out=vbs[:, hh * 256:(hh + 1) * 256],
                        in0=vsl[:, hh * 256:(hh + 1) * 256],
                        scalar1=stm[:, t * 32 + T_beta + hh:t * 32 + T_beta + hh + 1],
                        scalar2=None, op0=ALU.mult), inc=1)
                t_vb = DV.tok()
                vb_dv[par] = t_vb
                SY.phase()
                SY.wait(t_vb)
                st_vb[par] = SY.dma(lambda h, t=t, vbs=vbs: h.dma_start(
                    out=h_vbtm[t * 128:(t + 1) * 128, :], in_=vbs))
            barrier()
            if d_sub <= 5:
                return
            t_z = GP.do(lambda h: h.memset(arenaB[:, B_v32:B_v32 + 512], 0.0), inc=1)
            TE.wait(t_z)
            GD.phase()
            t_v32 = None
            for g in range(4):
                t_v32 = GD.dma(lambda h, g=g: h.dma_start(
                    out=arenaB[:, B_v32 + 512:B_v32 + 512 + 64 * 256]
                        .rearrange("p (b e) -> p b e", b=64)[32 * g:32 * (g + 1)],
                    in_=h_vtm.rearrange("(b r) (g e) -> g r b e", r=32, e=256)[g]))
            TE.wait(t_v32)
            if d_sub <= 6:
                return
            v32r = arenaB[:, B_v32:B_v32 + 66 * 256].rearrange("p (b e) -> p b e", e=256)
            lltv = arenaB[:, B_lltmp:B_lltmp + 8192].rearrange("p (b e) -> p b e", e=128)
            llta = arenaB[:, B_lltmp:B_lltmp + 8192].rearrange("p (t a e) -> p a t e", a=4, e=128)
            t_tpl = {}
            GD.phase()
            t_tpl[0] = GD.dma(lambda h: h.dma_start(
                out=arenaW[:, 0:W_tpl_n],
                in_=tpl_h.rearrange("p (e x) -> p e x", e=256)[:, 0:64]))
            for eb in range(4):
                W_tplb = (eb % 2) * W_tpl_n
                if eb + 1 < 4:
                    GD.phase()
                    GP.wait(TE.tok())
                    t_tpl[eb + 1] = GD.dma(lambda h, eb=eb: h.dma_start(
                        out=arenaW[:, ((eb + 1) % 2) * W_tpl_n:((eb + 1) % 2) * W_tpl_n + W_tpl_n],
                        in_=tpl_h.rearrange("p (e x) -> p e x", e=256)[:, (eb + 1) * 64:(eb + 2) * 64]))
                TE.wait(t_tpl[eb])
                for e8 in range(8):
                    for ee in range(8):
                        e_loc = e8 * 8 + ee
                        e_glob = eb * 64 + e_loc
                        for g in range(4):
                            psg = (pC0, pC1, p1a, p1b)[g]
                            for dd in range(3):
                                first = (ee == 0 and dd == 0)
                                mm(psg[32 * g:32 * (g + 1), ee * 64:(ee + 1) * 64],
                                   arenaW[32 * g:32 * (g + 1),
                                          W_tplb + (e_loc * 3 + dd) * 32:W_tplb + (e_loc * 3 + dd + 1) * 32],
                                   v32r[32 * g:32 * (g + 1), (2 - dd):(2 - dd) + 64,
                                        e_glob:e_glob + 1],
                                   start=first, stop=(ee == 7 and dd == 2),
                                   tp=(32 * g, 32 * g))
                    t_g = TE.bump()
                    AC.wait(t_g)
                    DV.wait(t_g)
                    ebase = (eb * 64 + e8 * 8) % 128
                    t_eva = t_evd = None
                    for g in range(4):
                        psg = (pC0, pC1, p1a, p1b)[g]
                        if g % 2 == 0:
                            t_eva = AC.do(lambda h, ebase=ebase, g=g, psg=psg: h.copy(
                                out=lltv[32 * g:32 * (g + 1), :, ebase:ebase + 8].transpose([0, 2, 1]),
                                in_=psg[32 * g:32 * (g + 1), 0:512].rearrange("p (s b) -> p s b", s=8)), inc=1)
                        else:
                            t_evd = DV.do(lambda h, ebase=ebase, g=g, psg=psg: h.tensor_copy(
                                out=lltv[32 * g:32 * (g + 1), :, ebase:ebase + 8].transpose([0, 2, 1]),
                                in_=psg[32 * g:32 * (g + 1), 0:512].rearrange("p (s b) -> p s b", s=8)), inc=1)
                    TE.wait(t_eva)
                    TE.wait(t_evd)
                if eb % 2 == 1:
                    GP.wait(AC.tok())
                    GP.wait(DV.tok())
                    ehalf = eb // 2
                    GD.phase()
                    t_asm = None
                    for g in range(4):
                        for a in range(4):
                            t_asm = GD.dma(lambda h, g=g, a=a, ehalf=ehalf: h.dma_start(
                                out=h_ll.rearrange("(t a r) (g e) -> g a r t e", a=4, r=32, e=256)
                                    [g, a][:, :, ehalf * 128:(ehalf + 1) * 128],
                                in_=llta[32 * g:32 * (g + 1), a, :, :]))
                    AC.wait(t_asm)
                    DV.wait(t_asm)
            barrier()
            if d_sub <= 7:
                return
            def get_ll(fc):
                pg = fc % 2
                SY.phase()
                SY.wait(TE.tok())
                tok = SY.dma(lambda h, fc=fc, pg=pg: h.dma_start_transpose(
                    out=arenaB[:, B_llst + pg * 2048:B_llst + pg * 2048 + 2048],
                    in_=h_ll[:, fc * 128:(fc + 1) * 128]))
                return arenaB[:, B_llst + pg * 2048:B_llst + pg * 2048 + 2048], tok
            presL = [(arenaB[:, B_llst + 4096:B_llst + 6144],
                      arenaB[:, B_llst + 6144:B_llst + 8192]),
                     (arenaB[:, B_llst + 8192:B_llst + 10240],
                      arenaB[:, B_llst + 10240:B_llst + 12288])]
            stats_fm2(get_ll, (16, 96 + 16), None, False, presL)

        def stage_B_k():
            st_kn = {0: None, 1: None}
            st_kbn = {0: None, 1: None}
            red_tok = {0: None, 1: None}
            for t in range(NTM):
                par = t % 2
                kraw = wsm[:, par * 2048:par * 2048 + 1024]
                kn_s = wsm[:, par * 2048 + 1024:par * 2048 + 2048]
                mbs = F_m + 0 if par == 0 else F_sc + 48
                SY.phase()
                SY.wait(st_kbn[par])
                tok = SY.dma(lambda h, t=t, kraw=kraw: h.dma_start_transpose(
                    out=kraw, in_=h_kfm[:, t * 128:(t + 1) * 128]))
                AC.wait(tok)
                AC.wait(red_tok[par])
                t_sq = AC.do(lambda h, kraw=kraw, par=par: h.activation(
                    out=aF(F_acc + par * 1024, 1024), in_=kraw, func=AF.Square), inc=1)
                DV.wait(t_sq)
                t_rd = DV.do(lambda h, par=par, mbs=mbs: h.tensor_reduce(
                    out=aF(mbs, 4),
                    in_=aF(F_acc + par * 1024, 1024).rearrange("p (hh e) -> p hh e", hh=4),
                    axis=AX.X, op=ALU.add), inc=1)
                red_tok[par] = t_rd
                AC.wait(t_rd)
                t_rk = AC.do(lambda h, mbs=mbs: h.activation(out=aF(mbs + 4, 4), in_=aF(mbs, 4),
                                                             func=AF.Sqrt,
                                                             bias=consts[:, C_el2:C_el2 + 1]), inc=1)
                DV.wait(t_rk)
                DV.do(lambda h, mbs=mbs: h.reciprocal(out=aF(mbs + 4, 4), in_=aF(mbs + 4, 4)), inc=1)
                DV.wait(st_kn[par])
                for hh in range(H):
                    DV.do(lambda h, hh=hh, kraw=kraw, kn_s=kn_s, mbs=mbs: h.tensor_scalar(
                        out=kn_s[:, hh * 256:(hh + 1) * 256],
                        in0=kraw[:, hh * 256:(hh + 1) * 256],
                        scalar1=aF(mbs + 4 + hh, 1), scalar2=None, op0=ALU.mult), inc=1)
                t_kn = DV.tok()
                SY.phase()
                SY.wait(t_kn)
                st_kn[par] = SY.dma(lambda h, t=t, kn_s=kn_s: h.dma_start(
                    out=h_kntm[t * 128:(t + 1) * 128, :], in_=kn_s))
                for hh in range(H):
                    DV.do(lambda h, t=t, hh=hh, kraw=kraw, kn_s=kn_s: h.tensor_scalar(
                        out=kraw[:, hh * 256:(hh + 1) * 256],
                        in0=kn_s[:, hh * 256:(hh + 1) * 256],
                        scalar1=stm[:, t * 32 + T_beta + hh:t * 32 + T_beta + hh + 1],
                        scalar2=None, op0=ALU.mult), inc=1)
                t_kbn = DV.tok()
                SY.phase()
                SY.wait(t_kbn)
                st_kbn[par] = SY.dma(lambda h, t=t, kraw=kraw: h.dma_start(
                    out=h_kbntm[t * 128:(t + 1) * 128, :], in_=kraw))
            barrier()
            st = {0: None, 1: None}
            for (tsrc, tdst) in ((h_kntm, h_knfm), (h_kbntm, h_kbnfm)):
                for fc in range(NFM):
                    par = fc % 2
                    SY.phase()
                    SY.wait(st[par])
                    t1 = SY.dma(lambda h, fc=fc, par=par, tsrc=tsrc: h.dma_start_transpose(
                        out=wsm[:, par * 2048:(par + 1) * 2048],
                        in_=tsrc[:, fc * 128:(fc + 1) * 128]))
                    SY.phase()
                    SY.wait(t1)
                    st[par] = SY.dma(lambda h, fc=fc, par=par, tdst=tdst: h.dma_start(
                        out=tdst[fc * 128:(fc + 1) * 128, :],
                        in_=wsm[:, par * 2048:(par + 1) * 2048]))
            barrier()

        def stage_C_pre():
            ev_t = {hh: None for hh in range(H)}
            est_tok = {}
            for c in range(NCH):
                pp = c % 2
                SY.phase()
                SY.wait(TE.tok())
                t_kn = SY.dma(lambda h, c=c, pp=pp: h.dma_start(
                    out=arenaB[:, S_kn + pp * 1024:S_kn + pp * 1024 + 1024]
                        .rearrange("p (kc t) -> p kc t", kc=8),
                    in_=h_knfm.rearrange("(kc p) t -> p kc t", p=128)[:, :, c * 128:(c + 1) * 128]))
                t_kbn = SY.dma(lambda h, c=c, pp=pp: h.dma_start(
                    out=arenaB[:, S_kbn + pp * 1024:S_kbn + pp * 1024 + 1024]
                        .rearrange("p (kc t) -> p kc t", kc=8),
                    in_=h_kbnfm.rearrange("(kc p) t -> p kc t", p=128)[:, :, c * 128:(c + 1) * 128]))
                t_q = SY.dma(lambda h, c=c, pp=pp: h.dma_start(
                    out=arenaB[:, S_q + pp * 1024:S_q + pp * 1024 + 1024]
                        .rearrange("p (kc t) -> p kc t", kc=8),
                    in_=h_qfm.rearrange("(kc p) t -> p kc t", p=128)[:, :, c * 128:(c + 1) * 128]))
                knb = lambda kc: arenaB[:, S_kn + pp * 1024 + kc * 128:S_kn + pp * 1024 + (kc + 1) * 128]
                kbnb = lambda kc: arenaB[:, S_kbn + pp * 1024 + kc * 128:S_kbn + pp * 1024 + (kc + 1) * 128]
                qb = lambda kc: arenaB[:, S_q + pp * 1024 + kc * 128:S_q + pp * 1024 + (kc + 1) * 128]
                TE.wait(t_kn); TE.wait(t_kbn); TE.wait(t_q)
                aT_dma = {}
                for hh in range(H):
                    ps = (pA0, pA1, pB0, pB1)[hh]
                    TE.wait(ev_t[hh])
                    mm(ps[:, 0:128], kbnb(2 * hh), knb(2 * hh), True, False)
                    mm(ps[:, 0:128], kbnb(2 * hh + 1), knb(2 * hh + 1), False, False)
                    mm(ps[:, 128:256], knb(2 * hh), kbnb(2 * hh), False, False)
                    mm(ps[:, 128:256], knb(2 * hh + 1), kbnb(2 * hh + 1), False, False)
                    mm(ps[:, 256:384], knb(2 * hh), qb(2 * hh), False, False)
                    mm(ps[:, 256:384], knb(2 * hh + 1), qb(2 * hh + 1), False, True)
                    t_a = TE.bump()
                    DV.wait(t_a)
                    wb = hh * 1024
                    t_m = DV.do(lambda h, ps=ps, wb=wb: h.tensor_tensor(
                        out=wsm[:, wb:wb + 384], in0=ps[:, 0:384],
                        in1=masks[:, 0:384], op=ALU.mult), inc=1)
                    SY.wait(t_m)
                    inst = c * H + hh
                    aT_dma[hh] = SY.dma(lambda h, inst=inst, wb=wb: h.dma_start(
                        out=h_aT[inst * 128:(inst + 1) * 128, :], in_=wsm[:, wb + 256:wb + 384]))
                    ev_t[hh] = DV.do(lambda h, wb=wb: h.tensor_add(
                        out=wsm[:, wb + 512:wb + 640], in0=ident_bf[:],
                        in1=wsm[:, wb + 128:wb + 256]), inc=1)
                mp = {hh: hh * 1024 + 0 for hh in range(H)}
                qp = {hh: hh * 1024 + 128 for hh in range(H)}
                rr = {hh: hh * 1024 + 512 for hh in range(H)}
                alt = {hh: hh * 1024 + 640 for hh in range(H)}
                for s in range(1, 7):
                    for hh in range(H):
                        ps = (pA0, pA1, pB0, pB1)[hh]
                        ENG = AC if hh % 2 == 0 else DV
                        TE.wait(ev_t[hh])
                        if s == 2:
                            TE.wait(aT_dma[hh])
                        mm(ps[:, 0:128], wsm[:, qp[hh]:qp[hh] + 128],
                           wsm[:, mp[hh]:mp[hh] + 128], True, False)
                        mm(ps[:, 128:256], wsm[:, mp[hh]:mp[hh] + 128],
                           wsm[:, qp[hh]:qp[hh] + 128], False, (s == 1))
                        if s >= 2:
                            mm(ps[:, 256:384], ident_bf[:], wsm[:, rr[hh]:rr[hh] + 128], False, False)
                            mm(ps[:, 256:384], wsm[:, mp[hh]:mp[hh] + 128],
                               wsm[:, rr[hh]:rr[hh] + 128], False, True)
                        t_lvl = TE.bump()
                        ENG.wait(t_lvl)
                        if s == 3:
                            ENG.wait(est_tok.get(hh))
                        nm = alt[hh]
                        if s >= 2:
                            t_ev = ENG.do(lambda h, ps=ps, nm=nm: (
                                h.copy(out=wsm[:, nm:nm + 384], in_=ps[:, 0:384])
                                if h is not DV.h else
                                h.tensor_copy(out=wsm[:, nm:nm + 384], in_=ps[:, 0:384])), inc=1)
                            mp[hh], qp[hh], rr[hh] = nm, nm + 128, nm + 256
                            alt[hh] = hh * 1024 + (640 if nm == hh * 1024 else 0)
                        else:
                            t_ev = ENG.do(lambda h, ps=ps, nm=nm: (
                                h.copy(out=wsm[:, nm:nm + 256], in_=ps[:, 0:256])
                                if h is not DV.h else
                                h.tensor_copy(out=wsm[:, nm:nm + 256], in_=ps[:, 0:256])), inc=1)
                            mp[hh], qp[hh] = nm, nm + 128
                            alt[hh] = hh * 1024 + 0
                        ev_t[hh] = t_ev
                for hh in range(H):
                    ps = (pA0, pA1, pB0, pB1)[hh]
                    ENG = AC if hh % 2 == 0 else DV
                    TE.wait(ev_t[hh])
                    mm(ps[:, 0:128], ident_bf[:], wsm[:, rr[hh]:rr[hh] + 128], True, False)
                    mm(ps[:, 0:128], wsm[:, mp[hh]:mp[hh] + 128],
                       wsm[:, rr[hh]:rr[hh] + 128], False, True)
                    t_e = TE.bump()
                    ENG.wait(t_e)
                    dst = hh * 1024 + 896
                    t_f = ENG.do(lambda h, ps=ps, dst=dst: (
                        h.copy(out=wsm[:, dst:dst + 128], in_=ps[:, 0:128])
                        if h is not DV.h else
                        h.tensor_copy(out=wsm[:, dst:dst + 128], in_=ps[:, 0:128])), inc=1)
                    ev_t[hh] = t_f
                    SY.phase()
                    SY.wait(t_f)
                    inst = c * H + hh
                    est_tok[hh] = SY.dma(lambda h, inst=inst, dst=dst: h.dma_start(
                        out=h_E[inst * 128:(inst + 1) * 128, :], in_=wsm[:, dst:dst + 128]))
            barrier()

        def stage_C_scan():
            dout_dma = {}
            for c in range(NCH):
                pp = c % 2
                SY.phase()
                t_kbn = SY.dma(lambda h, c=c, pp=pp: h.dma_start(
                    out=arenaB[:, S_kbn + pp * 1024:S_kbn + pp * 1024 + 1024]
                        .rearrange("p (kc t) -> p kc t", kc=8),
                    in_=h_kbnfm.rearrange("(kc p) t -> p kc t", p=128)[:, :, c * 128:(c + 1) * 128]))
                t_q = SY.dma(lambda h, c=c, pp=pp: h.dma_start(
                    out=arenaB[:, S_q + pp * 1024:S_q + pp * 1024 + 1024]
                        .rearrange("p (kc t) -> p kc t", kc=8),
                    in_=h_qfm.rearrange("(kc p) t -> p kc t", p=128)[:, :, c * 128:(c + 1) * 128]))
                t_kntm = SY.dma(lambda h, c=c, pp=pp: h.dma_start(
                    out=arenaB[:, S_kntm + pp * 1024:S_kntm + pp * 1024 + 1024],
                    in_=h_kntm[c * 128:(c + 1) * 128, :]))
                t_vb = SY.dma(lambda h, c=c, pp=pp: h.dma_start(
                    out=arenaB[:, S_vb + pp * 1024:S_vb + pp * 1024 + 1024],
                    in_=h_vbtm[c * 128:(c + 1) * 128, :]))
                t_E = SY.dma(lambda h, c=c, pp=pp: h.dma_start(
                    out=arenaB[:, S_E + pp * 512:S_E + pp * 512 + 512]
                        .rearrange("p (hh e) -> p hh e", hh=4),
                    in_=h_E[c * H * 128:(c + 1) * H * 128, :].rearrange("(hh p) e -> p hh e", p=128)))
                t_aT = SY.dma(lambda h, c=c, pp=pp: h.dma_start(
                    out=arenaB[:, S_aT + pp * 512:S_aT + pp * 512 + 512]
                        .rearrange("p (hh e) -> p hh e", hh=4),
                    in_=h_aT[c * H * 128:(c + 1) * H * 128, :].rearrange("(hh p) e -> p hh e", p=128)))
                kbnb = lambda kc: arenaB[:, S_kbn + pp * 1024 + kc * 128:S_kbn + pp * 1024 + (kc + 1) * 128]
                qb = lambda kc: arenaB[:, S_q + pp * 1024 + kc * 128:S_q + pp * 1024 + (kc + 1) * 128]
                kntm = arenaB[:, S_kntm + pp * 1024:S_kntm + pp * 1024 + 1024]
                vbt = arenaB[:, S_vb + pp * 1024:S_vb + pp * 1024 + 1024]
                Eb = lambda hh: arenaB[:, S_E + pp * 512 + hh * 128:S_E + pp * 512 + (hh + 1) * 128]
                aTb = lambda hh: arenaB[:, S_aT + pp * 512 + hh * 128:S_aT + pp * 512 + (hh + 1) * 128]
                dout = arenaB[:, S_dout + pp * 1024:S_dout + pp * 1024 + 1024]
                for t_ in (t_kbn, t_q, t_kntm, t_E, t_aT):
                    TE.wait(t_)
                DV.wait(t_vb)
                for hh in range(H):
                    ps = pA0 if hh < 2 else pB0
                    off = (hh % 2) * 256
                    mm(ps[:, off:off + 256], kbnb(2 * hh), S_bf[:, hh * 256:hh * 256 + 256],
                       start=(off == 0), stop=False)
                    mm(ps[:, off:off + 256], kbnb(2 * hh + 1),
                       S_bf[:, 1024 + hh * 256:1024 + hh * 256 + 256], False, (off != 0))
                t_g = TE.bump()
                DV.wait(t_g)
                for hh in range(H):
                    ps = pA0 if hh < 2 else pB0
                    off = (hh % 2) * 256
                    DV.do(lambda h, ps=ps, off=off, hh=hh: h.scalar_tensor_tensor(
                        out=wsm[:, hh * 256:(hh + 1) * 256], in0=ps[:, off:off + 256],
                        scalar=-1.0, in1=vbt[:, hh * 256:(hh + 1) * 256],
                        op0=ALU.mult, op1=ALU.add), inc=1)
                t_gev = DV.tok()
                TE.wait(t_gev)
                for hh in range(H):
                    pc = PC[hh // 2]
                    mm(pc[:, (hh % 2) * 256:(hh % 2) * 256 + 256], Eb(hh),
                       wsm[:, hh * 256:(hh + 1) * 256],
                       start=(hh % 2 == 0), stop=(hh % 2 == 1))
                t_ui = TE.bump()
                AC.wait(t_ui)
                AC.do(lambda h: h.copy(out=wsm[:, 1024:1536], in_=pC0[:, 0:512]), inc=1)
                t_uev = AC.do(lambda h: h.copy(out=wsm[:, 1536:2048], in_=pC1[:, 0:512]), inc=1)
                TE.wait(t_uev)
                for hh in range(H):
                    ps = pA0 if hh < 2 else pB0
                    off = (hh % 2) * 256
                    mm(ps[:, off:off + 256], qb(2 * hh), S_bf[:, hh * 256:hh * 256 + 256],
                       start=(off == 0), stop=False)
                    mm(ps[:, off:off + 256], qb(2 * hh + 1),
                       S_bf[:, 1024 + hh * 256:1024 + hh * 256 + 256], False, False)
                    mm(ps[:, off:off + 256], aTb(hh), wsm[:, 1024 + hh * 256:1024 + (hh + 1) * 256],
                       False, (off != 0))
                t_oi = TE.bump()
                AC.wait(t_oi)
                AC.wait(dout_dma.get(c - 2))
                for hh in range(H):
                    ps = pA0 if hh < 2 else pB0
                    off = (hh % 2) * 256
                    AC.do(lambda h, ps=ps, off=off, hh=hh, c=c: h.activation(
                        out=dout[:, hh * 256:(hh + 1) * 256], in_=ps[:, off:off + 256],
                        func=AF.Copy, scale=stm[:, c * 32 + T_rq + hh:c * 32 + T_rq + hh + 1]), inc=1)
                t_oev = AC.tok()
                SY.wait(t_oev)
                dout_dma[c] = SY.dma(lambda h, c=c, dout=dout: h.dma_start(
                    out=h_delta[c * 128:(c + 1) * 128, :], in_=dout[:]))
                prev_sup = {0: None, 1: None}
                for hh in range(H):
                    hp = hh % 2
                    pbank = p1a if hp == 0 else p1b
                    TE.wait(prev_sup[hp])
                    mm(pbank[:, 0:256], kntm[:, hh * 256:hh * 256 + 128],
                       wsm[:, 1024 + hh * 256:1024 + (hh + 1) * 256], True, False)
                    mm(pbank[:, 256:512], kntm[:, hh * 256 + 128:hh * 256 + 256],
                       wsm[:, 1024 + hh * 256:1024 + (hh + 1) * 256], False, True)
                    t_ds = TE.bump()
                    DV.wait(t_ds)
                    for half in range(2):
                        DV.do(lambda h, hh=hh, half=half, pbank=pbank: h.tensor_add(
                            out=S_f32[:, half * 1024 + hh * 256:half * 1024 + (hh + 1) * 256],
                            in0=S_f32[:, half * 1024 + hh * 256:half * 1024 + (hh + 1) * 256],
                            in1=pbank[:, half * 256:(half + 1) * 256]), inc=1)
                    t_sup = DV.tok()
                    prev_sup[hp] = t_sup
                    AC.wait(t_sup)
                    for half in range(2):
                        AC.do(lambda h, hh=hh, half=half: h.copy(
                            out=S_bf[:, half * 1024 + hh * 256:half * 1024 + (hh + 1) * 256],
                            in_=S_f32[:, half * 1024 + hh * 256:half * 1024 + (hh + 1) * 256]), inc=1)
                TE.wait(AC.tok())
            barrier()

        def stage_E():
            GD.phase()
            GD.dma(lambda h: h.dma_start(
                out=arenaW[:, W_wg1:W_wg1 + 8192].rearrange("p (kc j) -> p kc j", kc=8),
                in_=Wg1_h[0:1024].rearrange("(kc p) j -> p kc j", p=128)))
            GD.dma(lambda h: h.dma_start(
                out=arenaW[:, W_wg1 + 8192:W_wg1 + 9216], in_=Wg1t_h))
            GD.dma(lambda h: h.dma_start(
                out=arenaW[:, W_wg2:W_wg2 + 32].rearrange("p (kc j) -> p kc j", kc=8),
                in_=Wg2p_h.rearrange("(kc p) j -> p kc j", p=128)))
            GD.dma(lambda h: h.dma_start(
                out=arenaW[:, W_wog1:W_wog1 + 4096].rearrange("p (kc j) -> p kc j", kc=8),
                in_=Wog1_h[0:1024].rearrange("(kc p) j -> p kc j", p=128)))
            GD.dma(lambda h: h.dma_start(
                out=arenaW[:, W_wog1 + 4096:W_wog1 + 4608], in_=Wog1t_h))
            GD.dma(lambda h: h.dma_start(
                out=arenaW[:, W_wog2:W_wog2 + 4].rearrange("p (kc j) -> p kc j", kc=4),
                in_=Wog2_h.rearrange("(kc p) j -> p kc j", p=128),
                allow_slow_non_contiguous=True))
            t_wo = GD.dma(lambda h: h.dma_start(
                out=arenaW[:, W_wo:W_wo + 8192].rearrange("p (kc j) -> p kc j", kc=8),
                in_=Wo_h.rearrange("(kc p) j -> p kc j", p=128)))
            GD.dma(lambda h: h.dma_start(
                out=arenaW[:, W_vT:W_vT + 128].rearrange("p (kc j) -> p kc j", kc=8),
                in_=vT_h.rearrange("(kc p) j -> p kc j", p=128)))
            GD.dma(lambda h: h.dma_start(out=arenaW[0:16, W_uT:W_uT + 4096], in_=uT_h))
            GD.dma(lambda h: h.dma_start(out=arenaW[0:16, W_fbT:W_fbT + 16], in_=fbT_h))
            for fc in range(NFM):
                SY.dma(lambda h, fc=fc: h.dma_start_transpose(
                    out=arenaB[:, B_g1 + fc * 2048:B_g1 + (fc + 1) * 2048],
                    in_=h_xbf[:, fc * 128:(fc + 1) * 128]))
            AC.wait(t_wo)
            for kc in range(NFM):
                AC.do(lambda h, kc=kc: h.activation(
                    out=arenaW[:, W_wo + kc * 1024:W_wo + (kc + 1) * 1024],
                    in_=arenaW[:, W_wo + kc * 1024:W_wo + (kc + 1) * 1024],
                    func=AF.Copy, scale=consts[:, C_nw + kc:C_nw + kc + 1]), inc=1)
            pre0 = arenaB[:, B_mix:B_mix + 2048]
            pre1 = arenaB[:, B_mix + 2048:B_mix + 4096]
            presE = [(arenaB[:, B_gg:B_gg + 2048], arenaB[:, B_gg + 2048:B_gg + 4096]),
                     (arenaB[:, B_gg + 4096:B_gg + 6144], arenaB[:, B_gg + 6144:B_gg + 8192])]
            def get_d(fc):
                pg = fc % 2
                SY.phase()
                SY.wait(TE.tok())
                tok = SY.dma(lambda h, fc=fc, pg=pg: h.dma_start_transpose(
                    out=arenaB[:, B_mix + pg * 2048:B_mix + pg * 2048 + 2048],
                    in_=h_delta[:, fc * 128:(fc + 1) * 128]))
                return arenaB[:, B_mix + pg * 2048:B_mix + pg * 2048 + 2048], tok
            stats_fm2(get_d, (32, 96 + 32), None, False, presE)
            t_cp = DV.do(lambda h: h.tensor_copy(out=srows_bf[R_gpark:R_gpark + 16, :],
                                                 in_=srows[R_gpark:R_gpark + 16, :]), inc=1)
            TE.wait(t_cp)
            prev_ev = {0: None, 1: None, 2: None, 3: None}
            for fc in range(NFM):
                for th in range(2):
                    par = (fc * 2 + th) % 4
                    ps = (PA, PB, PC, PD)[par]
                    TE.wait(prev_ev[par])
                    for kc in range(NFM):
                        mm(ps,
                           arenaW[:, W_wg1 + kc * 1024 + fc * 128:W_wg1 + kc * 1024 + (fc + 1) * 128],
                           arenaB[:, B_g1 + kc * 2048 + th * 1024:B_g1 + kc * 2048 + (th + 1) * 1024],
                           start=(kc == 0), stop=False)
                    mm(ps,
                       arenaW[:, W_wg1 + 8192 + fc * 128:W_wg1 + 8192 + (fc + 1) * 128],
                       srows_bf[:, th * 1024:(th + 1) * 1024],
                       False, True)
                    t1 = TE.bump()
                    AC.wait(t1)
                    for hf in range(2):
                        t2 = AC.do(lambda h, fc=fc, th=th, ps=ps, hf=hf: h.activation(
                            out=arenaB[:, B_gg + fc * 2048 + th * 1024 + hf * 512:B_gg + fc * 2048 + th * 1024 + (hf + 1) * 512],
                            in_=ps[hf][:, 0:512], func=AF.Gelu,
                            bias=consts[:, C_bg1 + fc:C_bg1 + fc + 1]), inc=1)
                    prev_ev[par] = t2
            for _pv in range(4):
                TE.wait(prev_ev[_pv])
            AC.do(lambda h: h.activation(out=consts[0:4, C_tmp:C_tmp + 1],
                                         in_=consts[0:4, C_lt:C_lt + 1], func=AF.Exp), inc=1)
            t_e1 = AC.tok()
            DV.wait(t_e1)
            t_e2 = DV.do(lambda h: h.tensor_scalar(out=consts[0:4, C_tmp:C_tmp + 1],
                                                   in0=consts[0:4, C_tmp:C_tmp + 1],
                                                   scalar1=1.0, scalar2=None, op0=ALU.add), inc=1)
            AC.wait(t_e2)
            t_sp = AC.do(lambda h: h.activation(out=consts[0:4, C_tmp:C_tmp + 1],
                                                in_=consts[0:4, C_tmp:C_tmp + 1], func=AF.Ln), inc=1)
            DV.wait(t_sp)
            DV.do(lambda h: h.reciprocal(out=consts[0:4, C_rsp:C_rsp + 1],
                                         in_=consts[0:4, C_tmp:C_tmp + 1]), inc=1)
            t_rsp = DV.do(lambda h: h.tensor_mul(out=consts[0:4, C_b2r:C_b2r + 1],
                                                 in0=consts[0:4, C_bg2:C_bg2 + 1],
                                                 in1=consts[0:4, C_rsp:C_rsp + 1]), inc=1)
            AC.wait(t_rsp)
            for q in range(4):
                for kc in range(NFM):
                    mm(p1a[0:4, 0:512], arenaW[:, W_wg2 + kc * 4:W_wg2 + (kc + 1) * 4],
                       arenaB[:, B_gg + kc * 2048 + q * 512:B_gg + kc * 2048 + (q + 1) * 512],
                       start=(kc == 0), stop=(kc == NFM - 1))
                t1 = TE.bump()
                DV.wait(t1)
                t2 = DV.do(lambda h, q=q: h.scalar_tensor_tensor(
                    out=srows[R_logit:R_logit + 4, q * 512:(q + 1) * 512], in0=p1a[0:4, 0:512],
                    scalar=consts[0:4, C_rsp:C_rsp + 1],
                    in1=consts[0:4, C_b2r:C_b2r + 1].broadcast_to([4, 512]),
                    op0=ALU.mult, op1=ALU.add), inc=1)
                TE.wait(t2)
            barrier()
            small_rt(R_logit, 4, T_p)
            sp3 = stm[:, 0:512].rearrange("p (t c) -> p t c", c=32)
            P_ = sp3[:, :, T_p:T_p + 4]
            PLS_ = sp3[:, :, T_pls:T_pls + 4]
            mcol = lambda off: aF(F_acc + off, 16).rearrange(
                "p (t o) -> p t o", o=1).broadcast_to([128, 16, 4])
            DV.do(lambda h: h.tensor_reduce(out=aF(F_acc, 16), in_=P_, axis=AX.X,
                                            op=ALU.max), inc=1)
            DV.do(lambda h: h.tensor_tensor(out=P_, in0=P_, in1=mcol(0), op=ALU.subtract), inc=1)
            t1 = DV.tok()
            AC.wait(t1)
            t2 = AC.do(lambda h: h.activation(out=P_, in_=P_, func=AF.Exp), inc=1)
            DV.wait(t2)
            DV.do(lambda h: h.tensor_reduce(out=aF(F_acc, 16), in_=P_, axis=AX.X,
                                            op=ALU.add), inc=1)
            DV.do(lambda h: h.reciprocal(out=aF(F_acc, 16), in_=aF(F_acc, 16)), inc=1)
            DV.do(lambda h: h.tensor_tensor(out=P_, in0=P_, in1=mcol(0), op=ALU.mult), inc=1)
            DV.do(lambda h: h.tensor_scalar(out=P_, in0=P_, scalar1=0.02, scalar2=None,
                                            op0=ALU.max), inc=1)
            DV.do(lambda h: h.tensor_reduce(out=aF(F_acc, 16), in_=P_, axis=AX.X,
                                            op=ALU.add), inc=1)
            DV.do(lambda h: h.reciprocal(out=aF(F_acc, 16), in_=aF(F_acc, 16)), inc=1)
            DV.do(lambda h: h.tensor_tensor(out=P_, in0=P_, in1=mcol(0), op=ALU.mult), inc=1)
            DV.do(lambda h: h.tensor_tensor(
                out=PLS_, in0=PLS_,
                in1=sp3[:, :, T_p:T_p + 1].broadcast_to([128, 16, 4]), op=ALU.add), inc=1)
            barrier()
            mix_dv = {}
            st_tok = {}
            for t in range(NTM):
                pg = t % 2
                SY.phase()
                SY.wait(mix_dv.get(t - 2))
                SY.dma(lambda h, t=t, pg=pg: h.dma_start_transpose(
                    out=arenaB[:, B_mix + pg * 1024:B_mix + pg * 1024 + 1024],
                    in_=h_lsfm[:, t * 128:(t + 1) * 128]))
                SY.dma(lambda h, t=t, pg=pg: h.dma_start(
                    out=arenaB[:, B_mix + 2048 + pg * 1024:B_mix + 2048 + pg * 1024 + 1024],
                    in_=h_ll[t * 128:(t + 1) * 128, :]))
                SY.dma(lambda h, t=t, pg=pg: h.dma_start(
                    out=arenaB[:, B_mix + 4096 + pg * 1024:B_mix + 4096 + pg * 1024 + 1024],
                    in_=h_delta[t * 128:(t + 1) * 128, :]))
                t_vv = SY.dma(lambda h, t=t, pg=pg: h.dma_start(
                    out=arenaB[:, B_otm + 1024 + pg * 1024:B_otm + 1024 + pg * 1024 + 1024],
                    in_=h_vtm[t * 128:(t + 1) * 128, :]))
                lsb = arenaB[:, B_mix + pg * 1024:B_mix + pg * 1024 + 1024]
                llb = arenaB[:, B_mix + 2048 + pg * 1024:B_mix + 2048 + pg * 1024 + 1024]
                ddb = arenaB[:, B_mix + 4096 + pg * 1024:B_mix + 4096 + pg * 1024 + 1024]
                vvb = arenaB[:, B_otm + 1024 + pg * 1024:B_otm + 1024 + pg * 1024 + 1024]
                DV.wait(t_vv)
                DV.wait(st_tok.get(t - 2))
                for hh in range(H):
                    DV.do(lambda h, hh=hh, lsb=lsb, t=t, pg=pg: h.tensor_scalar(
                        out=aF(F_acc + pg * 1024 + hh * 256, 256),
                        in0=lsb[:, hh * 256:(hh + 1) * 256],
                        scalar1=stm[:, t * 32 + T_pls + hh:t * 32 + T_pls + hh + 1],
                        scalar2=None, op0=ALU.mult), inc=1)
                DV.do(lambda h, llb=llb, t=t, pg=pg: h.scalar_tensor_tensor(
                    out=aF(F_acc + pg * 1024, 1024), in0=llb,
                    scalar=stm[:, t * 32 + T_p + 1:t * 32 + T_p + 2],
                    in1=aF(F_acc + pg * 1024, 1024), op0=ALU.mult, op1=ALU.add), inc=1)
                DV.do(lambda h, ddb=ddb, t=t, pg=pg: h.scalar_tensor_tensor(
                    out=aF(F_acc + pg * 1024, 1024), in0=ddb,
                    scalar=stm[:, t * 32 + T_p + 2:t * 32 + T_p + 3],
                    in1=aF(F_acc + pg * 1024, 1024), op0=ALU.mult, op1=ALU.add), inc=1)
                t_ot = DV.do(lambda h, vvb=vvb, t=t, pg=pg: h.scalar_tensor_tensor(
                    out=aF(F_acc + pg * 1024, 1024), in0=vvb,
                    scalar=stm[:, t * 32 + T_p + 3:t * 32 + T_p + 4],
                    in1=aF(F_acc + pg * 1024, 1024), op0=ALU.mult, op1=ALU.add), inc=1)
                mix_dv[t] = t_ot
                GD.phase()
                GP.wait(t_ot)
                st_tok[t] = GD.dma(lambda h, t=t, pg=pg: h.dma_start(
                    out=h_o[t * 128:(t + 1) * 128, :], in_=aF(F_acc + pg * 1024, 1024)))
            barrier()
            def get_o(fc):
                pg = fc % 2
                SY.phase()
                SY.wait(TE.tok())
                tok = SY.dma(lambda h, fc=fc, pg=pg: h.dma_start_transpose(
                    out=arenaB[:, B_mix + pg * 2048:B_mix + pg * 2048 + 2048],
                    in_=h_o[:, fc * 128:(fc + 1) * 128]))
                return arenaB[:, B_mix + pg * 2048:B_mix + pg * 2048 + 2048], tok
            stats_fm2(get_o, None, (64, 160), True, presE)
            t_cg = DV.do(lambda h: h.tensor_copy(out=srows_bf[R_ogpark:R_ogpark + 8, :],
                                                 in_=srows[R_ogpark:R_ogpark + 8, :]), inc=1)
            TE.wait(t_cg)
            og_ps = [p1a[0:1, 0:512], p1b[0:1, 0:512], pC0[0:1, 0:512], pC1[0:1, 0:512]]
            prev_ev = {0: None, 1: None}
            prev_g2 = {0: None, 1: None}
            for fo in range(4):
                g2buf = arenaB[:, B_mix + (fo % 2) * 2048:B_mix + (fo % 2) * 2048 + 2048]
                for th in range(2):
                    ps = PA if th == 0 else PB
                    TE.wait(prev_ev[th])
                    for kc in range(NFM):
                        mm(ps,
                           arenaW[:, W_wog1 + kc * 512 + fo * 128:W_wog1 + kc * 512 + (fo + 1) * 128],
                           arenaB[:, B_g1 + kc * 2048 + th * 1024:B_g1 + kc * 2048 + (th + 1) * 1024],
                           start=(kc == 0), stop=False)
                    mm(ps,
                       arenaW[:, W_wog1 + 4096 + fo * 128:W_wog1 + 4096 + (fo + 1) * 128],
                       srows_bf[:, th * 1024:(th + 1) * 1024],
                       False, True)
                    t1 = TE.bump()
                    AC.wait(t1)
                    AC.wait(prev_g2[fo % 2])
                    for hf in range(2):
                        t2 = AC.do(lambda h, fo=fo, th=th, ps=ps, g2buf=g2buf, hf=hf: h.activation(
                            out=g2buf[:, th * 1024 + hf * 512:th * 1024 + (hf + 1) * 512],
                            in_=ps[hf][:, 0:512], func=AF.Gelu,
                            bias=consts[:, C_bog1 + fo:C_bog1 + fo + 1]), inc=1)
                    prev_ev[th] = t2
                TE.wait(prev_ev[0])
                TE.wait(prev_ev[1])
                for q in range(4):
                    mm(og_ps[q], arenaW[:, W_wog2 + fo:W_wog2 + fo + 1],
                       g2buf[:, q * 512:(q + 1) * 512],
                       start=(fo == 0), stop=(fo == 3))
                prev_g2[fo % 2] = TE.bump()
            t_og = TE.tok()
            AC.wait(t_og)
            for q in range(4):
                AC.do(lambda h, q=q: h.activation(
                    out=srows[R_og:R_og + 1, q * 512:(q + 1) * 512], in_=og_ps[q],
                    func=AF.Sigmoid, bias=consts[0:1, C_bog2:C_bog2 + 1]), inc=1)
            barrier()
            small_rt(0, 8, T_og)
            sg3 = stm[:, 0:512].rearrange("p (t c) -> p t c", c=32)
            OG_ = sg3[:, :, T_og:T_og + 1]
            SC_ = sg3[:, :, T_sc:T_sc + 4]
            DV.do(lambda h: h.tensor_tensor(
                out=sg3[:, :, T_tmp:T_tmp + 1], in0=OG_, in1=OG_, op=ALU.mult), inc=1)
            DV.do(lambda h: h.tensor_tensor(
                out=SC_, in0=sg3[:, :, T_rrms:T_rrms + 4],
                in1=sg3[:, :, T_tmp:T_tmp + 1].broadcast_to([128, 16, 4]),
                op=ALU.mult), inc=1)
            t_ms = DV.tok()
            AC.wait(t_ms)
            AC.do(lambda h: h.activation(
                out=SC_, in_=SC_, func=AF.Sqrt, bias=consts[:, C_eln:C_eln + 1]), inc=1)
            t_sq = AC.tok()
            DV.wait(t_sq)
            DV.do(lambda h: h.reciprocal(out=SC_, in_=SC_), inc=1)
            DV.do(lambda h: h.tensor_tensor(
                out=SC_, in0=SC_,
                in1=OG_.broadcast_to([128, 16, 4]), op=ALU.mult), inc=1)
            on_dv = {}
            on_st = {}
            for t in range(NTM):
                pg = t % 2
                SY.phase()
                SY.wait(on_dv.get(t - 2))
                t_o = SY.dma(lambda h, t=t, pg=pg: h.dma_start(
                    out=arenaB[:, B_mix + pg * 1024:B_mix + pg * 1024 + 1024],
                    in_=h_o[t * 128:(t + 1) * 128, :]))
                DV.wait(t_o)
                DV.wait(on_st.get(t - 2))
                for hh in range(H):
                    DV.do(lambda h, t=t, pg=pg, hh=hh: h.tensor_scalar(
                        out=arenaB[:, B_mix + 2048 + pg * 1024 + hh * 256:B_mix + 2048 + pg * 1024 + (hh + 1) * 256],
                        in0=arenaB[:, B_mix + pg * 1024 + hh * 256:B_mix + pg * 1024 + (hh + 1) * 256],
                        scalar1=stm[:, t * 32 + T_sc + hh:t * 32 + T_sc + hh + 1],
                        scalar2=None, op0=ALU.mult), inc=1)
                t_on = DV.tok()
                on_dv[t] = t_on
                SY.phase()
                SY.wait(t_on)
                on_st[t] = SY.dma(lambda h, t=t, pg=pg: h.dma_start(
                    out=h_on[t * 128:(t + 1) * 128, :],
                    in_=arenaB[:, B_mix + 2048 + pg * 1024:B_mix + 2048 + pg * 1024 + 1024]))
            barrier()
            for fc in range(NFM):
                SY.dma(lambda h, fc=fc: h.dma_start_transpose(
                    out=arenaB[:, B_onfm + fc * 2048:B_onfm + (fc + 1) * 2048],
                    in_=h_on[:, fc * 128:(fc + 1) * 128]))
            barrier()
            prev_res = {0: None, 1: None, 2: None, 3: None}
            cp_tok = {0: None, 1: None}
            st_tok = {0: None, 1: None}
            bst_tok = {0: None, 1: None}
            for tc in range(NTM):
                par = tc % 2
                par4 = tc % 4
                ps = (PA, PB, PC, PD)[par4]
                ln = aF(F_ln, 1024) if par == 0 else aF(F_acc, 1024)
                mb = F_m + par * 4
                SY.phase()
                SY.wait(cp_tok[par])
                SY.wait(st_tok[par])
                t_x = SY.dma(lambda h, tc=tc, ln=ln: h.dma_start(
                    out=ln, in_=x_in[tc * 128:(tc + 1) * 128, :]))
                TE.wait(prev_res[par4])
                for kc in range(NFM):
                    mm(ps,
                       arenaB[:, B_onfm + kc * 2048 + tc * 128:B_onfm + kc * 2048 + (tc + 1) * 128],
                       arenaW[:, W_wo + kc * 1024:W_wo + (kc + 1) * 1024],
                       start=(kc == 0), stop=(kc == NFM - 1))
                t_mm2 = TE.bump()
                DV.wait(t_mm2); DV.wait(t_x)
                DV.do(lambda h, ps=ps, ln=ln: h.tensor_add(out=ln[:, 0:512], in0=ln[:, 0:512],
                                                           in1=ps[0][:, 0:512]), inc=1)
                DV.do(lambda h, ps=ps, ln=ln: h.tensor_add(out=ln[:, 512:1024], in0=ln[:, 512:1024],
                                                           in1=ps[1][:, 0:512]), inc=1)
                t_res = DV.tok()
                prev_res[par4] = t_res
                DV.do(lambda h, ln=ln, mb=mb: h.reduce_sum(out=aF(mb, 1), in_=ln, axis=AX.X), inc=1)
                AC.wait(DV.tok())
                t_sq = AC.do(lambda h, ln=ln, mb=mb: h.activation(
                    out=aF(F_acc + 1024, 1024), in_=ln, func=AF.Square,
                    accum_out=aF(mb + 1, 1)), inc=1)
                DV.wait(t_sq)
                DV.do(lambda h, mb=mb: h.tensor_scalar(out=aF(mb, 1), in0=aF(mb, 1),
                                                       scalar1=1.0 / 1024, scalar2=None,
                                                       op0=ALU.mult), inc=1)
                DV.do(lambda h, mb=mb: h.tensor_mul(out=aF(mb + 2, 1), in0=aF(mb, 1),
                                                    in1=aF(mb, 1)), inc=1)
                t_v2 = DV.do(lambda h, mb=mb: h.scalar_tensor_tensor(
                    out=aF(mb + 3, 1), in0=aF(mb + 1, 1), scalar=1.0 / 1024,
                    in1=aF(mb + 2, 1), op0=ALU.mult, op1=ALU.subtract), inc=1)
                AC.wait(t_v2)
                t_rs = AC.do(lambda h, mb=mb: h.activation(out=aF(mb + 3, 1), in_=aF(mb + 3, 1),
                                                           func=AF.Sqrt,
                                                           bias=consts[:, C_eln:C_eln + 1]), inc=1)
                DV.wait(t_rs)
                DV.do(lambda h, mb=mb: h.reciprocal(out=aF(mb + 3, 1), in_=aF(mb + 3, 1)), inc=1)
                t_ap = DV.do(lambda h, ln=ln, mb=mb: h.tensor_scalar(
                    out=ln, in0=ln,
                    scalar1=aF(mb, 1), scalar2=aF(mb + 3, 1),
                    op0=ALU.subtract, op1=ALU.mult), inc=1)
                SY.phase()
                SY.wait(t_ap)
                st_tok[par] = SY.dma(lambda h, tc=tc, ln=ln: h.dma_start(
                    out=h_x2[tc * 128:(tc + 1) * 128, :], in_=ln))
                DV.wait(bst_tok[par])
                t_bf = DV.do(lambda h, ln=ln, par=par: h.tensor_copy(
                    out=wsm[:, par * 1024:(par + 1) * 1024], in_=ln), inc=1)
                cp_tok[par] = t_bf
                SY.phase()
                SY.wait(t_bf)
                bst_tok[par] = SY.dma(lambda h, tc=tc, par=par: h.dma_start(
                    out=h_x2bf[tc * 128:(tc + 1) * 128, :], in_=wsm[:, par * 1024:(par + 1) * 1024]))
            barrier()


        QW = 0
        q_tok = {}
        q_red = {}
        q_state = {"k": 0}

        def _q_buf(k):
            pg = k % 2
            return arenaB[:, B_lsld + pg * 2048:B_lsld + pg * 2048 + 2048].bitcast(F32)[:, 0:1024]

        def _q_load(k):
            qbuf = _q_buf(k)
            GD.phase()
            GP.wait(q_red.get(k - 2))
            if k < 32:
                jc = k
                q_tok[k] = GD.dma(lambda h, jc=jc, qbuf=qbuf: h.dma_start(
                    out=qbuf, in_=wfc_h[jc * 128:(jc + 1) * 128, :]))
            else:
                pc, qq = (k - 32) // 4, (k - 32) % 4
                q_tok[k] = GD.dma(lambda h, pc=pc, qq=qq, qbuf=qbuf: h.dma_start(
                    out=qbuf,
                    in_=wpj_h[pc * 128:(pc + 1) * 128, qq * 1024:(qq + 1) * 1024]))

        def _q_reduce(k):
            qbuf = _q_buf(k)
            DV.wait(q_tok[k])
            if k < 32:
                jc = k
                q_red[k] = DV.do(lambda h, jc=jc, qbuf=qbuf: h.tensor_reduce(
                    out=aF(F_sc + jc, 1), in_=qbuf, axis=AX.X,
                    op=ALU.add, apply_absolute_value=True), inc=1)
            else:
                pc, qq = (k - 32) // 4, (k - 32) % 4
                if qq == 0:
                    q_red[k] = DV.do(lambda h, pc=pc, qbuf=qbuf: h.tensor_reduce(
                        out=aF(F_sc + 32 + pc, 1), in_=qbuf, axis=AX.X,
                        op=ALU.add, apply_absolute_value=True), inc=1)
                else:
                    DV.do(lambda h, qbuf=qbuf: h.tensor_reduce(
                        out=aF(F_sc + 48, 1), in_=qbuf, axis=AX.X,
                        op=ALU.add, apply_absolute_value=True), inc=1)
                    q_red[k] = DV.do(lambda h, pc=pc: h.tensor_add(
                        out=aF(F_sc + 32 + pc, 1), in0=aF(F_sc + 32 + pc, 1),
                        in1=aF(F_sc + 48, 1)), inc=1)

        N_QITEMS = 64
        def q_hook():
            for _ in range(3):
                k = q_state["k"]
                if k < N_QITEMS:
                    if k - 2 >= 0:
                        _q_reduce(k - 2)
                    _q_load(k)
                    q_state["k"] = k + 1

        def stage_Qscale():
            while q_state["k"] < N_QITEMS:
                q_hook()
            for k in (N_QITEMS - 2, N_QITEMS - 1):
                if k not in q_red:
                    _q_reduce(k)

        def stage_Qfc():
            for jc in range(32):
                pg = jc % 2
                SY.phase()
                SY.wait(DV.tok())
                t_w = SY.dma(lambda h, jc=jc, pg=pg: h.dma_start(
                    out=aF(F_acc + pg * 1024, 1024),
                    in_=wfc_h[jc * 128:(jc + 1) * 128, :]))
                wb = aF(F_acc + pg * 1024, 1024)
                DV.do(lambda h, jc=jc: h.tensor_scalar(
                    out=aF(F_sc + jc, 1), in0=aF(F_sc + jc, 1),
                    scalar1=1.0 / 1024, scalar2=1e-5,
                    op0=ALU.mult, op1=ALU.max), inc=1)
                DV.do(lambda h, jc=jc: h.tensor_scalar(out=aF(F_sc + 50, 1), in0=aF(F_sc + jc, 1),
                                                       scalar1=0.5, scalar2=None, op0=ALU.mult), inc=1)
                DV.do(lambda h, jc=jc: h.tensor_scalar(out=aF(F_sc + 51, 1), in0=aF(F_sc + jc, 1),
                                                       scalar1=-0.5, scalar2=None, op0=ALU.mult), inc=1)
                DV.do(lambda h, jc=jc: h.tensor_scalar(out=aF(F_sc + 52, 1), in0=aF(F_sc + jc, 1),
                                                       scalar1=-1.0, scalar2=None, op0=ALU.mult), inc=1)
                DV.wait(t_w)
                DV.do(lambda h, wb=wb, jc=jc: h.tensor_scalar(
                    out=arenaW[:, QW:QW + 1024], in0=wb, scalar1=aF(F_sc + 50, 1),
                    scalar2=aF(F_sc + jc, 1), op0=ALU.is_gt, op1=ALU.mult), inc=1)
                DV.do(lambda h, wb=wb: h.tensor_scalar(
                    out=arenaW[:, QW + 1024:QW + 2048], in0=wb, scalar1=aF(F_sc + 51, 1),
                    scalar2=aF(F_sc + 52, 1), op0=ALU.is_lt, op1=ALU.mult), inc=1)
                t_t = DV.do(lambda h: h.tensor_add(out=arenaW[:, QW + 2048:QW + 3072],
                                                   in0=arenaW[:, QW:QW + 1024],
                                                   in1=arenaW[:, QW + 1024:QW + 2048]), inc=1)
                SY.phase()
                SY.wait(t_t)
                t_s = SY.dma(lambda h, jc=jc: h.dma_start(
                    out=h_tern[jc * 128:(jc + 1) * 128, :], in_=arenaW[:, QW + 2048:QW + 3072]))
                DV.wait(t_s)

        def stage_Qpj():
            for pc in range(8):
                wb = aF(F_acc, 2048)
                DV.do(lambda h, pc=pc: h.tensor_scalar(
                    out=aF(F_sc + 32 + pc, 1), in0=aF(F_sc + 32 + pc, 1),
                    scalar1=1.0 / 4096, scalar2=1e-5,
                    op0=ALU.mult, op1=ALU.max), inc=1)
                DV.do(lambda h, pc=pc: h.tensor_scalar(out=aF(F_sc + 50, 1),
                                                       in0=aF(F_sc + 32 + pc, 1),
                                                       scalar1=0.5, scalar2=None, op0=ALU.mult), inc=1)
                DV.do(lambda h, pc=pc: h.tensor_scalar(out=aF(F_sc + 51, 1),
                                                       in0=aF(F_sc + 32 + pc, 1),
                                                       scalar1=-0.5, scalar2=None, op0=ALU.mult), inc=1)
                for half in (1, 0):
                    SY.phase()
                    SY.wait(DV.tok())
                    t_w = SY.dma(lambda h, pc=pc, half=half: h.dma_start(
                        out=wb, in_=wpj_h[pc * 128:(pc + 1) * 128,
                                          half * 2048:(half + 1) * 2048]))
                    DV.wait(t_w)
                    DV.do(lambda h, wb=wb: h.tensor_scalar(
                        out=arenaW[:, QW:QW + 2048], in0=wb, scalar1=aF(F_sc + 50, 1),
                        scalar2=1.0, op0=ALU.is_gt, op1=ALU.mult), inc=1)
                    DV.do(lambda h, wb=wb: h.tensor_scalar(
                        out=arenaW[:, QW + 2048:QW + 4096], in0=wb, scalar1=aF(F_sc + 51, 1),
                        scalar2=-1.0, op0=ALU.is_lt, op1=ALU.mult), inc=1)
                    t_a = DV.do(lambda h: h.tensor_add(out=arenaW[:, QW:QW + 2048],
                                                       in0=arenaW[:, QW:QW + 2048],
                                                       in1=arenaW[:, QW + 2048:QW + 4096]), inc=1)
                    SY.phase()
                    SY.wait(t_a)
                    t_s = SY.dma(lambda h, pc=pc, half=half: h.dma_start(
                        out=h_ternp[pc * 128:(pc + 1) * 128, half * 2048:(half + 1) * 2048],
                        in_=arenaW[:, QW:QW + 2048]))
                    DV.wait(t_s)

        def stage_F():
            barrier()
            for dc in range(NFM):
                SY.dma(lambda h, dc=dc: h.dma_start_transpose(
                    out=arenaW[:, W_tern + dc * 4096:W_tern + (dc + 1) * 4096],
                    in_=h_tern[:, dc * 128:(dc + 1) * 128]))
            for fc in range(NFM):
                SY.dma(lambda h, fc=fc: h.dma_start_transpose(
                    out=arenaB[:, B_x2fm + fc * 2048:B_x2fm + (fc + 1) * 2048],
                    in_=h_x2bf[:, fc * 128:(fc + 1) * 128]))
            barrier()
            for th in range(2):
                for kc in range(NFM):
                    mm((pC0[0:16, 0:512], pC1[0:16, 0:512]),
                       arenaW[:, W_vT + kc * 16:W_vT + (kc + 1) * 16],
                       arenaB[:, B_x2fm + kc * 2048 + th * 1024:B_x2fm + kc * 2048 + (th + 1) * 1024],
                       start=(kc == 0), stop=(kc == NFM - 1))
                t1 = TE.bump()
                AC.wait(t1)
                AC.do(lambda h, th=th: h.copy(
                    out=wsm[0:16, th * 1024:th * 1024 + 512], in_=pC0[0:16, 0:512]), inc=1)
                t2 = AC.do(lambda h, th=th: h.copy(
                    out=wsm[0:16, th * 1024 + 512:(th + 1) * 1024], in_=pC1[0:16, 0:512]), inc=1)
                TE.wait(t2)
            for th in range(2):
                mm((pC0[0:16, 0:512], pC1[0:16, 0:512]), arenaW[0:16, W_fbT:W_fbT + 16],
                   wsm[0:16, th * 1024:(th + 1) * 1024], True, True)
                t1 = TE.bump()
                AC.wait(t1)
                AC.do(lambda h, th=th: h.activation(
                    out=wsm[0:16, 2048 + th * 1024:2048 + th * 1024 + 512], in_=pC0[0:16, 0:512],
                    func=AF.Copy, scale=consts[0:16, C_gate:C_gate + 1]), inc=1)
                t2 = AC.do(lambda h, th=th: h.activation(
                    out=wsm[0:16, 2048 + th * 1024 + 512:2048 + (th + 1) * 1024], in_=pC1[0:16, 0:512],
                    func=AF.Copy, scale=consts[0:16, C_gate:C_gate + 1]), inc=1)
                TE.wait(t2)
            barrier()
            ln2_state = {"ap": {0: None, 1: None}, "yst": {0: None, 1: None},
                         "add": {0: None, 1: None}}
            def ln2_quarter(tq4, prev_st):
                ap_tok = ln2_state["ap"]
                yst_tok = ln2_state["yst"]
                add_tok = ln2_state["add"]
                SY.phase()
                SY.wait(prev_st[0])
                SY.wait(prev_st[1])
                for tc in range(tq4 * 4, tq4 * 4 + 4):
                    par = tc % 2
                    ln = aF(F_ln, 1024) if par == 0 else aF(F_acc, 1024)
                    mb = F_m + par * 4
                    SY.phase()
                    SY.wait(ap_tok[par])
                    SY.wait(yst_tok[par])
                    t_x = SY.dma(lambda h, tc=tc, ln=ln: h.dma_start(
                        out=ln, in_=h_x2[tc * 128:(tc + 1) * 128, :]))
                    SY.phase()
                    SY.wait(add_tok[par])
                    t_f = SY.dma(lambda h, tc=tc, par=par: h.dma_start_transpose(
                        out=wsm[:, par * 1024:(par + 1) * 1024], in_=h_ffm[:, tc * 128:(tc + 1) * 128]))
                    DV.wait(t_x); DV.wait(t_f)
                    t_add = DV.do(lambda h, ln=ln, par=par: h.tensor_add(
                        out=ln, in0=ln, in1=wsm[:, par * 1024:(par + 1) * 1024]), inc=1)
                    add_tok[par] = t_add
                    DV.do(lambda h, ln=ln, mb=mb: h.reduce_sum(out=aF(mb, 1), in_=ln, axis=AX.X), inc=1)
                    AC.wait(DV.tok())
                    t_sq = AC.do(lambda h, ln=ln, mb=mb: h.activation(
                        out=aF(F_acc + 1024, 1024), in_=ln, func=AF.Square,
                        accum_out=aF(mb + 1, 1)), inc=1)
                    DV.wait(t_sq)
                    DV.do(lambda h, mb=mb: h.tensor_scalar(out=aF(mb, 1), in0=aF(mb, 1),
                                                           scalar1=1.0 / 1024, scalar2=None,
                                                           op0=ALU.mult), inc=1)
                    DV.do(lambda h, mb=mb: h.tensor_mul(out=aF(mb + 2, 1), in0=aF(mb, 1),
                                                        in1=aF(mb, 1)), inc=1)
                    t_v2 = DV.do(lambda h, mb=mb: h.scalar_tensor_tensor(
                        out=aF(mb + 3, 1), in0=aF(mb + 1, 1), scalar=1.0 / 1024,
                        in1=aF(mb + 2, 1), op0=ALU.mult, op1=ALU.subtract), inc=1)
                    AC.wait(t_v2)
                    t_rs = AC.do(lambda h, mb=mb: h.activation(out=aF(mb + 3, 1), in_=aF(mb + 3, 1),
                                                               func=AF.Sqrt,
                                                               bias=consts[:, C_eln:C_eln + 1]), inc=1)
                    DV.wait(t_rs)
                    DV.do(lambda h, mb=mb: h.reciprocal(out=aF(mb + 3, 1), in_=aF(mb + 3, 1)), inc=1)
                    t_ap = DV.do(lambda h, ln=ln, mb=mb: h.tensor_scalar(
                        out=ln, in0=ln,
                        scalar1=aF(mb, 1), scalar2=aF(mb + 3, 1),
                        op0=ALU.subtract, op1=ALU.mult), inc=1)
                    ap_tok[par] = t_ap
                    SY.phase()
                    SY.wait(t_ap)
                    yst_tok[par] = SY.dma(lambda h, tc=tc, ln=ln: h.dma_start(
                        out=y_out[tc * 128:(tc + 1) * 128, :], in_=ln))
            prev_st = {0: None, 1: None}
            for tq4 in range(4):
                th, tq = tq4 // 2, tq4 % 2
                t_ld = {}
                for j0 in range(3):
                    SY.phase()
                    SY.wait(TE.tok())
                    t_ld[j0] = SY.dma(lambda h, j0=j0: h.dma_start_transpose(
                        out=arenaW[:, W_pstr + (j0 % 4) * 1024:W_pstr + (j0 % 4) * 1024 + 1024],
                        in_=h_ternp[:, j0 * 128:(j0 + 1) * 128]))
                prev_sq = {0: None, 1: None, 2: None, 3: None}
                for jc in range(32):
                    pg = jc % 4
                    ps = (pA0, pA1, pB0, pB1)[pg]
                    TE.wait(prev_sq[pg])
                    for kc in range(NFM):
                        mm(ps[:, 0:512],
                           arenaW[:, W_tern + kc * 4096 + jc * 128:W_tern + kc * 4096 + (jc + 1) * 128],
                           arenaB[:, B_x2fm + kc * 2048 + tq4 * 512:B_x2fm + kc * 2048 + (tq4 + 1) * 512],
                           start=(kc == 0), stop=False)
                    mm(ps[:, 0:512], arenaW[0:16, W_uT + jc * 128:W_uT + (jc + 1) * 128],
                       wsm[0:16, 2048 + tq4 * 512:2048 + (tq4 + 1) * 512], False, True)
                    t1 = TE.bump()
                    DV.wait(t1)
                    t_r = DV.do(lambda h, ps=ps, pg=pg: h.tensor_scalar(
                        out=arenaB[:, B_fq + pg * 512:B_fq + pg * 512 + 512], in0=ps[:, 0:512],
                        scalar1=0.0, scalar2=None, op0=ALU.max), inc=1)
                    AC.wait(t_r)
                    t_sq = AC.do(lambda h, jc=jc, pg=pg: h.activation(
                        out=arenaB[:, B_r2 + jc * 512:B_r2 + (jc + 1) * 512],
                        in_=arenaB[:, B_fq + pg * 512:B_fq + pg * 512 + 512], func=AF.Square), inc=1)
                    prev_sq[pg] = t_sq
                for pg in range(4):
                    TE.wait(prev_sq[pg])
                psl = [pA0[:, 0:512], pA1[:, 0:512], pB0[:, 0:512], pB1[:, 0:512],
                       pC0[:, 0:512], pC1[:, 0:512], p1a[:, 0:512], p1b[:, 0:512]]
                for jc in range(32):
                    sl = jc % 4
                    if jc + 3 < 32:
                        SY.phase()
                        SY.wait(TE.tok())
                        t_ld[jc + 3] = SY.dma(lambda h, jc=jc: h.dma_start_transpose(
                            out=arenaW[:, W_pstr + ((jc + 3) % 4) * 1024:W_pstr + ((jc + 3) % 4) * 1024 + 1024],
                            in_=h_ternp[:, (jc + 3) * 128:(jc + 4) * 128]))
                    TE.wait(t_ld[jc])
                    for jout in range(NFM):
                        mm(psl[jout],
                           arenaW[:, W_pstr + sl * 1024 + jout * 128:W_pstr + sl * 1024 + (jout + 1) * 128],
                           arenaB[:, B_r2 + jc * 512:B_r2 + (jc + 1) * 512],
                           start=(jc == 0), stop=(jc == 31))
                    TE.bump()
                t1 = TE.tok()
                AC.wait(t1)
                for jout in range(NFM):
                    jp = jout % 2
                    AC.wait(prev_st[jp])
                    t2 = AC.do(lambda h, jout=jout, jp=jp: h.activation(
                        out=arenaB[:, B_fq + 2048 + jp * 512:B_fq + 2048 + jp * 512 + 512],
                        in_=psl[jout], func=AF.Copy, scale=aF(F_sc + 32 + jout, 1)), inc=1)
                    SY.phase()
                    SY.wait(t2)
                    prev_st[jp] = SY.dma(lambda h, jout=jout, tq4=tq4, jp=jp: h.dma_start(
                        out=h_ffm[jout * 128:(jout + 1) * 128, tq4 * 512:(tq4 + 1) * 512],
                        in_=arenaB[:, B_fq + 2048 + jp * 512:B_fq + 2048 + jp * 512 + 512]))
                TE.wait(AC.tok())
                ln2_quarter(tq4, prev_st)
            barrier()
            wait_all_sy(SY)

        def script():
            stages = [st_init, stage_A_v,
                      lambda: small_rt(0, 8, T_beta),
                      stage_A_k, stage_A_q,
                      lambda: small_rt(R_rq, 4, T_rq),
                      stage_Qscale,
                      stage_D, stage_B_k, stage_C_pre, stage_C_scan,
                      stage_E, stage_Qfc, stage_Qpj, stage_F]
            for i, st in enumerate(stages):
                if i >= stage_limit:
                    break
                st()
            if stage_limit < len(stages):
                barrier()
                wait_all_sy(SY)

        def reset_all():
            for e in ALLE:
                e.h, e.active, e.count = None, False, 0
            for ee in (GD, SY):
                ee.sem_pool = [(s, 0) for (s, c) in ee.sem_pool]
                ee.pool_idx = 0
            q_state["k"] = 0
            q_red.clear()
            q_tok.clear()

        @blk.sync
        def _(sync):
            reset_all(); SY.h, SY.active = sync, True
            script()

        @blk.gpsimd
        def _(gph):
            reset_all(); GP.h, GP.active = gph, True; GD.h, GD.active = gph, True
            script()

        @blk.tensor
        def _(teh):
            reset_all(); TE.h, TE.active = teh, True
            script()

        @blk.scalar
        def _(ach):
            reset_all(); AC.h, AC.active = ach, True
            script()

        @blk.vector
        def _(dvh):
            reset_all(); DV.h, DV.active = dvh, True
            script()

    return nc, dbg



def _const_onesv():
    out = np.zeros((128, 512), np.float32)
    for hh in range(4):
        out[:, 32 * (0 + hh) + 0 + hh] = 1.0
        out[:, 32 * (4 + hh) + 4 + hh] = 1.0
        out[:, 32 * (8 + hh) + 8 + hh] = 1.0
        out[:, 32 * (12 + hh) + 0 + hh] = 1.0
    return out


def _const_mixmx():
    out = np.zeros((128, 192), np.float32)
    for t in range(4):
        for hh in range(4):
            out[0 + hh, 16 * t + 4 * t + 0] = 0.25 / 256
            out[4 + hh, 16 * t + 4 * t + 1] = 0.25 / 256
    out[:] = 0.0
    for t in range(4):
        c = 16 * t
        j = 4 * t
        for hh in range(4):
            out[0 + hh, c + j + 0] = 0.25 / 256
            out[4 + hh, c + j + 1] = 0.25 / 256
            out[8 + hh, c + j + 2] = 0.25 / 256
            out[96 + hh, c + j + 3] = 0.25
            out[0 + hh, 96 + c + j + 1] = -0.25
    for base, j0, p2 in ((64, 0, 160), (72, 4, 168)):
        for hh in range(4):
            out[0 + hh, base + j0 + 0] = 0.25 / 256
            out[4 + hh, base + j0 + 1] = 0.25 / 256
            out[8 + hh, base + j0 + 2] = 0.25 / 256
            out[96 + hh, base + j0 + 3] = 0.25
            out[0 + hh, p2 + j0 + 1] = -0.25
    return out


def _tapcol(w):
    return np.ascontiguousarray(w.reshape(8, 128, 4).transpose(1, 0, 2).reshape(128, 32))


def _pad_tail(w, row0):
    out = np.zeros((128, w.shape[1]), np.float32)
    out[row0:row0 + w.shape[0]] = w
    return out


def _host_inputs(inputs, core):
    g = lambda k: np.asarray(inputs[k], np.float32)
    x = np.asarray(inputs["L0_attn_hidden_states"])[core]

    def diag_layout(w):
        K = w.shape[1]
        out = np.zeros((128, NFM * K * 128), np.float32)
        for fc in range(NFM):
            for t in range(K):
                b = out[:, (fc * K + t) * 128:(fc * K + t + 1) * 128]
                np.fill_diagonal(b, w[fc * 128:(fc + 1) * 128, t])
        return out

    def tpl_layout(w):
        out = np.zeros((128, 256 * 3 * 32), np.float32)
        r = np.arange(32)[:, None]
        j = np.arange(32)[None, :]
        for gg in range(4):
            for e in range(256):
                c = 256 * gg + e
                for d in range(3):
                    t = 63 - 32 * d + r - j
                    T = np.where((t >= 0) & (t <= 63), w[c][np.clip(t, 0, 63)], 0.0)
                    out[32 * gg:32 * (gg + 1), (e * 3 + d) * 32:(e * 3 + d + 1) * 32] = T
        return out

    def col128(v):
        return np.ascontiguousarray(np.asarray(v, np.float32).reshape(-1, 128).T)

    def pad_cols(w, total, at):
        out = np.zeros((w.shape[0], total), np.float32)
        out[:, at:at + w.shape[1]] = w
        return out

    d = {
        "x": np.ascontiguousarray(x, np.float32),
        "Wq": g("L0_attn_Wq"), "Wk": g("L0_attn_Wk"), "Wv": g("L0_attn_Wv"),
        "Wbp": g("L0_attn_Wb"),
        "Wrp": pad_cols(g("L0_attn_Wres"), 8, 4),
        "dq": diag_layout(g("L0_attn_wq_conv")),
        "dk": diag_layout(g("L0_attn_wk_conv")),
        "dv": diag_layout(g("L0_attn_wv_conv")),
        "ds": diag_layout(g("L0_attn_fir_short").reshape(1024, 5)),
        "tpl": tpl_layout(g("L0_attn_fir_long").reshape(1024, 64)),
        "tapq": _tapcol(g("L0_attn_wq_conv")),
        "tapk": _tapcol(g("L0_attn_wk_conv")),
        "tapv": _tapcol(g("L0_attn_wv_conv")),
        "taps5": np.ascontiguousarray(g("L0_attn_fir_short").reshape(8, 128, 5)
                                      .transpose(1, 0, 2).reshape(128, 40)),
        "Wg1": g("L0_attn_Wg1"),
        "Wg1t": _pad_tail(g("L0_attn_Wg1")[1024:1040], 32),
        "bg1": col128(g("L0_attn_bg1")),
        "Wg2p": g("L0_attn_Wg2"),
        "bg2": g("L0_attn_bg2").reshape(H, 1),
        "logit_temp": g("L0_attn_logit_temp").reshape(1, 1),
        "conv_res_logit": g("L0_attn_conv_res_logit").reshape(H, 1),
        "bres": g("L0_attn_bres").reshape(H, 1),
        "Wog1": g("L0_attn_Wog1"),
        "Wog1t": _pad_tail(g("L0_attn_Wog1")[1024:1032], 64),
        "bog1": col128(g("L0_attn_bog1")),
        "Wog2": g("L0_attn_Wog2"),
        "bog2": g("L0_attn_bog2").reshape(1, 1),
        "norm_w": col128(np.tile(g("L0_attn_norm_w"), 4)),
        "Wo": g("L0_attn_Wo"),
        "w_fc": g("L0_ffn_w_fc"), "w_proj": g("L0_ffn_w_proj"),
        "uT": np.ascontiguousarray(g("L0_ffn_u").T),
        "vT": np.ascontiguousarray(g("L0_ffn_v").T),
        "fbT": np.ascontiguousarray(g("L0_ffn_fast_b").T),
        "gate": g("L0_ffn_gate").reshape(1, 1),
        "onesv_c": _const_onesv(),
        "mixmx_c": _const_mixmx(),
    }
    return {k: np.ascontiguousarray(v, np.float32) for k, v in d.items()}


def kernel(**inputs):
    nc, _ = build_core()
    in_maps = [_host_inputs(inputs, c) for c in range(NCORE)]
    res = run_bass_kernel_spmd(nc, in_maps, core_ids=list(range(NCORE)))
    return np.stack([r["y"] for r in res.results], axis=0).astype(np.float32)


# version 31
# speedup vs baseline: 1.0080x; 1.0008x over previous
import contextlib
import numpy as np
import ml_dtypes
import concourse.bass as bass
import concourse.mybir as mybir
from concourse.bass_utils import run_bass_kernel_spmd

F32 = mybir.dt.float32
BF16 = mybir.dt.bfloat16
AF = mybir.ActivationFunctionType
ALU = mybir.AluOpType
AX = mybir.AxisListType

L, D, H, DK = 2048, 1024, 4, 256
NCH, NFM, NTM = 16, 8, 16
FFH = 4096
NCORE = 8
EL2, ERMS, ELN = 1e-6, 1e-5, 1e-5


class Tok:
    __slots__ = ("sem", "val")
    def __init__(self, sem, val):
        self.sem, self.val = sem, val


class Eng:
    GD_REF = None

    def __init__(self, name):
        self.name, self.sem, self.h, self.active, self.count = name, None, None, False, 0
        self.sem_pool = None
        self.pool_idx = 0
        self.self_sync = False

    def phase(self):
        if self.sem_pool is not None:
            self.pool_idx = (self.pool_idx + 1) % len(self.sem_pool)

    def do(self, fn, inc=0):
        if self.active:
            if self.self_sync and self.count > 0:
                self.h.wait_ge(self.sem, self.count)
            ins = fn(self.h)
            if inc:
                ins.then_inc(self.sem, inc)
        if inc:
            self.count += inc
        return Tok(self.sem, self.count)

    def dma(self, fn):
        if self.sem_pool is not None:
            sem, cnt = self.sem_pool[self.pool_idx]
            cnt += 16
            self.sem_pool[self.pool_idx] = (sem, cnt)
            if self.active:
                ins = fn(self.h)
                ins.then_inc(sem, 16)
            return Tok(sem, cnt)
        return self.do(fn, inc=16)

    def bump(self, n=1):
        if self.active:
            self.h.drain()
            self.h.sem_inc(self.sem, n)
        self.count += n
        return Tok(self.sem, self.count)

    def tok(self):
        return Tok(self.sem, self.count)

    POOLED = ()

    def wait(self, tok):
        if tok is None:
            return
        for eng in Eng.POOLED:
            if eng.sem_pool is not None:
                done = False
                for sem, cnt in eng.sem_pool:
                    if tok.sem is sem:
                        tok = Tok(sem, cnt)
                        done = True
                        break
                if done:
                    break
        if self.active:
            self.h.wait_ge(tok.sem, tok.val)


def build_core(debug_outs=(), stage_limit=99, d_sub=99, f_sub=99):
    nc = bass.Bass("TRN2", target_bir_lowering=False)
    DI = lambda n, s: nc.dram_tensor(n, s, F32, kind="ExternalInput").ap()
    DSC = lambda n, s, dt: nc.dram_tensor(
        n, s, dt, kind=("ExternalOutput" if n in debug_outs else "Internal")).ap()

    x_in = DI("x", [L, D])
    Wq_h, Wk_h, Wv_h = DI("Wq", [D, D]), DI("Wk", [D, D]), DI("Wv", [D, D])
    Wbp_h = DI("Wbp", [D, 4])
    Wrp_h = DI("Wrp", [D, 8])
    dq_h = DI("dq", [128, NFM * 4 * 128])
    dk_h = DI("dk", [128, NFM * 4 * 128])
    dvv_h = DI("dv", [128, NFM * 4 * 128])
    dss_h = DI("ds", [128, NFM * 5 * 128])
    tpl_h = DI("tpl", [128, 256 * 3 * 32])
    tapq_h = DI("tapq", [128, 32])
    tapk_h = DI("tapk", [128, 32])
    tapv_h = DI("tapv", [128, 32])
    taps_h = DI("taps5", [128, 40])
    Wg1_h = DI("Wg1", [1040, D])
    Wg1t_h = DI("Wg1t", [128, D])
    bg1_h = DI("bg1", [128, NFM])
    Wg2p_h = DI("Wg2p", [D, 4])
    bg2_h = DI("bg2", [H, 1])
    lt_h = DI("logit_temp", [1, 1])
    crl_h = DI("conv_res_logit", [H, 1])
    bres_h = DI("bres", [H, 1])
    Wog1_h = DI("Wog1", [1032, 512])
    Wog1t_h = DI("Wog1t", [128, 512])
    bog1_h = DI("bog1", [128, 4])
    Wog2_h = DI("Wog2", [512, 1])
    bog2_h = DI("bog2", [1, 1])
    nw_h = DI("norm_w", [128, NFM])
    Wo_h = DI("Wo", [D, D])
    wfc_h = DI("w_fc", [FFH, D])
    wpj_h = DI("w_proj", [D, FFH])
    uT_h = DI("uT", [16, FFH])
    vT_h = DI("vT", [D, 16])
    fbT_h = DI("fbT", [16, 16])
    gate_h = DI("gate", [1, 1])
    onesv_h = DI("onesv_c", [128, 512])
    mixmx_h = DI("mixmx_c", [128, 192])

    y_out = nc.dram_tensor("y", [L, D], F32, kind="ExternalOutput").ap()
    dbg = {}
    def dbg_ap(name, shape, dt=F32):
        if name in debug_outs:
            dbg[name] = nc.dram_tensor("dbg_" + name, shape, dt, kind="ExternalOutput").ap()
        return dbg.get(name)

    h_xbf = DSC("h_xbf", [L, D], BF16)
    h_qfm = DSC("h_qfm", [D, L], BF16)
    h_kfm = DSC("h_kfm", [D, L], BF16)
    h_kntm = DSC("h_kntm", [L, D], BF16)
    h_kbntm = DSC("h_kbntm", [L, D], BF16)
    h_knfm = DSC("h_knfm", [D, L], BF16)
    h_kbnfm = DSC("h_kbnfm", [D, L], BF16)
    h_vfm = DSC("h_vfm", [D, L], BF16)
    h_vtm = DSC("h_vtm", [L, D], BF16)
    h_vbtm = DSC("h_vbtm", [L, D], BF16)
    h_lsfm = DSC("h_lsfm", [D, L], BF16)
    h_ll = DSC("h_ll", [L, D], BF16)
    h_delta = DSC("h_delta", [L, D], BF16)
    h_E = DSC("h_E", [NCH * H * 128, 128], BF16)
    h_aT = DSC("h_aT", [NCH * H * 128, 128], BF16)
    h_o = DSC("h_o", [L, D], BF16)
    h_on = DSC("h_on", [L, D], BF16)
    h_x2 = DSC("h_x2", [L, D], F32)
    h_x2bf = DSC("h_x2bf", [L, D], BF16)
    h_tern = DSC("h_tern", [FFH, D], BF16)
    h_ternp = DSC("h_ternp", [D, FFH], BF16)
    h_ffm = DSC("h_ffm", [D, L], BF16)

    es = contextlib.ExitStack()
    with es:
        sbuf = lambda n, s, dt: es.enter_context(nc.sbuf_tensor(n, s, dt))
        psum_t = lambda n, s, dt=F32: es.enter_context(nc.psum_tensor(n, s, dt))

        TE, DV, AC, GP, SY, GD = (Eng(n) for n in ["te", "dv", "ac", "gp", "sy", "gd"])
        Eng.GD_REF = GD
        GD.sem_pool = [(es.enter_context(nc.semaphore(f"s_gd{i}")), 0) for i in range(8)]
        SY.sem_pool = [(es.enter_context(nc.semaphore(f"s_sy{i}")), 0) for i in range(8)]
        Eng.POOLED = (GD, SY)
        DV.self_sync = AC.self_sync = GP.self_sync = True
        ALLE = [TE, DV, AC, GP, SY, GD]
        for e in ALLE:
            e.sem = es.enter_context(nc.semaphore("s_" + e.name))

        masks = sbuf("masks", [128, 384], F32)
        ident_bf = sbuf("ident_bf", [128, 128], BF16)
        ident_f = sbuf("ident_f", [128, 128], F32)
        onesv = sbuf("onesv", [128, 512], BF16)
        mixmx = sbuf("mixmx", [128, 192], BF16)
        srows = sbuf("srows", [128, L], F32)
        srows_bf = sbuf("srows_bf", [128, L], BF16)
        stm = sbuf("stm", [128, NTM * 32], F32)
        consts = sbuf("consts", [128, 64], F32)
        S_f32 = sbuf("S_f32", [128, 2048], F32)
        S_bf = sbuf("S_bf", [128, 2048], BF16)
        arenaW = sbuf("arenaW", [128, 36864], BF16)
        arenaB = sbuf("arenaB", [128, 41984], BF16)
        arenaF = sbuf("arenaF", [128, 3144], F32)
        wsm = sbuf("wsm", [128, 4096], BF16)

        pA0 = psum_t("pA0", [128, 512])
        pA1 = psum_t("pA1", [128, 512])
        pB0 = psum_t("pB0", [128, 512])
        pB1 = psum_t("pB1", [128, 512])
        pC0 = psum_t("pC0", [128, 512])
        pC1 = psum_t("pC1", [128, 512])
        p1a = psum_t("p1a", [128, 512])
        p1b = psum_t("p1b", [128, 512])
        PA, PB, PC = (pA0, pA1), (pB0, pB1), (pC0, pC1)
        PD = (p1a, p1b)

        C_bg2, C_bres, C_crl, C_lt, C_rsp, C_b2r, C_bog2, C_gate = 0, 1, 2, 3, 4, 5, 6, 7
        C_stat, C_tmp, C_el2, C_eln, C_i256 = 8, 9, 10, 11, 32
        C_bog1, C_bg1, C_nw = 12, 16, 24

        R_scr = 0
        R_beta, R_coef, R_rq = 0, 4, 0
        R_gpark, R_ogpark = 32, 64
        R_logit, R_og, R_rrms = 0, 0, 4

        T_beta, T_pls, T_rq, T_p, T_og, T_rrms, T_sc, T_tmp = 0, 4, 8, 12, 16, 20, 24, 28

        SV_X, SV_X2, SV_AB, SV_RQ = 0, 4, 8, 12
        def sv(i, hh):
            j = 32 * (i + hh)
            return onesv[:, j:j + 32]

        W_wq, W_wk, W_wv = 0, 8192, 16384
        W_diag = 24576
        W_small = 29696
        W_tpl, W_tpl_n = 30208, 6144
        W_wg1, W_wg2 = 0, 9216
        W_wog1 = 9280
        W_wog2 = 13888
        W_wo = 13920
        W_vT, W_uT, W_fbT = 22112, 22240, 26336
        W_tern, W_pstr = 0, 32768

        B_xfm = 0
        B_pre = 16384
        B_vfm = 20488
        B_lsld = 36904
        B_v32 = 0
        B_lltmp = 20488
        B_llst = 28680
        S_kn, S_kbn, S_q = 0, 4096, 8192
        S_kntm, S_vb = 12288, 14336
        S_E, S_aT, S_dout = 16384, 17408, 18432
        B_g1 = 0
        B_gg = 16384
        B_mix = 32768
        B_otm = 38912
        B_g2 = 39936
        B_onfm = 16384
        B_x2fm = 0
        B_r2 = 16384
        B_fq = 32768

        F_acc, F_ln, F_sc, F_m = 0, 2048, 3072, 3136

        blk = es.enter_context(nc.Block())
        aF = lambda off, ln: arenaF[:, off:off + ln]

        def wait_all_sy(e):
            for (s, c) in SY.sem_pool:
                if c > 0:
                    e.wait(Tok(s, c))

        def barrier():
            toks = [e.bump() for e in (TE, DV, AC, GP)]
            toks += [Tok(s, c) for ee in (GD, SY) for (s, c) in ee.sem_pool if c > 0]
            for e in (TE, DV, AC, GP, SY):
                for t in toks:
                    e.wait(t)
            GD.phase()
            SY.phase()

        def mini_sync(engs=(SY, DV, AC, GP, TE)):
            toks = []
            for e in engs:
                if e.sem_pool is not None:
                    toks += [Tok(s, c) for (s, c) in e.sem_pool if c > 0]
                else:
                    toks.append(e.tok())
            for e in engs:
                for t in toks:
                    e.wait(t)
            if SY in engs:
                SY.phase()

        def mm(out_ap, lhsT, rhs, start, stop, tp=None):
            if isinstance(out_ap, tuple):
                tokk = None
                for i, oa in enumerate(out_ap):
                    tokk = TE.do(lambda h, i=i, oa=oa: h.matmul(
                        oa[:, 0:512], lhsT, rhs[:, i * 512:(i + 1) * 512],
                        start=start, stop=stop, tile_position=tp))
                return tokk
            return TE.do(lambda h: h.matmul(out_ap, lhsT, rhs, start=start, stop=stop,
                                            tile_position=tp))

        def st_init():
            t_gp = GP.do(lambda h: h.memset(masks[:, 0:128], -1.0), inc=1)
            GP.wait(t_gp)
            GP.do(lambda h: h.affine_select(out=masks[:, 0:128], in_=masks[:, 0:128],
                                            compare_op=ALU.is_gt, fill=0.0, base=0,
                                            pattern=[[-1, 128]], channel_multiplier=1))
            t_gp = GP.do(lambda h: h.memset(masks[:, 128:256], -1.0), inc=1)
            GP.wait(t_gp)
            GP.do(lambda h: h.affine_select(out=masks[:, 128:256], in_=masks[:, 128:256],
                                            compare_op=ALU.is_gt, fill=0.0, base=0,
                                            pattern=[[1, 128]], channel_multiplier=-1))
            t_gp = GP.do(lambda h: h.memset(masks[:, 256:384], 1.0), inc=1)
            GP.wait(t_gp)
            GP.do(lambda h: h.affine_select(out=masks[:, 256:384], in_=masks[:, 256:384],
                                            compare_op=ALU.is_ge, fill=0.0, base=0,
                                            pattern=[[1, 128]], channel_multiplier=-1))
            for t_ in (ident_bf, ident_f):
                t_gp = GP.do(lambda h, t=t_: h.memset(t[:], 0.0), inc=1)
                GP.wait(t_gp)
                GP.do(lambda h, t=t_: h.affine_select(out=t[:], in_=t[:],
                                                      compare_op=ALU.not_equal, fill=1.0,
                                                      base=0, pattern=[[-1, 128]],
                                                      channel_multiplier=1))
            GP.do(lambda h: h.memset(stm[:], 0.0))
            GP.do(lambda h: h.memset(consts[:, C_el2:C_el2 + 1], EL2))
            GP.do(lambda h: h.memset(consts[0:4, C_bres:C_bres + 1], 0.0))
            GP.do(lambda h: h.memset(consts[0:4, C_crl:C_crl + 1], 0.0))
            GP.do(lambda h: h.memset(consts[:, C_eln:C_eln + 1], ELN))
            GP.do(lambda h: h.memset(consts[:, C_i256:C_i256 + 1], 1.0 / 256))
            GP.do(lambda h: h.memset(srows[R_gpark:R_gpark + 16, :], 0.0))
            GP.do(lambda h: h.memset(srows[R_ogpark:R_ogpark + 8, :], 0.0))
            GP.do(lambda h: h.memset(srows_bf[:], 0.0))
            GP.do(lambda h: h.memset(S_f32[:], 0.0))
            GP.do(lambda h: h.memset(S_bf[:], 0.0))
            for pp in range(2):
                GP.do(lambda h, pp=pp: h.memset(
                    arenaB[:, B_pre + pp * 2052:B_pre + pp * 2052 + 4], 0.0))
            for fc in range(NFM):
                GP.do(lambda h, fc=fc: h.memset(
                    arenaB[:, B_vfm + fc * 2052:B_vfm + fc * 2052 + 4], 0.0))
            GP.bump()

            SY.dma(lambda h: h.dma_start(out=consts[0:4, C_bg2:C_bg2 + 1], in_=bg2_h))
            SY.dma(lambda h: h.dma_start(out=consts[4:8, C_bres:C_bres + 1], in_=bres_h))
            SY.dma(lambda h: h.dma_start(out=consts[4:8, C_crl:C_crl + 1], in_=crl_h))
            SY.dma(lambda h: h.dma_start(out=consts[0:4, C_lt:C_lt + 1],
                                         in_=lt_h.broadcast_to([4, 1])))
            SY.dma(lambda h: h.dma_start(out=consts[0:1, C_bog2:C_bog2 + 1], in_=bog2_h))
            SY.dma(lambda h: h.dma_start(out=consts[0:16, C_gate:C_gate + 1],
                                         in_=gate_h.broadcast_to([16, 1])))
            SY.dma(lambda h: h.dma_start(out=consts[:, C_bog1:C_bog1 + 4], in_=bog1_h))
            SY.dma(lambda h: h.dma_start(out=consts[:, C_bg1:C_bg1 + 8], in_=bg1_h))
            SY.dma(lambda h: h.dma_start(out=consts[:, C_nw:C_nw + 8], in_=nw_h))

            t_xbf = GD.dma(lambda h: h.dma_start(out=h_xbf[:, :], in_=x_in))
            for Wh, off in ((Wq_h, W_wq), (Wk_h, W_wk), (Wv_h, W_wv)):
                GD.dma(lambda h, Wh=Wh, off=off: h.dma_start(
                    out=arenaW[:, off:off + 8192].rearrange("p (kc j) -> p kc j", kc=8),
                    in_=Wh.rearrange("(kc p) j -> p kc j", p=128)))
            GD.dma(lambda h: h.dma_start(
                out=arenaW[:, W_small:W_small + 32].rearrange("p (kc j) -> p kc j", kc=8),
                in_=Wbp_h.rearrange("(kc p) j -> p kc j", p=128)))
            GD.dma(lambda h: h.dma_start(
                out=arenaW[:, W_small + 32:W_small + 96].rearrange("p (kc j) -> p kc j", kc=8),
                in_=Wrp_h.rearrange("(kc p) j -> p kc j", p=128)))
            WdF = 30720 // 2
            GD.dma(lambda h, WdF=WdF: h.dma_start(
                out=arenaW[:, :].bitcast(F32)[:, WdF:WdF + 32], in_=tapq_h))
            GD.dma(lambda h, WdF=WdF: h.dma_start(
                out=arenaW[:, :].bitcast(F32)[:, WdF + 32:WdF + 64], in_=tapk_h))
            GD.dma(lambda h, WdF=WdF: h.dma_start(
                out=arenaW[:, :].bitcast(F32)[:, WdF + 64:WdF + 96], in_=tapv_h))
            GD.dma(lambda h, WdF=WdF: h.dma_start(
                out=arenaW[:, :].bitcast(F32)[:, WdF + 96:WdF + 136], in_=taps_h))
            GD.dma(lambda h: h.dma_start(out=onesv[:, :], in_=onesv_h))
            GD.dma(lambda h: h.dma_start(out=mixmx[:, :], in_=mixmx_h))
            SY.wait(t_xbf)
            for fc in range(NFM):
                SY.dma(lambda h, fc=fc: h.dma_start_transpose(
                    out=arenaB[:, B_xfm + fc * 2048:B_xfm + (fc + 1) * 2048],
                    in_=h_xbf[:, fc * 128:(fc + 1) * 128]))
            barrier()

        def proj_conv(w_off, tap_off, diag_hbm, Kt, out_writer):
            GD.phase()
            t_diag = GD.dma(lambda h, diag_hbm=diag_hbm: h.dma_start(
                out=arenaW[:, W_diag:W_diag + NFM * Kt * 128],
                in_=diag_hbm[:, 0:NFM * Kt * 128]))
            TE.wait(t_diag)
            tapsF = arenaW[:, :].bitcast(F32)
            evac_t = [None, None]
            conv_dv = {0: None, 1: None}
            conv_te = {0: None, 1: None}
            for fc in range(NFM):
                q_hook()
                pp = fc % 2
                pre = arenaB[:, B_pre + pp * 2052:B_pre + pp * 2052 + 2052]
                for th in range(2):
                    ps = PA if th == 0 else PB
                    TE.wait(evac_t[th])
                    for kc in range(NFM):
                        mm(ps,
                           arenaW[:, w_off + kc * 1024 + fc * 128:w_off + kc * 1024 + (fc + 1) * 128],
                           arenaB[:, B_xfm + kc * 2048 + th * 1024:B_xfm + kc * 2048 + (th + 1) * 1024],
                           start=(kc == 0), stop=(kc == NFM - 1))
                    t_mm = TE.bump()
                    AC.wait(t_mm)
                    AC.do(lambda h, ps=ps, pre=pre, th=th: h.copy(
                        out=pre[:, 4 + th * 1024:4 + th * 1024 + 512], in_=ps[0][:, 0:512]), inc=1)
                    evac_t[th] = AC.do(lambda h, ps=ps, pre=pre, th=th: h.copy(
                        out=pre[:, 4 + th * 1024 + 512:4 + (th + 1) * 1024], in_=ps[1][:, 0:512]), inc=1)
                scr = aF(F_acc + pp * 1024, 1024)
                DV.wait(evac_t[0])
                DV.wait(conv_dv[pp])
                for t in range(Kt):
                    base = 4 - (Kt - 1) + t
                    tap = tapsF[:, tap_off + fc * Kt + t:tap_off + fc * Kt + t + 1]
                    if t == 0:
                        DV.do(lambda h, pre=pre, base=base, tap=tap, scr=scr: h.tensor_scalar(
                            out=scr, in0=pre[:, base:base + 1024],
                            scalar1=tap, scalar2=None, op0=ALU.mult), inc=1)
                    else:
                        DV.do(lambda h, pre=pre, base=base, tap=tap, scr=scr: h.scalar_tensor_tensor(
                            out=scr, in0=pre[:, base:base + 1024], scalar=tap,
                            in1=scr, op0=ALU.mult, op1=ALU.add), inc=1)
                t_cv0 = DV.tok()
                AC.wait(t_cv0)
                conv_dv[pp] = out_writer(fc, 0, [(scr, 0, 1024)])
                pcp = PC if pp == 0 else PD
                TE.wait(evac_t[1])
                TE.wait(conv_te[pp])
                for t in range(Kt):
                    base = 4 + 1024 - (Kt - 1) + t
                    mm(pcp,
                       arenaW[:, W_diag + (fc * Kt + t) * 128:W_diag + (fc * Kt + t + 1) * 128],
                       pre[:, base:base + 1024],
                       start=(t == 0), stop=(t == Kt - 1))
                t_cv1 = TE.bump()
                AC.wait(t_cv1)
                conv_te[pp] = out_writer(fc, 1, [(pcp[0][:, 0:512], 0, 512),
                                                 (pcp[1][:, 0:512], 512, 512)])
            barrier()

        def stats_mix3(slab, slab2, park, n):
            prev = {0: None, 1: None}
            for q in range(4):
                pc = PC[q % 2]
                TE.wait(prev[q % 2])
                mm(pc[park:park + n, 0:512],
                   mixmx[0:128, slab:slab + n],
                   srows_bf[0:128, q * 512:(q + 1) * 512], True, True, tp=(0, park))
                t_park = TE.bump()
                DV.wait(t_park)
                prev[q % 2] = DV.do(lambda h, q=q, pc=pc: h.tensor_add(
                    out=srows[park:park + n, q * 512:(q + 1) * 512],
                    in0=srows[park:park + n, q * 512:(q + 1) * 512],
                    in1=pc[park:park + n, 0:512]), inc=1)
            TE.wait(prev[0])
            TE.wait(prev[1])

        def stats_mix_p2(slab2, park, n):
            prev = {0: None, 1: None}
            for q in range(4):
                pc = PC[q % 2]
                TE.wait(prev[q % 2])
                mm(pc[park:park + n, 0:512],
                   mixmx[0:4, slab2:slab2 + n],
                   srows_bf[0:4, q * 512:(q + 1) * 512], True, True, tp=(0, park))
                t_park = TE.bump()
                DV.wait(t_park)
                prev[q % 2] = DV.do(lambda h, q=q, pc=pc: h.tensor_add(
                    out=srows[park:park + n, q * 512:(q + 1) * 512],
                    in0=srows[park:park + n, q * 512:(q + 1) * 512],
                    in1=pc[park:park + n, 0:512]), inc=1)
            TE.wait(prev[0])
            TE.wait(prev[1])

        def stats_fm2(get_chunk, slab, og_slab, rrms, pres):
            bank_first = {0: True, 1: True, 2: True, 3: True}
            bank_first_d = dict(bank_first)
            t_te_prev = {0: None, 1: None}
            for fc in range(NFM):
                pg = fc % 2 if len(pres) > 1 else 0
                pre0, pre1 = pres[pg]
                src_, t_src = get_chunk(fc)
                DV.wait(t_src)
                DV.wait(t_te_prev[pg])
                t_sq = DV.do(lambda h, s=src_, d=pre0: h.tensor_mul(out=d, in0=s, in1=s), inc=1)
                AC.wait(t_src)
                AC.wait(t_te_prev[pg])
                t_ab = AC.do(lambda h, s=src_, d=pre1: h.activation(out=d, in_=s, func=AF.Abs), inc=1)
                hh = fc // 2
                for sv_i, data, t_d, dup in ((SV_X, src_, t_src, False), (SV_X2, pre0, t_sq, True),
                                             (SV_AB, pre1, t_ab, False)):
                    TE.wait(t_d)
                    for q in range(4):
                        ps = (pA0, pA1, pB0, pB1)[q]
                        st = bank_first[q]
                        bank_first[q] = False
                        last = (fc == NFM - 1 and sv_i == SV_AB)
                        mm(ps[0:32, 0:512],
                           sv(sv_i, hh), data[:, q * 512:(q + 1) * 512],
                           start=st, stop=last)
                        if dup and f_sub >= 2:
                            psd = (pC0, pC1, p1a, p1b)[q]
                            std = bank_first_d[q]
                            bank_first_d[q] = False
                            mm(psd[96:128, 0:512],
                               sv(SV_RQ, hh), data[:, q * 512:(q + 1) * 512],
                               start=std, stop=(fc == NFM - 1), tp=(0, 96))
                t_te_prev[pg] = TE.bump()
            t_all = TE.tok()
            AC.wait(t_all)
            for q in range(4):
                psq = (pA0, pA1, pB0, pB1)[q]
                psd = (pC0, pC1, p1a, p1b)[q]
                AC.do(lambda h, q=q, psq=psq: h.copy(
                    out=srows[0:12, q * 512:(q + 1) * 512], in_=psq[0:12, 0:512]), inc=1)
                if f_sub >= 2:
                    AC.do(lambda h, q=q, psd=psd: h.copy(
                        out=srows[96:100, q * 512:(q + 1) * 512], in_=psd[96:100, 0:512]), inc=1)
            t_e = AC.tok()
            AC.wait(t_e)
            if f_sub >= 2:
                t_l2 = AC.do(lambda h: h.activation(out=srows[96:100, :], in_=srows[96:100, :],
                                                    func=AF.Sqrt), inc=1)
                DV.wait(t_l2)
            else:
                DV.wait(AC.tok())
            DV.do(lambda h: h.tensor_copy(out=srows_bf[0:12, :], in_=srows[0:12, :]), inc=1)
            t_c = DV.do(lambda h: h.tensor_copy(out=srows_bf[96:100, :], in_=srows[96:100, :]), inc=1)
            TE.wait(t_c)
            if f_sub < 3:
                barrier()
                return
            if slab is not None:
                stats_mix3(slab[0], slab[1], R_gpark, 16)
            if og_slab is not None:
                stats_mix3(og_slab[0], og_slab[1], R_ogpark, 8)
            AC.wait(DV.tok())
            t_m2 = AC.do(lambda h: h.activation(out=srows[0:4, :], in_=srows[0:4, :],
                                                func=AF.Square,
                                                scale=consts[0:4, C_i256:C_i256 + 1]), inc=1)
            DV.wait(t_m2)
            t_c2 = DV.do(lambda h: h.tensor_copy(out=srows_bf[0:4, :], in_=srows[0:4, :]), inc=1)
            TE.wait(t_c2)
            if f_sub < 4:
                barrier()
                return
            if slab is not None:
                stats_mix_p2(slab[1], R_gpark, 16)
            if og_slab is not None:
                stats_mix_p2(og_slab[1], R_ogpark, 8)
            if rrms:
                AC.wait(DV.tok())
                t_sr = AC.do(lambda h: h.activation(out=srows[0:8, :], in_=srows[0:8, :],
                                                    func=AF.Copy, bias=0.0,
                                                    scale=consts[0:8, C_i256:C_i256 + 1]), inc=1)
                DV.wait(t_sr)
            barrier()

        def small_rt(row0, nrows, slot0):
            prev = {0: None, 1: None}
            for t in range(NTM):
                pb = p1b if t % 2 == 0 else p1a
                TE.wait(prev[t % 2])
                t_tr = TE.do(lambda h, t=t, pb=pb: h.transpose(
                    pb[:, 0:nrows], srows[row0:row0 + nrows, t * 128:(t + 1) * 128],
                    ident_f[0:nrows, 0:nrows]), inc=1)
                AC.wait(t_tr)
                prev[t % 2] = AC.do(lambda h, t=t, pb=pb: h.copy(
                    out=stm[:, t * 32 + slot0:t * 32 + slot0 + nrows], in_=pb[:, 0:nrows]), inc=1)
            barrier()

        def stage_A_v():
            def writer(fc, th, segs):
                tk = None
                off = B_vfm + fc * 2052 + 4 + th * 1024
                for ap, o, n in segs:
                    tk = AC.do(lambda h, ap=ap, o=o, n=n, off=off: h.activation(
                        out=arenaB[:, off + o:off + o + n], in_=ap, func=AF.Silu), inc=1)
                return tk
            proj_conv(W_wv, 30720 // 2 + 64, dvv_h, 4, writer)
            for q in range(4):
                for kc in range(NFM):
                    mm(p1a[0:8, 0:512],
                       arenaW[:, W_small + 32 + kc * 8:W_small + 32 + (kc + 1) * 8],
                       arenaB[:, B_xfm + kc * 2048 + q * 512:B_xfm + kc * 2048 + (q + 1) * 512],
                       start=(kc == 0), stop=(kc == NFM - 1))
                t3 = TE.bump()
                AC.wait(t3)
                t4 = AC.do(lambda h, q=q: h.activation(
                    out=srows[0:8, q * 512:(q + 1) * 512], in_=p1a[0:8, 0:512],
                    func=AF.Sigmoid, bias=consts[0:8, C_bres:C_bres + 1]), inc=1)
                TE.wait(t4)
            AC.do(lambda h: h.activation(out=consts[0:8, C_stat:C_stat + 1],
                                         in_=consts[0:8, C_crl:C_crl + 1], func=AF.Sigmoid), inc=1)
            t5 = AC.tok()
            DV.wait(t5)
            DV.do(lambda h: h.tensor_scalar(out=srows[0:8, :], in0=srows[0:8, :],
                                            scalar1=consts[0:8, C_stat:C_stat + 1], scalar2=None,
                                            op0=ALU.mult), inc=1)
            t6 = DV.tok()
            TE.wait(t6)
            AC.wait(t6)
            for q in range(4):
                for kc in range(NFM):
                    mm(p1a[0:4, 0:512], arenaW[:, W_small + kc * 4:W_small + (kc + 1) * 4],
                       arenaB[:, B_xfm + kc * 2048 + q * 512:B_xfm + kc * 2048 + (q + 1) * 512],
                       start=(kc == 0), stop=(kc == NFM - 1))
                t1 = TE.bump()
                AC.wait(t1)
                t2 = AC.do(lambda h, q=q: h.activation(
                    out=srows[0:4, q * 512:(q + 1) * 512], in_=p1a[0:4, 0:512],
                    func=AF.Sigmoid), inc=1)
                TE.wait(t2)
            barrier()

        def stage_A_k():
            st_tok = {}
            def writer(fc, th, segs):
                base = (fc % 2) * 2048
                if th == 0:
                    AC.wait(st_tok.get(fc - 2))
                t = None
                for ap, o, n in segs:
                    t = AC.do(lambda h, th=th, base=base, ap=ap, o=o, n=n: h.activation(
                        out=wsm[:, base + th * 1024 + o:base + th * 1024 + o + n],
                        in_=ap, func=AF.Silu), inc=1)
                if th == 1:
                    SY.phase()
                    SY.wait(t)
                    st_tok[fc] = SY.dma(lambda h, fc=fc, base=base: h.dma_start(
                        out=h_kfm[fc * 128:(fc + 1) * 128, :], in_=wsm[:, base:base + 2048]))
                return t
            proj_conv(W_wk, 30720 // 2 + 32, dk_h, 4, writer)

        def stage_A_q():
            st_tok = {}
            def writer(fc, th, segs):
                base = (fc % 2) * 2048
                if th == 0:
                    AC.wait(st_tok.get(fc - 2))
                t = None
                for ap, o, n in segs:
                    t = AC.do(lambda h, th=th, base=base, ap=ap, o=o, n=n: h.activation(
                        out=wsm[:, base + th * 1024 + o:base + th * 1024 + o + n],
                        in_=ap, func=AF.Silu), inc=1)
                if th == 1:
                    SY.phase()
                    SY.wait(t)
                    st_tok[fc] = SY.dma(lambda h, fc=fc, base=base: h.dma_start(
                        out=h_qfm[fc * 128:(fc + 1) * 128, :], in_=wsm[:, base:base + 2048]))
                return t
            proj_conv(W_wq, 30720 // 2, dq_h, 4, writer)
            bank_first = {0: True, 1: True, 2: True, 3: True}
            mm_done = {0: None, 1: None}
            for fc in range(NFM):
                par = fc % 2
                slot = wsm[:, par * 2048:(par + 1) * 2048]
                SY.phase()
                SY.wait(mm_done[par])
                tok = SY.dma(lambda h, fc=fc, slot=slot: h.dma_start(
                    out=slot, in_=h_qfm[fc * 128:(fc + 1) * 128, :]))
                AC.wait(tok)
                t_sq = AC.do(lambda h, slot=slot: h.activation(out=slot, in_=slot,
                                                               func=AF.Square), inc=1)
                TE.wait(t_sq)
                hh = fc // 2
                for q in range(4):
                    ps = (pA0, pA1, pB0, pB1)[q]
                    st = bank_first[q]
                    bank_first[q] = False
                    mm(ps[0:32, 0:512],
                       sv(SV_RQ, hh), slot[:, q * 512:(q + 1) * 512],
                       start=st, stop=(fc == NFM - 1))
                mm_done[par] = TE.bump()
            t_all = TE.bump()
            AC.wait(t_all)
            for q in range(4):
                psq = (pA0, pA1, pB0, pB1)[q]
                t_q2 = AC.do(lambda h, q=q, psq=psq: h.activation(
                    out=srows[R_rq:R_rq + 4, q * 512:(q + 1) * 512], in_=psq[0:4, 0:512],
                    func=AF.Sqrt, bias=consts[0:4, C_el2:C_el2 + 1]), inc=1)
            DV.wait(t_q2)
            DV.do(lambda h: h.reciprocal(out=srows[R_rq:R_rq + 4, :], in_=srows[R_rq:R_rq + 4, :]), inc=1)
            barrier()

        def stage_D():
            pre0 = arenaB[:, B_pre:B_pre + 2048]
            pre1 = arenaB[:, B_pre + 2052:B_pre + 2052 + 2048]
            presD = [(pre0, pre1), (arenaB[:, 0:2048], arenaB[:, 2048:4096])]
            stats_fm2(lambda fc: (arenaB[:, B_vfm + fc * 2052 + 4:B_vfm + fc * 2052 + 4 + 2048], None),
                      (48, 96 + 48), (72, 168), False, presD)
            if d_sub <= 1:
                return
            GD.phase()
            t_diag = GD.dma(lambda h: h.dma_start(
                out=arenaW[:, W_diag:W_diag + NFM * 5 * 128], in_=dss_h[:, :]))
            TE.wait(t_diag)
            tapsF = arenaW[:, :].bitcast(F32)
            TS_OFF = 30720 // 2 + 96
            cv_dv = {0: None, 1: None}
            cv_te = {0: None, 1: None}
            st1 = {0: None, 1: None}
            for fc in range(NFM):
                pp = fc % 2
                scr = aF(F_acc + pp * 1024, 1024)
                for t in range(5):
                    base = B_vfm + fc * 2052 + t
                    tap = tapsF[:, TS_OFF + fc * 5 + t:TS_OFF + fc * 5 + t + 1]
                    if t == 0:
                        DV.do(lambda h, base=base, tap=tap, scr=scr: h.tensor_scalar(
                            out=scr, in0=arenaB[:, base:base + 1024],
                            scalar1=tap, scalar2=None, op0=ALU.mult), inc=1)
                    else:
                        DV.do(lambda h, base=base, tap=tap, scr=scr: h.scalar_tensor_tensor(
                            out=scr, in0=arenaB[:, base:base + 1024], scalar=tap,
                            in1=scr, op0=ALU.mult, op1=ALU.add), inc=1)
                DV.wait(cv_dv[pp])
                DV.do(lambda h, pp=pp, scr=scr: h.tensor_copy(
                    out=wsm[:, 2048 + pp * 1024:2048 + (pp + 1) * 1024], in_=scr), inc=1)
                t0c = DV.tok()
                SY.phase()
                SY.wait(t0c)
                cv_dv[pp] = SY.dma(lambda h, fc=fc, pp=pp: h.dma_start(
                    out=h_lsfm[fc * 128:(fc + 1) * 128, 0:1024],
                    in_=wsm[:, 2048 + pp * 1024:2048 + (pp + 1) * 1024]))
                pcp = PC if pp == 0 else PD
                TE.wait(cv_te[pp])
                for t in range(5):
                    base = B_vfm + fc * 2052 + 1024 + t
                    mm(pcp,
                       arenaW[:, W_diag + (fc * 5 + t) * 128:W_diag + (fc * 5 + t + 1) * 128],
                       arenaB[:, base:base + 1024],
                       start=(t == 0), stop=(t == 4))
                tmm = TE.bump()
                AC.wait(tmm)
                AC.wait(st1[pp])
                AC.do(lambda h, pp=pp: h.copy(
                    out=wsm[:, pp * 1024:pp * 1024 + 512], in_=pC0[:, 0:512]
                    if pp == 0 else p1a[:, 0:512]), inc=1)
                tev = AC.do(lambda h, pp=pp: h.copy(
                    out=wsm[:, pp * 1024 + 512:(pp + 1) * 1024], in_=pC1[:, 0:512]
                    if pp == 0 else p1b[:, 0:512]), inc=1)
                cv_te[pp] = tev
                SY.phase()
                SY.wait(tev)
                st1[pp] = SY.dma(lambda h, fc=fc, pp=pp: h.dma_start(
                    out=h_lsfm[fc * 128:(fc + 1) * 128, 1024:2048],
                    in_=wsm[:, pp * 1024:(pp + 1) * 1024]))
            barrier()
            if d_sub <= 2:
                return
            def get_ls(fc):
                pg = fc % 2
                SY.phase()
                SY.wait(TE.tok())
                tok = SY.dma(lambda h, fc=fc, pg=pg: h.dma_start(
                    out=arenaB[:, B_lsld + pg * 2048:B_lsld + pg * 2048 + 2048],
                    in_=h_lsfm[fc * 128:(fc + 1) * 128, :]))
                return arenaB[:, B_lsld + pg * 2048:B_lsld + pg * 2048 + 2048], tok
            stats_fm2(get_ls, (0, 96), None, False, presD)
            if d_sub <= 3:
                return
            for fc in range(NFM):
                SY.dma(lambda h, fc=fc: h.dma_start(
                    out=h_vfm[fc * 128:(fc + 1) * 128, :],
                    in_=arenaB[:, B_vfm + fc * 2052 + 4:B_vfm + fc * 2052 + 4 + 2048]))
            barrier()
            if d_sub <= 4:
                return
            wait_all_sy(SY)
            st_v = {0: None, 1: None}
            st_vb = {0: None, 1: None}
            vb_dv = {0: None, 1: None}
            for t in range(NTM):
                par = t % 2
                vsl = wsm[:, par * 2048:par * 2048 + 1024]
                vbs = wsm[:, par * 2048 + 1024:par * 2048 + 2048]
                SY.phase()
                SY.wait(st_v[par])
                SY.wait(vb_dv[par])
                tok = SY.dma(lambda h, t=t, vsl=vsl: h.dma_start_transpose(
                    out=vsl, in_=h_vfm[:, t * 128:(t + 1) * 128]))
                SY.phase()
                SY.wait(tok)
                st_v[par] = SY.dma(lambda h, t=t, vsl=vsl: h.dma_start(
                    out=h_vtm[t * 128:(t + 1) * 128, :], in_=vsl))
                DV.wait(tok)
                DV.wait(st_vb[par])
                for hh in range(H):
                    DV.do(lambda h, t=t, hh=hh, vsl=vsl, vbs=vbs: h.tensor_scalar(
                        out=vbs[:, hh * 256:(hh + 1) * 256],
                        in0=vsl[:, hh * 256:(hh + 1) * 256],
                        scalar1=stm[:, t * 32 + T_beta + hh:t * 32 + T_beta + hh + 1],
                        scalar2=None, op0=ALU.mult), inc=1)
                t_vb = DV.tok()
                vb_dv[par] = t_vb
                SY.phase()
                SY.wait(t_vb)
                st_vb[par] = SY.dma(lambda h, t=t, vbs=vbs: h.dma_start(
                    out=h_vbtm[t * 128:(t + 1) * 128, :], in_=vbs))
            barrier()
            if d_sub <= 5:
                return
            t_z = GP.do(lambda h: h.memset(arenaB[:, B_v32:B_v32 + 512], 0.0), inc=1)
            TE.wait(t_z)
            GD.phase()
            t_v32 = None
            for g in range(4):
                t_v32 = GD.dma(lambda h, g=g: h.dma_start(
                    out=arenaB[:, B_v32 + 512:B_v32 + 512 + 64 * 256]
                        .rearrange("p (b e) -> p b e", b=64)[32 * g:32 * (g + 1)],
                    in_=h_vtm.rearrange("(b r) (g e) -> g r b e", r=32, e=256)[g]))
            TE.wait(t_v32)
            if d_sub <= 6:
                return
            v32r = arenaB[:, B_v32:B_v32 + 66 * 256].rearrange("p (b e) -> p b e", e=256)
            lltv = arenaB[:, B_lltmp:B_lltmp + 8192].rearrange("p (b e) -> p b e", e=128)
            llta = arenaB[:, B_lltmp:B_lltmp + 8192].rearrange("p (t a e) -> p a t e", a=4, e=128)
            t_tpl = {}
            GD.phase()
            t_tpl[0] = GD.dma(lambda h: h.dma_start(
                out=arenaW[:, 0:W_tpl_n],
                in_=tpl_h.rearrange("p (e x) -> p e x", e=256)[:, 0:64]))
            for eb in range(4):
                W_tplb = (eb % 2) * W_tpl_n
                if eb + 1 < 4:
                    GD.phase()
                    GP.wait(TE.tok())
                    t_tpl[eb + 1] = GD.dma(lambda h, eb=eb: h.dma_start(
                        out=arenaW[:, ((eb + 1) % 2) * W_tpl_n:((eb + 1) % 2) * W_tpl_n + W_tpl_n],
                        in_=tpl_h.rearrange("p (e x) -> p e x", e=256)[:, (eb + 1) * 64:(eb + 2) * 64]))
                TE.wait(t_tpl[eb])
                for e8 in range(8):
                    for ee in range(8):
                        e_loc = e8 * 8 + ee
                        e_glob = eb * 64 + e_loc
                        for g in range(4):
                            psg = (pC0, pC1, p1a, p1b)[g]
                            for dd in range(3):
                                first = (ee == 0 and dd == 0)
                                mm(psg[32 * g:32 * (g + 1), ee * 64:(ee + 1) * 64],
                                   arenaW[32 * g:32 * (g + 1),
                                          W_tplb + (e_loc * 3 + dd) * 32:W_tplb + (e_loc * 3 + dd + 1) * 32],
                                   v32r[32 * g:32 * (g + 1), (2 - dd):(2 - dd) + 64,
                                        e_glob:e_glob + 1],
                                   start=first, stop=(ee == 7 and dd == 2),
                                   tp=(32 * g, 32 * g))
                    t_g = TE.bump()
                    AC.wait(t_g)
                    DV.wait(t_g)
                    ebase = (eb * 64 + e8 * 8) % 128
                    t_eva = t_evd = None
                    for g in range(4):
                        psg = (pC0, pC1, p1a, p1b)[g]
                        if g % 2 == 0:
                            t_eva = AC.do(lambda h, ebase=ebase, g=g, psg=psg: h.copy(
                                out=lltv[32 * g:32 * (g + 1), :, ebase:ebase + 8].transpose([0, 2, 1]),
                                in_=psg[32 * g:32 * (g + 1), 0:512].rearrange("p (s b) -> p s b", s=8)), inc=1)
                        else:
                            t_evd = DV.do(lambda h, ebase=ebase, g=g, psg=psg: h.tensor_copy(
                                out=lltv[32 * g:32 * (g + 1), :, ebase:ebase + 8].transpose([0, 2, 1]),
                                in_=psg[32 * g:32 * (g + 1), 0:512].rearrange("p (s b) -> p s b", s=8)), inc=1)
                    TE.wait(t_eva)
                    TE.wait(t_evd)
                if eb % 2 == 1:
                    GP.wait(AC.tok())
                    GP.wait(DV.tok())
                    ehalf = eb // 2
                    GD.phase()
                    t_asm = None
                    for g in range(4):
                        for a in range(4):
                            t_asm = GD.dma(lambda h, g=g, a=a, ehalf=ehalf: h.dma_start(
                                out=h_ll.rearrange("(t a r) (g e) -> g a r t e", a=4, r=32, e=256)
                                    [g, a][:, :, ehalf * 128:(ehalf + 1) * 128],
                                in_=llta[32 * g:32 * (g + 1), a, :, :]))
                    AC.wait(t_asm)
                    DV.wait(t_asm)
            barrier()
            if d_sub <= 7:
                return
            def get_ll(fc):
                pg = fc % 2
                SY.phase()
                SY.wait(TE.tok())
                tok = SY.dma(lambda h, fc=fc, pg=pg: h.dma_start_transpose(
                    out=arenaB[:, B_llst + pg * 2048:B_llst + pg * 2048 + 2048],
                    in_=h_ll[:, fc * 128:(fc + 1) * 128]))
                return arenaB[:, B_llst + pg * 2048:B_llst + pg * 2048 + 2048], tok
            presL = [(arenaB[:, B_llst + 4096:B_llst + 6144],
                      arenaB[:, B_llst + 6144:B_llst + 8192]),
                     (arenaB[:, B_llst + 8192:B_llst + 10240],
                      arenaB[:, B_llst + 10240:B_llst + 12288])]
            stats_fm2(get_ll, (16, 96 + 16), None, False, presL)

        def stage_B_k():
            st_kn = {0: None, 1: None}
            st_kbn = {0: None, 1: None}
            red_tok = {0: None, 1: None}
            for t in range(NTM):
                par = t % 2
                kraw = wsm[:, par * 2048:par * 2048 + 1024]
                kn_s = wsm[:, par * 2048 + 1024:par * 2048 + 2048]
                mbs = F_m + 0 if par == 0 else F_sc + 48
                SY.phase()
                SY.wait(st_kbn[par])
                tok = SY.dma(lambda h, t=t, kraw=kraw: h.dma_start_transpose(
                    out=kraw, in_=h_kfm[:, t * 128:(t + 1) * 128]))
                AC.wait(tok)
                AC.wait(red_tok[par])
                t_sq = AC.do(lambda h, kraw=kraw, par=par: h.activation(
                    out=aF(F_acc + par * 1024, 1024), in_=kraw, func=AF.Square), inc=1)
                DV.wait(t_sq)
                t_rd = DV.do(lambda h, par=par, mbs=mbs: h.tensor_reduce(
                    out=aF(mbs, 4),
                    in_=aF(F_acc + par * 1024, 1024).rearrange("p (hh e) -> p hh e", hh=4),
                    axis=AX.X, op=ALU.add), inc=1)
                red_tok[par] = t_rd
                AC.wait(t_rd)
                t_rk = AC.do(lambda h, mbs=mbs: h.activation(out=aF(mbs + 4, 4), in_=aF(mbs, 4),
                                                             func=AF.Sqrt,
                                                             bias=consts[:, C_el2:C_el2 + 1]), inc=1)
                DV.wait(t_rk)
                DV.do(lambda h, mbs=mbs: h.reciprocal(out=aF(mbs + 4, 4), in_=aF(mbs + 4, 4)), inc=1)
                DV.wait(st_kn[par])
                for hh in range(H):
                    DV.do(lambda h, hh=hh, kraw=kraw, kn_s=kn_s, mbs=mbs: h.tensor_scalar(
                        out=kn_s[:, hh * 256:(hh + 1) * 256],
                        in0=kraw[:, hh * 256:(hh + 1) * 256],
                        scalar1=aF(mbs + 4 + hh, 1), scalar2=None, op0=ALU.mult), inc=1)
                t_kn = DV.tok()
                SY.phase()
                SY.wait(t_kn)
                st_kn[par] = SY.dma(lambda h, t=t, kn_s=kn_s: h.dma_start(
                    out=h_kntm[t * 128:(t + 1) * 128, :], in_=kn_s))
                for hh in range(H):
                    DV.do(lambda h, t=t, hh=hh, kraw=kraw, kn_s=kn_s: h.tensor_scalar(
                        out=kraw[:, hh * 256:(hh + 1) * 256],
                        in0=kn_s[:, hh * 256:(hh + 1) * 256],
                        scalar1=stm[:, t * 32 + T_beta + hh:t * 32 + T_beta + hh + 1],
                        scalar2=None, op0=ALU.mult), inc=1)
                t_kbn = DV.tok()
                SY.phase()
                SY.wait(t_kbn)
                st_kbn[par] = SY.dma(lambda h, t=t, kraw=kraw: h.dma_start(
                    out=h_kbntm[t * 128:(t + 1) * 128, :], in_=kraw))
            barrier()
            st = {0: None, 1: None}
            for (tsrc, tdst) in ((h_kntm, h_knfm), (h_kbntm, h_kbnfm)):
                for fc in range(NFM):
                    par = fc % 2
                    SY.phase()
                    SY.wait(st[par])
                    t1 = SY.dma(lambda h, fc=fc, par=par, tsrc=tsrc: h.dma_start_transpose(
                        out=wsm[:, par * 2048:(par + 1) * 2048],
                        in_=tsrc[:, fc * 128:(fc + 1) * 128]))
                    SY.phase()
                    SY.wait(t1)
                    st[par] = SY.dma(lambda h, fc=fc, par=par, tdst=tdst: h.dma_start(
                        out=tdst[fc * 128:(fc + 1) * 128, :],
                        in_=wsm[:, par * 2048:(par + 1) * 2048]))
            barrier()

        def stage_C_pre():
            ev_t = {hh: None for hh in range(H)}
            est_tok = {}
            te_a_done = {}
            for c in range(NCH):
                pp = c % 2
                SY.phase()
                SY.wait(te_a_done.get(c - 2, None) if c >= 2 else None)
                t_kn = SY.dma(lambda h, c=c, pp=pp: h.dma_start(
                    out=arenaB[:, S_kn + pp * 1024:S_kn + pp * 1024 + 1024]
                        .rearrange("p (kc t) -> p kc t", kc=8),
                    in_=h_knfm.rearrange("(kc p) t -> p kc t", p=128)[:, :, c * 128:(c + 1) * 128]))
                t_kbn = SY.dma(lambda h, c=c, pp=pp: h.dma_start(
                    out=arenaB[:, S_kbn + pp * 1024:S_kbn + pp * 1024 + 1024]
                        .rearrange("p (kc t) -> p kc t", kc=8),
                    in_=h_kbnfm.rearrange("(kc p) t -> p kc t", p=128)[:, :, c * 128:(c + 1) * 128]))
                t_q = SY.dma(lambda h, c=c, pp=pp: h.dma_start(
                    out=arenaB[:, S_q + pp * 1024:S_q + pp * 1024 + 1024]
                        .rearrange("p (kc t) -> p kc t", kc=8),
                    in_=h_qfm.rearrange("(kc p) t -> p kc t", p=128)[:, :, c * 128:(c + 1) * 128]))
                knb = lambda kc: arenaB[:, S_kn + pp * 1024 + kc * 128:S_kn + pp * 1024 + (kc + 1) * 128]
                kbnb = lambda kc: arenaB[:, S_kbn + pp * 1024 + kc * 128:S_kbn + pp * 1024 + (kc + 1) * 128]
                qb = lambda kc: arenaB[:, S_q + pp * 1024 + kc * 128:S_q + pp * 1024 + (kc + 1) * 128]
                TE.wait(t_kn); TE.wait(t_kbn); TE.wait(t_q)
                aT_dma = {}
                for hh in range(H):
                    ps = (pA0, pA1, pB0, pB1)[hh]
                    TE.wait(ev_t[hh])
                    mm(ps[:, 0:128], kbnb(2 * hh), knb(2 * hh), True, False)
                    mm(ps[:, 0:128], kbnb(2 * hh + 1), knb(2 * hh + 1), False, False)
                    mm(ps[:, 128:256], knb(2 * hh), kbnb(2 * hh), False, False)
                    mm(ps[:, 128:256], knb(2 * hh + 1), kbnb(2 * hh + 1), False, False)
                    mm(ps[:, 256:384], knb(2 * hh), qb(2 * hh), False, False)
                    mm(ps[:, 256:384], knb(2 * hh + 1), qb(2 * hh + 1), False, True)
                    t_a = TE.bump()
                    DV.wait(t_a)
                    wb = hh * 1024
                    t_m = DV.do(lambda h, ps=ps, wb=wb: h.tensor_tensor(
                        out=wsm[:, wb:wb + 384], in0=ps[:, 0:384],
                        in1=masks[:, 0:384], op=ALU.mult), inc=1)
                    SY.wait(t_m)
                    inst = c * H + hh
                    aT_dma[hh] = SY.dma(lambda h, inst=inst, wb=wb: h.dma_start(
                        out=h_aT[inst * 128:(inst + 1) * 128, :], in_=wsm[:, wb + 256:wb + 384]))
                    ev_t[hh] = DV.do(lambda h, wb=wb: h.tensor_add(
                        out=wsm[:, wb + 512:wb + 640], in0=ident_bf[:],
                        in1=wsm[:, wb + 128:wb + 256]), inc=1)
                te_a_done[c] = TE.tok()
                mp = {hh: hh * 1024 + 0 for hh in range(H)}
                qp = {hh: hh * 1024 + 128 for hh in range(H)}
                rr = {hh: hh * 1024 + 512 for hh in range(H)}
                alt = {hh: hh * 1024 + 640 for hh in range(H)}
                for s in range(1, 7):
                    for hh in range(H):
                        ps = (pA0, pA1, pB0, pB1)[hh]
                        ENG = AC if hh % 2 == 0 else DV
                        TE.wait(ev_t[hh])
                        if s == 2:
                            TE.wait(aT_dma[hh])
                        mm(ps[:, 0:128], wsm[:, qp[hh]:qp[hh] + 128],
                           wsm[:, mp[hh]:mp[hh] + 128], True, False)
                        mm(ps[:, 128:256], wsm[:, mp[hh]:mp[hh] + 128],
                           wsm[:, qp[hh]:qp[hh] + 128], False, (s == 1))
                        if s >= 2:
                            mm(ps[:, 256:384], ident_bf[:], wsm[:, rr[hh]:rr[hh] + 128], False, False)
                            mm(ps[:, 256:384], wsm[:, mp[hh]:mp[hh] + 128],
                               wsm[:, rr[hh]:rr[hh] + 128], False, True)
                        t_lvl = TE.bump()
                        ENG.wait(t_lvl)
                        if s == 3:
                            ENG.wait(est_tok.get(hh))
                        nm = alt[hh]
                        if s >= 2:
                            t_ev = ENG.do(lambda h, ps=ps, nm=nm: (
                                h.copy(out=wsm[:, nm:nm + 384], in_=ps[:, 0:384])
                                if h is not DV.h else
                                h.tensor_copy(out=wsm[:, nm:nm + 384], in_=ps[:, 0:384])), inc=1)
                            mp[hh], qp[hh], rr[hh] = nm, nm + 128, nm + 256
                            alt[hh] = hh * 1024 + (640 if nm == hh * 1024 else 0)
                        else:
                            t_ev = ENG.do(lambda h, ps=ps, nm=nm: (
                                h.copy(out=wsm[:, nm:nm + 256], in_=ps[:, 0:256])
                                if h is not DV.h else
                                h.tensor_copy(out=wsm[:, nm:nm + 256], in_=ps[:, 0:256])), inc=1)
                            mp[hh], qp[hh] = nm, nm + 128
                            alt[hh] = hh * 1024 + 0
                        ev_t[hh] = t_ev
                for hh in range(H):
                    ps = (pA0, pA1, pB0, pB1)[hh]
                    ENG = AC if hh % 2 == 0 else DV
                    TE.wait(ev_t[hh])
                    mm(ps[:, 0:128], ident_bf[:], wsm[:, rr[hh]:rr[hh] + 128], True, False)
                    mm(ps[:, 0:128], wsm[:, mp[hh]:mp[hh] + 128],
                       wsm[:, rr[hh]:rr[hh] + 128], False, True)
                    t_e = TE.bump()
                    ENG.wait(t_e)
                    dst = hh * 1024 + 896
                    t_f = ENG.do(lambda h, ps=ps, dst=dst: (
                        h.copy(out=wsm[:, dst:dst + 128], in_=ps[:, 0:128])
                        if h is not DV.h else
                        h.tensor_copy(out=wsm[:, dst:dst + 128], in_=ps[:, 0:128])), inc=1)
                    ev_t[hh] = t_f
                    SY.phase()
                    SY.wait(t_f)
                    inst = c * H + hh
                    est_tok[hh] = SY.dma(lambda h, inst=inst, dst=dst: h.dma_start(
                        out=h_E[inst * 128:(inst + 1) * 128, :], in_=wsm[:, dst:dst + 128]))
            barrier()

        def stage_C_scan():
            dout_dma = {}
            for c in range(NCH):
                pp = c % 2
                SY.phase()
                t_kbn = SY.dma(lambda h, c=c, pp=pp: h.dma_start(
                    out=arenaB[:, S_kbn + pp * 1024:S_kbn + pp * 1024 + 1024]
                        .rearrange("p (kc t) -> p kc t", kc=8),
                    in_=h_kbnfm.rearrange("(kc p) t -> p kc t", p=128)[:, :, c * 128:(c + 1) * 128]))
                t_q = SY.dma(lambda h, c=c, pp=pp: h.dma_start(
                    out=arenaB[:, S_q + pp * 1024:S_q + pp * 1024 + 1024]
                        .rearrange("p (kc t) -> p kc t", kc=8),
                    in_=h_qfm.rearrange("(kc p) t -> p kc t", p=128)[:, :, c * 128:(c + 1) * 128]))
                t_kntm = SY.dma(lambda h, c=c, pp=pp: h.dma_start(
                    out=arenaB[:, S_kntm + pp * 1024:S_kntm + pp * 1024 + 1024],
                    in_=h_kntm[c * 128:(c + 1) * 128, :]))
                t_vb = SY.dma(lambda h, c=c, pp=pp: h.dma_start(
                    out=arenaB[:, S_vb + pp * 1024:S_vb + pp * 1024 + 1024],
                    in_=h_vbtm[c * 128:(c + 1) * 128, :]))
                t_E = SY.dma(lambda h, c=c, pp=pp: h.dma_start(
                    out=arenaB[:, S_E + pp * 512:S_E + pp * 512 + 512]
                        .rearrange("p (hh e) -> p hh e", hh=4),
                    in_=h_E[c * H * 128:(c + 1) * H * 128, :].rearrange("(hh p) e -> p hh e", p=128)))
                t_aT = SY.dma(lambda h, c=c, pp=pp: h.dma_start(
                    out=arenaB[:, S_aT + pp * 512:S_aT + pp * 512 + 512]
                        .rearrange("p (hh e) -> p hh e", hh=4),
                    in_=h_aT[c * H * 128:(c + 1) * H * 128, :].rearrange("(hh p) e -> p hh e", p=128)))
                kbnb = lambda kc: arenaB[:, S_kbn + pp * 1024 + kc * 128:S_kbn + pp * 1024 + (kc + 1) * 128]
                qb = lambda kc: arenaB[:, S_q + pp * 1024 + kc * 128:S_q + pp * 1024 + (kc + 1) * 128]
                kntm = arenaB[:, S_kntm + pp * 1024:S_kntm + pp * 1024 + 1024]
                vbt = arenaB[:, S_vb + pp * 1024:S_vb + pp * 1024 + 1024]
                Eb = lambda hh: arenaB[:, S_E + pp * 512 + hh * 128:S_E + pp * 512 + (hh + 1) * 128]
                aTb = lambda hh: arenaB[:, S_aT + pp * 512 + hh * 128:S_aT + pp * 512 + (hh + 1) * 128]
                dout = arenaB[:, S_dout + pp * 1024:S_dout + pp * 1024 + 1024]
                for t_ in (t_kbn, t_q, t_kntm, t_E, t_aT):
                    TE.wait(t_)
                DV.wait(t_vb)
                for hh in range(H):
                    ps = pA0 if hh < 2 else pB0
                    off = (hh % 2) * 256
                    mm(ps[:, off:off + 256], kbnb(2 * hh), S_bf[:, hh * 256:hh * 256 + 256],
                       start=(off == 0), stop=False)
                    mm(ps[:, off:off + 256], kbnb(2 * hh + 1),
                       S_bf[:, 1024 + hh * 256:1024 + hh * 256 + 256], False, (off != 0))
                t_g = TE.bump()
                DV.wait(t_g)
                for hh in range(H):
                    ps = pA0 if hh < 2 else pB0
                    off = (hh % 2) * 256
                    DV.do(lambda h, ps=ps, off=off, hh=hh: h.scalar_tensor_tensor(
                        out=wsm[:, hh * 256:(hh + 1) * 256], in0=ps[:, off:off + 256],
                        scalar=-1.0, in1=vbt[:, hh * 256:(hh + 1) * 256],
                        op0=ALU.mult, op1=ALU.add), inc=1)
                t_gev = DV.tok()
                TE.wait(t_gev)
                for hh in range(H):
                    pc = PC[hh // 2]
                    mm(pc[:, (hh % 2) * 256:(hh % 2) * 256 + 256], Eb(hh),
                       wsm[:, hh * 256:(hh + 1) * 256],
                       start=(hh % 2 == 0), stop=(hh % 2 == 1))
                t_ui = TE.bump()
                AC.wait(t_ui)
                AC.do(lambda h: h.copy(out=wsm[:, 1024:1536], in_=pC0[:, 0:512]), inc=1)
                t_uev = AC.do(lambda h: h.copy(out=wsm[:, 1536:2048], in_=pC1[:, 0:512]), inc=1)
                TE.wait(t_uev)
                for hh in range(H):
                    ps = pA0 if hh < 2 else pB0
                    off = (hh % 2) * 256
                    mm(ps[:, off:off + 256], qb(2 * hh), S_bf[:, hh * 256:hh * 256 + 256],
                       start=(off == 0), stop=False)
                    mm(ps[:, off:off + 256], qb(2 * hh + 1),
                       S_bf[:, 1024 + hh * 256:1024 + hh * 256 + 256], False, False)
                    mm(ps[:, off:off + 256], aTb(hh), wsm[:, 1024 + hh * 256:1024 + (hh + 1) * 256],
                       False, (off != 0))
                t_oi = TE.bump()
                AC.wait(t_oi)
                AC.wait(dout_dma.get(c - 2))
                for hh in range(H):
                    ps = pA0 if hh < 2 else pB0
                    off = (hh % 2) * 256
                    AC.do(lambda h, ps=ps, off=off, hh=hh, c=c: h.activation(
                        out=dout[:, hh * 256:(hh + 1) * 256], in_=ps[:, off:off + 256],
                        func=AF.Copy, scale=stm[:, c * 32 + T_rq + hh:c * 32 + T_rq + hh + 1]), inc=1)
                t_oev = AC.tok()
                SY.wait(t_oev)
                dout_dma[c] = SY.dma(lambda h, c=c, dout=dout: h.dma_start(
                    out=h_delta[c * 128:(c + 1) * 128, :], in_=dout[:]))
                prev_sup = {0: None, 1: None}
                for hh in range(H):
                    hp = hh % 2
                    pbank = p1a if hp == 0 else p1b
                    TE.wait(prev_sup[hp])
                    mm(pbank[:, 0:256], kntm[:, hh * 256:hh * 256 + 128],
                       wsm[:, 1024 + hh * 256:1024 + (hh + 1) * 256], True, False)
                    mm(pbank[:, 256:512], kntm[:, hh * 256 + 128:hh * 256 + 256],
                       wsm[:, 1024 + hh * 256:1024 + (hh + 1) * 256], False, True)
                    t_ds = TE.bump()
                    DV.wait(t_ds)
                    for half in range(2):
                        DV.do(lambda h, hh=hh, half=half, pbank=pbank: h.tensor_add(
                            out=S_f32[:, half * 1024 + hh * 256:half * 1024 + (hh + 1) * 256],
                            in0=S_f32[:, half * 1024 + hh * 256:half * 1024 + (hh + 1) * 256],
                            in1=pbank[:, half * 256:(half + 1) * 256]), inc=1)
                    t_sup = DV.tok()
                    prev_sup[hp] = t_sup
                    AC.wait(t_sup)
                    for half in range(2):
                        AC.do(lambda h, hh=hh, half=half: h.copy(
                            out=S_bf[:, half * 1024 + hh * 256:half * 1024 + (hh + 1) * 256],
                            in_=S_f32[:, half * 1024 + hh * 256:half * 1024 + (hh + 1) * 256]), inc=1)
                TE.wait(AC.tok())
            barrier()

        def stage_E():
            GD.phase()
            GD.dma(lambda h: h.dma_start(
                out=arenaW[:, W_wg1:W_wg1 + 8192].rearrange("p (kc j) -> p kc j", kc=8),
                in_=Wg1_h[0:1024].rearrange("(kc p) j -> p kc j", p=128)))
            GD.dma(lambda h: h.dma_start(
                out=arenaW[:, W_wg1 + 8192:W_wg1 + 9216], in_=Wg1t_h))
            GD.dma(lambda h: h.dma_start(
                out=arenaW[:, W_wg2:W_wg2 + 32].rearrange("p (kc j) -> p kc j", kc=8),
                in_=Wg2p_h.rearrange("(kc p) j -> p kc j", p=128)))
            GD.dma(lambda h: h.dma_start(
                out=arenaW[:, W_wog1:W_wog1 + 4096].rearrange("p (kc j) -> p kc j", kc=8),
                in_=Wog1_h[0:1024].rearrange("(kc p) j -> p kc j", p=128)))
            GD.dma(lambda h: h.dma_start(
                out=arenaW[:, W_wog1 + 4096:W_wog1 + 4608], in_=Wog1t_h))
            GD.dma(lambda h: h.dma_start(
                out=arenaW[:, W_wog2:W_wog2 + 4].rearrange("p (kc j) -> p kc j", kc=4),
                in_=Wog2_h.rearrange("(kc p) j -> p kc j", p=128),
                allow_slow_non_contiguous=True))
            t_wo = GD.dma(lambda h: h.dma_start(
                out=arenaW[:, W_wo:W_wo + 8192].rearrange("p (kc j) -> p kc j", kc=8),
                in_=Wo_h.rearrange("(kc p) j -> p kc j", p=128)))
            GD.dma(lambda h: h.dma_start(
                out=arenaW[:, W_vT:W_vT + 128].rearrange("p (kc j) -> p kc j", kc=8),
                in_=vT_h.rearrange("(kc p) j -> p kc j", p=128)))
            GD.dma(lambda h: h.dma_start(out=arenaW[0:16, W_uT:W_uT + 4096], in_=uT_h))
            GD.dma(lambda h: h.dma_start(out=arenaW[0:16, W_fbT:W_fbT + 16], in_=fbT_h))
            for fc in range(NFM):
                SY.dma(lambda h, fc=fc: h.dma_start_transpose(
                    out=arenaB[:, B_g1 + fc * 2048:B_g1 + (fc + 1) * 2048],
                    in_=h_xbf[:, fc * 128:(fc + 1) * 128]))
            AC.wait(t_wo)
            for kc in range(NFM):
                AC.do(lambda h, kc=kc: h.activation(
                    out=arenaW[:, W_wo + kc * 1024:W_wo + (kc + 1) * 1024],
                    in_=arenaW[:, W_wo + kc * 1024:W_wo + (kc + 1) * 1024],
                    func=AF.Copy, scale=consts[:, C_nw + kc:C_nw + kc + 1]), inc=1)
            pre0 = arenaB[:, B_mix:B_mix + 2048]
            pre1 = arenaB[:, B_mix + 2048:B_mix + 4096]
            presE = [(arenaB[:, B_gg:B_gg + 2048], arenaB[:, B_gg + 2048:B_gg + 4096]),
                     (arenaB[:, B_gg + 4096:B_gg + 6144], arenaB[:, B_gg + 6144:B_gg + 8192])]
            def get_d(fc):
                pg = fc % 2
                SY.phase()
                SY.wait(TE.tok())
                tok = SY.dma(lambda h, fc=fc, pg=pg: h.dma_start_transpose(
                    out=arenaB[:, B_mix + pg * 2048:B_mix + pg * 2048 + 2048],
                    in_=h_delta[:, fc * 128:(fc + 1) * 128]))
                return arenaB[:, B_mix + pg * 2048:B_mix + pg * 2048 + 2048], tok
            stats_fm2(get_d, (32, 96 + 32), None, False, presE)
            t_cp = DV.do(lambda h: h.tensor_copy(out=srows_bf[R_gpark:R_gpark + 16, :],
                                                 in_=srows[R_gpark:R_gpark + 16, :]), inc=1)
            TE.wait(t_cp)
            prev_ev = {0: None, 1: None}
            for fc in range(NFM):
                for th in range(2):
                    par = (fc * 2 + th) % 2
                    ps = PA if par == 0 else PB
                    TE.wait(prev_ev[par])
                    for kc in range(NFM):
                        mm(ps,
                           arenaW[:, W_wg1 + kc * 1024 + fc * 128:W_wg1 + kc * 1024 + (fc + 1) * 128],
                           arenaB[:, B_g1 + kc * 2048 + th * 1024:B_g1 + kc * 2048 + (th + 1) * 1024],
                           start=(kc == 0), stop=False)
                    mm(ps,
                       arenaW[:, W_wg1 + 8192 + fc * 128:W_wg1 + 8192 + (fc + 1) * 128],
                       srows_bf[:, th * 1024:(th + 1) * 1024],
                       False, True)
                    t1 = TE.bump()
                    AC.wait(t1)
                    for hf in range(2):
                        t2 = AC.do(lambda h, fc=fc, th=th, ps=ps, hf=hf: h.activation(
                            out=arenaB[:, B_gg + fc * 2048 + th * 1024 + hf * 512:B_gg + fc * 2048 + th * 1024 + (hf + 1) * 512],
                            in_=ps[hf][:, 0:512], func=AF.Gelu,
                            bias=consts[:, C_bg1 + fc:C_bg1 + fc + 1]), inc=1)
                    prev_ev[par] = t2
            TE.wait(prev_ev[0])
            TE.wait(prev_ev[1])
            AC.do(lambda h: h.activation(out=consts[0:4, C_tmp:C_tmp + 1],
                                         in_=consts[0:4, C_lt:C_lt + 1], func=AF.Exp), inc=1)
            t_e1 = AC.tok()
            DV.wait(t_e1)
            t_e2 = DV.do(lambda h: h.tensor_scalar(out=consts[0:4, C_tmp:C_tmp + 1],
                                                   in0=consts[0:4, C_tmp:C_tmp + 1],
                                                   scalar1=1.0, scalar2=None, op0=ALU.add), inc=1)
            AC.wait(t_e2)
            t_sp = AC.do(lambda h: h.activation(out=consts[0:4, C_tmp:C_tmp + 1],
                                                in_=consts[0:4, C_tmp:C_tmp + 1], func=AF.Ln), inc=1)
            DV.wait(t_sp)
            DV.do(lambda h: h.reciprocal(out=consts[0:4, C_rsp:C_rsp + 1],
                                         in_=consts[0:4, C_tmp:C_tmp + 1]), inc=1)
            t_rsp = DV.do(lambda h: h.tensor_mul(out=consts[0:4, C_b2r:C_b2r + 1],
                                                 in0=consts[0:4, C_bg2:C_bg2 + 1],
                                                 in1=consts[0:4, C_rsp:C_rsp + 1]), inc=1)
            AC.wait(t_rsp)
            for q in range(4):
                for kc in range(NFM):
                    mm(p1a[0:4, 0:512], arenaW[:, W_wg2 + kc * 4:W_wg2 + (kc + 1) * 4],
                       arenaB[:, B_gg + kc * 2048 + q * 512:B_gg + kc * 2048 + (q + 1) * 512],
                       start=(kc == 0), stop=(kc == NFM - 1))
                t1 = TE.bump()
                DV.wait(t1)
                t2 = DV.do(lambda h, q=q: h.scalar_tensor_tensor(
                    out=srows[R_logit:R_logit + 4, q * 512:(q + 1) * 512], in0=p1a[0:4, 0:512],
                    scalar=consts[0:4, C_rsp:C_rsp + 1],
                    in1=consts[0:4, C_b2r:C_b2r + 1].broadcast_to([4, 512]),
                    op0=ALU.mult, op1=ALU.add), inc=1)
                TE.wait(t2)
            barrier()
            small_rt(R_logit, 4, T_p)
            sp3 = stm[:, 0:512].rearrange("p (t c) -> p t c", c=32)
            P_ = sp3[:, :, T_p:T_p + 4]
            PLS_ = sp3[:, :, T_pls:T_pls + 4]
            mcol = lambda off: aF(F_acc + off, 16).rearrange(
                "p (t o) -> p t o", o=1).broadcast_to([128, 16, 4])
            DV.do(lambda h: h.tensor_reduce(out=aF(F_acc, 16), in_=P_, axis=AX.X,
                                            op=ALU.max), inc=1)
            DV.do(lambda h: h.tensor_tensor(out=P_, in0=P_, in1=mcol(0), op=ALU.subtract), inc=1)
            t1 = DV.tok()
            AC.wait(t1)
            t2 = AC.do(lambda h: h.activation(out=P_, in_=P_, func=AF.Exp), inc=1)
            DV.wait(t2)
            DV.do(lambda h: h.tensor_reduce(out=aF(F_acc, 16), in_=P_, axis=AX.X,
                                            op=ALU.add), inc=1)
            DV.do(lambda h: h.reciprocal(out=aF(F_acc, 16), in_=aF(F_acc, 16)), inc=1)
            DV.do(lambda h: h.tensor_tensor(out=P_, in0=P_, in1=mcol(0), op=ALU.mult), inc=1)
            DV.do(lambda h: h.tensor_scalar(out=P_, in0=P_, scalar1=0.02, scalar2=None,
                                            op0=ALU.max), inc=1)
            DV.do(lambda h: h.tensor_reduce(out=aF(F_acc, 16), in_=P_, axis=AX.X,
                                            op=ALU.add), inc=1)
            DV.do(lambda h: h.reciprocal(out=aF(F_acc, 16), in_=aF(F_acc, 16)), inc=1)
            DV.do(lambda h: h.tensor_tensor(out=P_, in0=P_, in1=mcol(0), op=ALU.mult), inc=1)
            DV.do(lambda h: h.tensor_tensor(
                out=PLS_, in0=PLS_,
                in1=sp3[:, :, T_p:T_p + 1].broadcast_to([128, 16, 4]), op=ALU.add), inc=1)
            barrier()
            mix_dv = {}
            st_tok = {}
            for t in range(NTM):
                pg = t % 2
                SY.phase()
                SY.wait(mix_dv.get(t - 2))
                SY.dma(lambda h, t=t, pg=pg: h.dma_start_transpose(
                    out=arenaB[:, B_mix + pg * 1024:B_mix + pg * 1024 + 1024],
                    in_=h_lsfm[:, t * 128:(t + 1) * 128]))
                SY.dma(lambda h, t=t, pg=pg: h.dma_start(
                    out=arenaB[:, B_mix + 2048 + pg * 1024:B_mix + 2048 + pg * 1024 + 1024],
                    in_=h_ll[t * 128:(t + 1) * 128, :]))
                SY.dma(lambda h, t=t, pg=pg: h.dma_start(
                    out=arenaB[:, B_mix + 4096 + pg * 1024:B_mix + 4096 + pg * 1024 + 1024],
                    in_=h_delta[t * 128:(t + 1) * 128, :]))
                t_vv = SY.dma(lambda h, t=t, pg=pg: h.dma_start(
                    out=arenaB[:, B_otm + 1024 + pg * 1024:B_otm + 1024 + pg * 1024 + 1024],
                    in_=h_vtm[t * 128:(t + 1) * 128, :]))
                lsb = arenaB[:, B_mix + pg * 1024:B_mix + pg * 1024 + 1024]
                llb = arenaB[:, B_mix + 2048 + pg * 1024:B_mix + 2048 + pg * 1024 + 1024]
                ddb = arenaB[:, B_mix + 4096 + pg * 1024:B_mix + 4096 + pg * 1024 + 1024]
                vvb = arenaB[:, B_otm + 1024 + pg * 1024:B_otm + 1024 + pg * 1024 + 1024]
                DV.wait(t_vv)
                DV.wait(st_tok.get(t - 2))
                for hh in range(H):
                    DV.do(lambda h, hh=hh, lsb=lsb, t=t, pg=pg: h.tensor_scalar(
                        out=aF(F_acc + pg * 1024 + hh * 256, 256),
                        in0=lsb[:, hh * 256:(hh + 1) * 256],
                        scalar1=stm[:, t * 32 + T_pls + hh:t * 32 + T_pls + hh + 1],
                        scalar2=None, op0=ALU.mult), inc=1)
                DV.do(lambda h, llb=llb, t=t, pg=pg: h.scalar_tensor_tensor(
                    out=aF(F_acc + pg * 1024, 1024), in0=llb,
                    scalar=stm[:, t * 32 + T_p + 1:t * 32 + T_p + 2],
                    in1=aF(F_acc + pg * 1024, 1024), op0=ALU.mult, op1=ALU.add), inc=1)
                DV.do(lambda h, ddb=ddb, t=t, pg=pg: h.scalar_tensor_tensor(
                    out=aF(F_acc + pg * 1024, 1024), in0=ddb,
                    scalar=stm[:, t * 32 + T_p + 2:t * 32 + T_p + 3],
                    in1=aF(F_acc + pg * 1024, 1024), op0=ALU.mult, op1=ALU.add), inc=1)
                t_ot = DV.do(lambda h, vvb=vvb, t=t, pg=pg: h.scalar_tensor_tensor(
                    out=aF(F_acc + pg * 1024, 1024), in0=vvb,
                    scalar=stm[:, t * 32 + T_p + 3:t * 32 + T_p + 4],
                    in1=aF(F_acc + pg * 1024, 1024), op0=ALU.mult, op1=ALU.add), inc=1)
                mix_dv[t] = t_ot
                GD.phase()
                GP.wait(t_ot)
                st_tok[t] = GD.dma(lambda h, t=t, pg=pg: h.dma_start(
                    out=h_o[t * 128:(t + 1) * 128, :], in_=aF(F_acc + pg * 1024, 1024)))
            barrier()
            def get_o(fc):
                pg = fc % 2
                SY.phase()
                SY.wait(TE.tok())
                tok = SY.dma(lambda h, fc=fc, pg=pg: h.dma_start_transpose(
                    out=arenaB[:, B_mix + pg * 2048:B_mix + pg * 2048 + 2048],
                    in_=h_o[:, fc * 128:(fc + 1) * 128]))
                return arenaB[:, B_mix + pg * 2048:B_mix + pg * 2048 + 2048], tok
            stats_fm2(get_o, None, (64, 160), True, presE)
            t_cg = DV.do(lambda h: h.tensor_copy(out=srows_bf[R_ogpark:R_ogpark + 8, :],
                                                 in_=srows[R_ogpark:R_ogpark + 8, :]), inc=1)
            TE.wait(t_cg)
            og_ps = [p1a[0:1, 0:512], p1b[0:1, 0:512], pC0[0:1, 0:512], pC1[0:1, 0:512]]
            prev_ev = {0: None, 1: None}
            prev_g2 = {0: None, 1: None}
            for fo in range(4):
                g2buf = arenaB[:, B_mix + (fo % 2) * 2048:B_mix + (fo % 2) * 2048 + 2048]
                for th in range(2):
                    ps = PA if th == 0 else PB
                    TE.wait(prev_ev[th])
                    for kc in range(NFM):
                        mm(ps,
                           arenaW[:, W_wog1 + kc * 512 + fo * 128:W_wog1 + kc * 512 + (fo + 1) * 128],
                           arenaB[:, B_g1 + kc * 2048 + th * 1024:B_g1 + kc * 2048 + (th + 1) * 1024],
                           start=(kc == 0), stop=False)
                    mm(ps,
                       arenaW[:, W_wog1 + 4096 + fo * 128:W_wog1 + 4096 + (fo + 1) * 128],
                       srows_bf[:, th * 1024:(th + 1) * 1024],
                       False, True)
                    t1 = TE.bump()
                    AC.wait(t1)
                    AC.wait(prev_g2[fo % 2])
                    for hf in range(2):
                        t2 = AC.do(lambda h, fo=fo, th=th, ps=ps, g2buf=g2buf, hf=hf: h.activation(
                            out=g2buf[:, th * 1024 + hf * 512:th * 1024 + (hf + 1) * 512],
                            in_=ps[hf][:, 0:512], func=AF.Gelu,
                            bias=consts[:, C_bog1 + fo:C_bog1 + fo + 1]), inc=1)
                    prev_ev[th] = t2
                TE.wait(prev_ev[0])
                TE.wait(prev_ev[1])
                for q in range(4):
                    mm(og_ps[q], arenaW[:, W_wog2 + fo:W_wog2 + fo + 1],
                       g2buf[:, q * 512:(q + 1) * 512],
                       start=(fo == 0), stop=(fo == 3))
                prev_g2[fo % 2] = TE.bump()
            t_og = TE.tok()
            AC.wait(t_og)
            for q in range(4):
                AC.do(lambda h, q=q: h.activation(
                    out=srows[R_og:R_og + 1, q * 512:(q + 1) * 512], in_=og_ps[q],
                    func=AF.Sigmoid, bias=consts[0:1, C_bog2:C_bog2 + 1]), inc=1)
            barrier()
            small_rt(0, 8, T_og)
            sg3 = stm[:, 0:512].rearrange("p (t c) -> p t c", c=32)
            OG_ = sg3[:, :, T_og:T_og + 1]
            SC_ = sg3[:, :, T_sc:T_sc + 4]
            DV.do(lambda h: h.tensor_tensor(
                out=sg3[:, :, T_tmp:T_tmp + 1], in0=OG_, in1=OG_, op=ALU.mult), inc=1)
            DV.do(lambda h: h.tensor_tensor(
                out=SC_, in0=sg3[:, :, T_rrms:T_rrms + 4],
                in1=sg3[:, :, T_tmp:T_tmp + 1].broadcast_to([128, 16, 4]),
                op=ALU.mult), inc=1)
            t_ms = DV.tok()
            AC.wait(t_ms)
            AC.do(lambda h: h.activation(
                out=SC_, in_=SC_, func=AF.Sqrt, bias=consts[:, C_eln:C_eln + 1]), inc=1)
            t_sq = AC.tok()
            DV.wait(t_sq)
            DV.do(lambda h: h.reciprocal(out=SC_, in_=SC_), inc=1)
            DV.do(lambda h: h.tensor_tensor(
                out=SC_, in0=SC_,
                in1=OG_.broadcast_to([128, 16, 4]), op=ALU.mult), inc=1)
            on_dv = {}
            on_st = {}
            for t in range(NTM):
                pg = t % 2
                SY.phase()
                SY.wait(on_dv.get(t - 2))
                t_o = SY.dma(lambda h, t=t, pg=pg: h.dma_start(
                    out=arenaB[:, B_mix + pg * 1024:B_mix + pg * 1024 + 1024],
                    in_=h_o[t * 128:(t + 1) * 128, :]))
                DV.wait(t_o)
                DV.wait(on_st.get(t - 2))
                for hh in range(H):
                    DV.do(lambda h, t=t, pg=pg, hh=hh: h.tensor_scalar(
                        out=arenaB[:, B_mix + 2048 + pg * 1024 + hh * 256:B_mix + 2048 + pg * 1024 + (hh + 1) * 256],
                        in0=arenaB[:, B_mix + pg * 1024 + hh * 256:B_mix + pg * 1024 + (hh + 1) * 256],
                        scalar1=stm[:, t * 32 + T_sc + hh:t * 32 + T_sc + hh + 1],
                        scalar2=None, op0=ALU.mult), inc=1)
                t_on = DV.tok()
                on_dv[t] = t_on
                SY.phase()
                SY.wait(t_on)
                on_st[t] = SY.dma(lambda h, t=t, pg=pg: h.dma_start(
                    out=h_on[t * 128:(t + 1) * 128, :],
                    in_=arenaB[:, B_mix + 2048 + pg * 1024:B_mix + 2048 + pg * 1024 + 1024]))
            barrier()
            for fc in range(NFM):
                SY.dma(lambda h, fc=fc: h.dma_start_transpose(
                    out=arenaB[:, B_onfm + fc * 2048:B_onfm + (fc + 1) * 2048],
                    in_=h_on[:, fc * 128:(fc + 1) * 128]))
            barrier()
            prev_res = {0: None, 1: None}
            cp_tok = {0: None, 1: None}
            st_tok = {0: None, 1: None}
            bst_tok = {0: None, 1: None}
            for tc in range(NTM):
                par = tc % 2
                ps = PA if par == 0 else PB
                ln = aF(F_ln, 1024) if par == 0 else aF(F_acc, 1024)
                mb = F_m + par * 4
                SY.phase()
                SY.wait(cp_tok[par])
                SY.wait(st_tok[par])
                t_x = SY.dma(lambda h, tc=tc, ln=ln: h.dma_start(
                    out=ln, in_=x_in[tc * 128:(tc + 1) * 128, :]))
                TE.wait(prev_res[par])
                for kc in range(NFM):
                    mm(ps,
                       arenaB[:, B_onfm + kc * 2048 + tc * 128:B_onfm + kc * 2048 + (tc + 1) * 128],
                       arenaW[:, W_wo + kc * 1024:W_wo + (kc + 1) * 1024],
                       start=(kc == 0), stop=(kc == NFM - 1))
                t_mm2 = TE.bump()
                DV.wait(t_mm2); DV.wait(t_x)
                DV.do(lambda h, ps=ps, ln=ln: h.tensor_add(out=ln[:, 0:512], in0=ln[:, 0:512],
                                                           in1=ps[0][:, 0:512]), inc=1)
                DV.do(lambda h, ps=ps, ln=ln: h.tensor_add(out=ln[:, 512:1024], in0=ln[:, 512:1024],
                                                           in1=ps[1][:, 0:512]), inc=1)
                t_res = DV.tok()
                prev_res[par] = t_res
                DV.do(lambda h, ln=ln, mb=mb: h.reduce_sum(out=aF(mb, 1), in_=ln, axis=AX.X), inc=1)
                AC.wait(DV.tok())
                t_sq = AC.do(lambda h, ln=ln, mb=mb: h.activation(
                    out=aF(F_acc + 1024, 1024), in_=ln, func=AF.Square,
                    accum_out=aF(mb + 1, 1)), inc=1)
                DV.wait(t_sq)
                DV.do(lambda h, mb=mb: h.tensor_scalar(out=aF(mb, 1), in0=aF(mb, 1),
                                                       scalar1=1.0 / 1024, scalar2=None,
                                                       op0=ALU.mult), inc=1)
                DV.do(lambda h, mb=mb: h.tensor_mul(out=aF(mb + 2, 1), in0=aF(mb, 1),
                                                    in1=aF(mb, 1)), inc=1)
                t_v2 = DV.do(lambda h, mb=mb: h.scalar_tensor_tensor(
                    out=aF(mb + 3, 1), in0=aF(mb + 1, 1), scalar=1.0 / 1024,
                    in1=aF(mb + 2, 1), op0=ALU.mult, op1=ALU.subtract), inc=1)
                AC.wait(t_v2)
                t_rs = AC.do(lambda h, mb=mb: h.activation(out=aF(mb + 3, 1), in_=aF(mb + 3, 1),
                                                           func=AF.Sqrt,
                                                           bias=consts[:, C_eln:C_eln + 1]), inc=1)
                DV.wait(t_rs)
                DV.do(lambda h, mb=mb: h.reciprocal(out=aF(mb + 3, 1), in_=aF(mb + 3, 1)), inc=1)
                t_ap = DV.do(lambda h, ln=ln, mb=mb: h.tensor_scalar(
                    out=ln, in0=ln,
                    scalar1=aF(mb, 1), scalar2=aF(mb + 3, 1),
                    op0=ALU.subtract, op1=ALU.mult), inc=1)
                SY.phase()
                SY.wait(t_ap)
                st_tok[par] = SY.dma(lambda h, tc=tc, ln=ln: h.dma_start(
                    out=h_x2[tc * 128:(tc + 1) * 128, :], in_=ln))
                DV.wait(bst_tok[par])
                t_bf = DV.do(lambda h, ln=ln, par=par: h.tensor_copy(
                    out=wsm[:, par * 1024:(par + 1) * 1024], in_=ln), inc=1)
                cp_tok[par] = t_bf
                SY.phase()
                SY.wait(t_bf)
                bst_tok[par] = SY.dma(lambda h, tc=tc, par=par: h.dma_start(
                    out=h_x2bf[tc * 128:(tc + 1) * 128, :], in_=wsm[:, par * 1024:(par + 1) * 1024]))
            barrier()


        QW = 0
        q_tok = {}
        q_red = {}
        q_state = {"k": 0}

        def _q_buf(k):
            pg = k % 2
            return arenaB[:, B_lsld + pg * 2048:B_lsld + pg * 2048 + 2048].bitcast(F32)[:, 0:1024]

        def _q_load(k):
            qbuf = _q_buf(k)
            GD.phase()
            GP.wait(q_red.get(k - 2))
            if k < 32:
                jc = k
                q_tok[k] = GD.dma(lambda h, jc=jc, qbuf=qbuf: h.dma_start(
                    out=qbuf, in_=wfc_h[jc * 128:(jc + 1) * 128, :]))
            else:
                pc, qq = (k - 32) // 4, (k - 32) % 4
                q_tok[k] = GD.dma(lambda h, pc=pc, qq=qq, qbuf=qbuf: h.dma_start(
                    out=qbuf,
                    in_=wpj_h[pc * 128:(pc + 1) * 128, qq * 1024:(qq + 1) * 1024]))

        def _q_reduce(k):
            qbuf = _q_buf(k)
            DV.wait(q_tok[k])
            if k < 32:
                jc = k
                q_red[k] = DV.do(lambda h, jc=jc, qbuf=qbuf: h.tensor_reduce(
                    out=aF(F_sc + jc, 1), in_=qbuf, axis=AX.X,
                    op=ALU.add, apply_absolute_value=True), inc=1)
            else:
                pc, qq = (k - 32) // 4, (k - 32) % 4
                if qq == 0:
                    q_red[k] = DV.do(lambda h, pc=pc, qbuf=qbuf: h.tensor_reduce(
                        out=aF(F_sc + 32 + pc, 1), in_=qbuf, axis=AX.X,
                        op=ALU.add, apply_absolute_value=True), inc=1)
                else:
                    DV.do(lambda h, qbuf=qbuf: h.tensor_reduce(
                        out=aF(F_sc + 48, 1), in_=qbuf, axis=AX.X,
                        op=ALU.add, apply_absolute_value=True), inc=1)
                    q_red[k] = DV.do(lambda h, pc=pc: h.tensor_add(
                        out=aF(F_sc + 32 + pc, 1), in0=aF(F_sc + 32 + pc, 1),
                        in1=aF(F_sc + 48, 1)), inc=1)

        N_QITEMS = 64
        def q_hook():
            for _ in range(3):
                k = q_state["k"]
                if k < N_QITEMS:
                    if k - 2 >= 0:
                        _q_reduce(k - 2)
                    _q_load(k)
                    q_state["k"] = k + 1

        def stage_Qscale():
            while q_state["k"] < N_QITEMS:
                q_hook()
            for k in (N_QITEMS - 2, N_QITEMS - 1):
                if k not in q_red:
                    _q_reduce(k)

        def stage_Qfc():
            for jc in range(32):
                pg = jc % 2
                SY.phase()
                SY.wait(DV.tok())
                t_w = SY.dma(lambda h, jc=jc, pg=pg: h.dma_start(
                    out=aF(F_acc + pg * 1024, 1024),
                    in_=wfc_h[jc * 128:(jc + 1) * 128, :]))
                wb = aF(F_acc + pg * 1024, 1024)
                DV.do(lambda h, jc=jc: h.tensor_scalar(
                    out=aF(F_sc + jc, 1), in0=aF(F_sc + jc, 1),
                    scalar1=1.0 / 1024, scalar2=1e-5,
                    op0=ALU.mult, op1=ALU.max), inc=1)
                DV.do(lambda h, jc=jc: h.tensor_scalar(out=aF(F_sc + 50, 1), in0=aF(F_sc + jc, 1),
                                                       scalar1=0.5, scalar2=None, op0=ALU.mult), inc=1)
                DV.do(lambda h, jc=jc: h.tensor_scalar(out=aF(F_sc + 51, 1), in0=aF(F_sc + jc, 1),
                                                       scalar1=-0.5, scalar2=None, op0=ALU.mult), inc=1)
                DV.do(lambda h, jc=jc: h.tensor_scalar(out=aF(F_sc + 52, 1), in0=aF(F_sc + jc, 1),
                                                       scalar1=-1.0, scalar2=None, op0=ALU.mult), inc=1)
                DV.wait(t_w)
                DV.do(lambda h, wb=wb, jc=jc: h.tensor_scalar(
                    out=arenaW[:, QW:QW + 1024], in0=wb, scalar1=aF(F_sc + 50, 1),
                    scalar2=aF(F_sc + jc, 1), op0=ALU.is_gt, op1=ALU.mult), inc=1)
                DV.do(lambda h, wb=wb: h.tensor_scalar(
                    out=arenaW[:, QW + 1024:QW + 2048], in0=wb, scalar1=aF(F_sc + 51, 1),
                    scalar2=aF(F_sc + 52, 1), op0=ALU.is_lt, op1=ALU.mult), inc=1)
                t_t = DV.do(lambda h: h.tensor_add(out=arenaW[:, QW + 2048:QW + 3072],
                                                   in0=arenaW[:, QW:QW + 1024],
                                                   in1=arenaW[:, QW + 1024:QW + 2048]), inc=1)
                SY.phase()
                SY.wait(t_t)
                t_s = SY.dma(lambda h, jc=jc: h.dma_start(
                    out=h_tern[jc * 128:(jc + 1) * 128, :], in_=arenaW[:, QW + 2048:QW + 3072]))
                DV.wait(t_s)

        def stage_Qpj():
            for pc in range(8):
                wb = aF(F_acc, 2048)
                DV.do(lambda h, pc=pc: h.tensor_scalar(
                    out=aF(F_sc + 32 + pc, 1), in0=aF(F_sc + 32 + pc, 1),
                    scalar1=1.0 / 4096, scalar2=1e-5,
                    op0=ALU.mult, op1=ALU.max), inc=1)
                DV.do(lambda h, pc=pc: h.tensor_scalar(out=aF(F_sc + 50, 1),
                                                       in0=aF(F_sc + 32 + pc, 1),
                                                       scalar1=0.5, scalar2=None, op0=ALU.mult), inc=1)
                DV.do(lambda h, pc=pc: h.tensor_scalar(out=aF(F_sc + 51, 1),
                                                       in0=aF(F_sc + 32 + pc, 1),
                                                       scalar1=-0.5, scalar2=None, op0=ALU.mult), inc=1)
                for half in (1, 0):
                    SY.phase()
                    SY.wait(DV.tok())
                    t_w = SY.dma(lambda h, pc=pc, half=half: h.dma_start(
                        out=wb, in_=wpj_h[pc * 128:(pc + 1) * 128,
                                          half * 2048:(half + 1) * 2048]))
                    DV.wait(t_w)
                    DV.do(lambda h, wb=wb: h.tensor_scalar(
                        out=arenaW[:, QW:QW + 2048], in0=wb, scalar1=aF(F_sc + 50, 1),
                        scalar2=1.0, op0=ALU.is_gt, op1=ALU.mult), inc=1)
                    DV.do(lambda h, wb=wb: h.tensor_scalar(
                        out=arenaW[:, QW + 2048:QW + 4096], in0=wb, scalar1=aF(F_sc + 51, 1),
                        scalar2=-1.0, op0=ALU.is_lt, op1=ALU.mult), inc=1)
                    t_a = DV.do(lambda h: h.tensor_add(out=arenaW[:, QW:QW + 2048],
                                                       in0=arenaW[:, QW:QW + 2048],
                                                       in1=arenaW[:, QW + 2048:QW + 4096]), inc=1)
                    SY.phase()
                    SY.wait(t_a)
                    t_s = SY.dma(lambda h, pc=pc, half=half: h.dma_start(
                        out=h_ternp[pc * 128:(pc + 1) * 128, half * 2048:(half + 1) * 2048],
                        in_=arenaW[:, QW:QW + 2048]))
                    DV.wait(t_s)

        def stage_F():
            barrier()
            for dc in range(NFM):
                SY.dma(lambda h, dc=dc: h.dma_start_transpose(
                    out=arenaW[:, W_tern + dc * 4096:W_tern + (dc + 1) * 4096],
                    in_=h_tern[:, dc * 128:(dc + 1) * 128]))
            for fc in range(NFM):
                SY.dma(lambda h, fc=fc: h.dma_start_transpose(
                    out=arenaB[:, B_x2fm + fc * 2048:B_x2fm + (fc + 1) * 2048],
                    in_=h_x2bf[:, fc * 128:(fc + 1) * 128]))
            barrier()
            for th in range(2):
                for kc in range(NFM):
                    mm((pC0[0:16, 0:512], pC1[0:16, 0:512]),
                       arenaW[:, W_vT + kc * 16:W_vT + (kc + 1) * 16],
                       arenaB[:, B_x2fm + kc * 2048 + th * 1024:B_x2fm + kc * 2048 + (th + 1) * 1024],
                       start=(kc == 0), stop=(kc == NFM - 1))
                t1 = TE.bump()
                AC.wait(t1)
                AC.do(lambda h, th=th: h.copy(
                    out=wsm[0:16, th * 1024:th * 1024 + 512], in_=pC0[0:16, 0:512]), inc=1)
                t2 = AC.do(lambda h, th=th: h.copy(
                    out=wsm[0:16, th * 1024 + 512:(th + 1) * 1024], in_=pC1[0:16, 0:512]), inc=1)
                TE.wait(t2)
            for th in range(2):
                mm((pC0[0:16, 0:512], pC1[0:16, 0:512]), arenaW[0:16, W_fbT:W_fbT + 16],
                   wsm[0:16, th * 1024:(th + 1) * 1024], True, True)
                t1 = TE.bump()
                AC.wait(t1)
                AC.do(lambda h, th=th: h.activation(
                    out=wsm[0:16, 2048 + th * 1024:2048 + th * 1024 + 512], in_=pC0[0:16, 0:512],
                    func=AF.Copy, scale=consts[0:16, C_gate:C_gate + 1]), inc=1)
                t2 = AC.do(lambda h, th=th: h.activation(
                    out=wsm[0:16, 2048 + th * 1024 + 512:2048 + (th + 1) * 1024], in_=pC1[0:16, 0:512],
                    func=AF.Copy, scale=consts[0:16, C_gate:C_gate + 1]), inc=1)
                TE.wait(t2)
            barrier()
            ln2_state = {"ap": {0: None, 1: None}, "yst": {0: None, 1: None},
                         "add": {0: None, 1: None}}
            def ln2_quarter(tq4, prev_st):
                ap_tok = ln2_state["ap"]
                yst_tok = ln2_state["yst"]
                add_tok = ln2_state["add"]
                SY.phase()
                SY.wait(prev_st[0])
                SY.wait(prev_st[1])
                for tc in range(tq4 * 4, tq4 * 4 + 4):
                    par = tc % 2
                    ln = aF(F_ln, 1024) if par == 0 else aF(F_acc, 1024)
                    mb = F_m + par * 4
                    SY.phase()
                    SY.wait(ap_tok[par])
                    SY.wait(yst_tok[par])
                    t_x = SY.dma(lambda h, tc=tc, ln=ln: h.dma_start(
                        out=ln, in_=h_x2[tc * 128:(tc + 1) * 128, :]))
                    SY.phase()
                    SY.wait(add_tok[par])
                    t_f = SY.dma(lambda h, tc=tc, par=par: h.dma_start_transpose(
                        out=wsm[:, par * 1024:(par + 1) * 1024], in_=h_ffm[:, tc * 128:(tc + 1) * 128]))
                    DV.wait(t_x); DV.wait(t_f)
                    t_add = DV.do(lambda h, ln=ln, par=par: h.tensor_add(
                        out=ln, in0=ln, in1=wsm[:, par * 1024:(par + 1) * 1024]), inc=1)
                    add_tok[par] = t_add
                    DV.do(lambda h, ln=ln, mb=mb: h.reduce_sum(out=aF(mb, 1), in_=ln, axis=AX.X), inc=1)
                    AC.wait(DV.tok())
                    t_sq = AC.do(lambda h, ln=ln, mb=mb: h.activation(
                        out=aF(F_acc + 1024, 1024), in_=ln, func=AF.Square,
                        accum_out=aF(mb + 1, 1)), inc=1)
                    DV.wait(t_sq)
                    DV.do(lambda h, mb=mb: h.tensor_scalar(out=aF(mb, 1), in0=aF(mb, 1),
                                                           scalar1=1.0 / 1024, scalar2=None,
                                                           op0=ALU.mult), inc=1)
                    DV.do(lambda h, mb=mb: h.tensor_mul(out=aF(mb + 2, 1), in0=aF(mb, 1),
                                                        in1=aF(mb, 1)), inc=1)
                    t_v2 = DV.do(lambda h, mb=mb: h.scalar_tensor_tensor(
                        out=aF(mb + 3, 1), in0=aF(mb + 1, 1), scalar=1.0 / 1024,
                        in1=aF(mb + 2, 1), op0=ALU.mult, op1=ALU.subtract), inc=1)
                    AC.wait(t_v2)
                    t_rs = AC.do(lambda h, mb=mb: h.activation(out=aF(mb + 3, 1), in_=aF(mb + 3, 1),
                                                               func=AF.Sqrt,
                                                               bias=consts[:, C_eln:C_eln + 1]), inc=1)
                    DV.wait(t_rs)
                    DV.do(lambda h, mb=mb: h.reciprocal(out=aF(mb + 3, 1), in_=aF(mb + 3, 1)), inc=1)
                    t_ap = DV.do(lambda h, ln=ln, mb=mb: h.tensor_scalar(
                        out=ln, in0=ln,
                        scalar1=aF(mb, 1), scalar2=aF(mb + 3, 1),
                        op0=ALU.subtract, op1=ALU.mult), inc=1)
                    ap_tok[par] = t_ap
                    SY.phase()
                    SY.wait(t_ap)
                    yst_tok[par] = SY.dma(lambda h, tc=tc, ln=ln: h.dma_start(
                        out=y_out[tc * 128:(tc + 1) * 128, :], in_=ln))
            prev_st = {0: None, 1: None}
            for tq4 in range(4):
                th, tq = tq4 // 2, tq4 % 2
                t_ld = {}
                for j0 in range(3):
                    SY.phase()
                    SY.wait(TE.tok())
                    t_ld[j0] = SY.dma(lambda h, j0=j0: h.dma_start_transpose(
                        out=arenaW[:, W_pstr + (j0 % 4) * 1024:W_pstr + (j0 % 4) * 1024 + 1024],
                        in_=h_ternp[:, j0 * 128:(j0 + 1) * 128]))
                prev_sq = {0: None, 1: None, 2: None, 3: None}
                for jc in range(32):
                    pg = jc % 4
                    ps = (pA0, pA1, pB0, pB1)[pg]
                    TE.wait(prev_sq[pg])
                    for kc in range(NFM):
                        mm(ps[:, 0:512],
                           arenaW[:, W_tern + kc * 4096 + jc * 128:W_tern + kc * 4096 + (jc + 1) * 128],
                           arenaB[:, B_x2fm + kc * 2048 + tq4 * 512:B_x2fm + kc * 2048 + (tq4 + 1) * 512],
                           start=(kc == 0), stop=False)
                    mm(ps[:, 0:512], arenaW[0:16, W_uT + jc * 128:W_uT + (jc + 1) * 128],
                       wsm[0:16, 2048 + tq4 * 512:2048 + (tq4 + 1) * 512], False, True)
                    t1 = TE.bump()
                    DV.wait(t1)
                    t_r = DV.do(lambda h, ps=ps, pg=pg: h.tensor_scalar(
                        out=arenaB[:, B_fq + pg * 512:B_fq + pg * 512 + 512], in0=ps[:, 0:512],
                        scalar1=0.0, scalar2=None, op0=ALU.max), inc=1)
                    AC.wait(t_r)
                    t_sq = AC.do(lambda h, jc=jc, pg=pg: h.activation(
                        out=arenaB[:, B_r2 + jc * 512:B_r2 + (jc + 1) * 512],
                        in_=arenaB[:, B_fq + pg * 512:B_fq + pg * 512 + 512], func=AF.Square), inc=1)
                    prev_sq[pg] = t_sq
                for pg in range(4):
                    TE.wait(prev_sq[pg])
                psl = [pA0[:, 0:512], pA1[:, 0:512], pB0[:, 0:512], pB1[:, 0:512],
                       pC0[:, 0:512], pC1[:, 0:512], p1a[:, 0:512], p1b[:, 0:512]]
                for jc in range(32):
                    sl = jc % 4
                    if jc + 3 < 32:
                        SY.phase()
                        SY.wait(TE.tok())
                        t_ld[jc + 3] = SY.dma(lambda h, jc=jc: h.dma_start_transpose(
                            out=arenaW[:, W_pstr + ((jc + 3) % 4) * 1024:W_pstr + ((jc + 3) % 4) * 1024 + 1024],
                            in_=h_ternp[:, (jc + 3) * 128:(jc + 4) * 128]))
                    TE.wait(t_ld[jc])
                    for jout in range(NFM):
                        mm(psl[jout],
                           arenaW[:, W_pstr + sl * 1024 + jout * 128:W_pstr + sl * 1024 + (jout + 1) * 128],
                           arenaB[:, B_r2 + jc * 512:B_r2 + (jc + 1) * 512],
                           start=(jc == 0), stop=(jc == 31))
                    TE.bump()
                t1 = TE.tok()
                AC.wait(t1)
                for jout in range(NFM):
                    jp = jout % 2
                    AC.wait(prev_st[jp])
                    t2 = AC.do(lambda h, jout=jout, jp=jp: h.activation(
                        out=arenaB[:, B_fq + 2048 + jp * 512:B_fq + 2048 + jp * 512 + 512],
                        in_=psl[jout], func=AF.Copy, scale=aF(F_sc + 32 + jout, 1)), inc=1)
                    SY.phase()
                    SY.wait(t2)
                    prev_st[jp] = SY.dma(lambda h, jout=jout, tq4=tq4, jp=jp: h.dma_start(
                        out=h_ffm[jout * 128:(jout + 1) * 128, tq4 * 512:(tq4 + 1) * 512],
                        in_=arenaB[:, B_fq + 2048 + jp * 512:B_fq + 2048 + jp * 512 + 512]))
                TE.wait(AC.tok())
                ln2_quarter(tq4, prev_st)
            barrier()
            wait_all_sy(SY)

        def script():
            stages = [st_init, stage_A_v,
                      lambda: small_rt(0, 8, T_beta),
                      stage_A_k, stage_A_q,
                      lambda: small_rt(R_rq, 4, T_rq),
                      stage_Qscale,
                      stage_D, stage_B_k, stage_C_pre, stage_C_scan,
                      stage_E, stage_Qfc, stage_Qpj, stage_F]
            for i, st in enumerate(stages):
                if i >= stage_limit:
                    break
                st()
            if stage_limit < len(stages):
                barrier()
                wait_all_sy(SY)

        def reset_all():
            for e in ALLE:
                e.h, e.active, e.count = None, False, 0
            for ee in (GD, SY):
                ee.sem_pool = [(s, 0) for (s, c) in ee.sem_pool]
                ee.pool_idx = 0
            q_state["k"] = 0
            q_red.clear()
            q_tok.clear()

        @blk.sync
        def _(sync):
            reset_all(); SY.h, SY.active = sync, True
            script()

        @blk.gpsimd
        def _(gph):
            reset_all(); GP.h, GP.active = gph, True; GD.h, GD.active = gph, True
            script()

        @blk.tensor
        def _(teh):
            reset_all(); TE.h, TE.active = teh, True
            script()

        @blk.scalar
        def _(ach):
            reset_all(); AC.h, AC.active = ach, True
            script()

        @blk.vector
        def _(dvh):
            reset_all(); DV.h, DV.active = dvh, True
            script()

    return nc, dbg



def _const_onesv():
    out = np.zeros((128, 512), np.float32)
    for hh in range(4):
        out[:, 32 * (0 + hh) + 0 + hh] = 1.0
        out[:, 32 * (4 + hh) + 4 + hh] = 1.0
        out[:, 32 * (8 + hh) + 8 + hh] = 1.0
        out[:, 32 * (12 + hh) + 0 + hh] = 1.0
    return out


def _const_mixmx():
    out = np.zeros((128, 192), np.float32)
    for t in range(4):
        for hh in range(4):
            out[0 + hh, 16 * t + 4 * t + 0] = 0.25 / 256
            out[4 + hh, 16 * t + 4 * t + 1] = 0.25 / 256
    out[:] = 0.0
    for t in range(4):
        c = 16 * t
        j = 4 * t
        for hh in range(4):
            out[0 + hh, c + j + 0] = 0.25 / 256
            out[4 + hh, c + j + 1] = 0.25 / 256
            out[8 + hh, c + j + 2] = 0.25 / 256
            out[96 + hh, c + j + 3] = 0.25
            out[0 + hh, 96 + c + j + 1] = -0.25
    for base, j0, p2 in ((64, 0, 160), (72, 4, 168)):
        for hh in range(4):
            out[0 + hh, base + j0 + 0] = 0.25 / 256
            out[4 + hh, base + j0 + 1] = 0.25 / 256
            out[8 + hh, base + j0 + 2] = 0.25 / 256
            out[96 + hh, base + j0 + 3] = 0.25
            out[0 + hh, p2 + j0 + 1] = -0.25
    return out


def _tapcol(w):
    return np.ascontiguousarray(w.reshape(8, 128, 4).transpose(1, 0, 2).reshape(128, 32))


def _pad_tail(w, row0):
    out = np.zeros((128, w.shape[1]), np.float32)
    out[row0:row0 + w.shape[0]] = w
    return out


def _host_inputs(inputs, core):
    g = lambda k: np.asarray(inputs[k], np.float32)
    x = np.asarray(inputs["L0_attn_hidden_states"])[core]

    def diag_layout(w):
        K = w.shape[1]
        out = np.zeros((128, NFM * K * 128), np.float32)
        for fc in range(NFM):
            for t in range(K):
                b = out[:, (fc * K + t) * 128:(fc * K + t + 1) * 128]
                np.fill_diagonal(b, w[fc * 128:(fc + 1) * 128, t])
        return out

    def tpl_layout(w):
        out = np.zeros((128, 256 * 3 * 32), np.float32)
        r = np.arange(32)[:, None]
        j = np.arange(32)[None, :]
        for gg in range(4):
            for e in range(256):
                c = 256 * gg + e
                for d in range(3):
                    t = 63 - 32 * d + r - j
                    T = np.where((t >= 0) & (t <= 63), w[c][np.clip(t, 0, 63)], 0.0)
                    out[32 * gg:32 * (gg + 1), (e * 3 + d) * 32:(e * 3 + d + 1) * 32] = T
        return out

    def col128(v):
        return np.ascontiguousarray(np.asarray(v, np.float32).reshape(-1, 128).T)

    def pad_cols(w, total, at):
        out = np.zeros((w.shape[0], total), np.float32)
        out[:, at:at + w.shape[1]] = w
        return out

    d = {
        "x": np.ascontiguousarray(x, np.float32),
        "Wq": g("L0_attn_Wq"), "Wk": g("L0_attn_Wk"), "Wv": g("L0_attn_Wv"),
        "Wbp": g("L0_attn_Wb"),
        "Wrp": pad_cols(g("L0_attn_Wres"), 8, 4),
        "dq": diag_layout(g("L0_attn_wq_conv")),
        "dk": diag_layout(g("L0_attn_wk_conv")),
        "dv": diag_layout(g("L0_attn_wv_conv")),
        "ds": diag_layout(g("L0_attn_fir_short").reshape(1024, 5)),
        "tpl": tpl_layout(g("L0_attn_fir_long").reshape(1024, 64)),
        "tapq": _tapcol(g("L0_attn_wq_conv")),
        "tapk": _tapcol(g("L0_attn_wk_conv")),
        "tapv": _tapcol(g("L0_attn_wv_conv")),
        "taps5": np.ascontiguousarray(g("L0_attn_fir_short").reshape(8, 128, 5)
                                      .transpose(1, 0, 2).reshape(128, 40)),
        "Wg1": g("L0_attn_Wg1"),
        "Wg1t": _pad_tail(g("L0_attn_Wg1")[1024:1040], 32),
        "bg1": col128(g("L0_attn_bg1")),
        "Wg2p": g("L0_attn_Wg2"),
        "bg2": g("L0_attn_bg2").reshape(H, 1),
        "logit_temp": g("L0_attn_logit_temp").reshape(1, 1),
        "conv_res_logit": g("L0_attn_conv_res_logit").reshape(H, 1),
        "bres": g("L0_attn_bres").reshape(H, 1),
        "Wog1": g("L0_attn_Wog1"),
        "Wog1t": _pad_tail(g("L0_attn_Wog1")[1024:1032], 64),
        "bog1": col128(g("L0_attn_bog1")),
        "Wog2": g("L0_attn_Wog2"),
        "bog2": g("L0_attn_bog2").reshape(1, 1),
        "norm_w": col128(np.tile(g("L0_attn_norm_w"), 4)),
        "Wo": g("L0_attn_Wo"),
        "w_fc": g("L0_ffn_w_fc"), "w_proj": g("L0_ffn_w_proj"),
        "uT": np.ascontiguousarray(g("L0_ffn_u").T),
        "vT": np.ascontiguousarray(g("L0_ffn_v").T),
        "fbT": np.ascontiguousarray(g("L0_ffn_fast_b").T),
        "gate": g("L0_ffn_gate").reshape(1, 1),
        "onesv_c": _const_onesv(),
        "mixmx_c": _const_mixmx(),
    }
    return {k: np.ascontiguousarray(v, np.float32) for k, v in d.items()}


def kernel(**inputs):
    nc, _ = build_core()
    in_maps = [_host_inputs(inputs, c) for c in range(NCORE)]
    res = run_bass_kernel_spmd(nc, in_maps, core_ids=list(range(NCORE)))
    return np.stack([r["y"] for r in res.results], axis=0).astype(np.float32)
